# version 11 (final)
# baseline (speedup 1.0000x reference)
import numpy as np
from contextlib import ExitStack

import concourse.bass as bass
import concourse.mybir as mybir
from concourse.bass_utils import run_bass_kernel_spmd

F32 = mybir.dt.float32
BF16 = mybir.dt.bfloat16
I32 = mybir.dt.int32
AF = mybir.ActivationFunctionType
ALU = mybir.AluOpType
AX = mybir.AxisListType

D = 2048
DI = 4096
NH = 64
DST = 128
DXBC = DI + 2 * DST
DPROJ = 2 * DI + 2 * DST + NH
T = 2048
CH = 256
NCH = T // CH
TE = T + 3
EPS = 1e-5
KD = D // 128
KDI = DI // 128
NPAIR = NH // 2
NXBC = DXBC // 128
M_INP = 67
DE = 1408
DS = 2816
KE = DE // 128
KS = DS // 128

_DBG = False


class Ctr:
    def __init__(self, nc, ctx, name):
        self.h = ctx.enter_context(nc.semaphore(name))
        self.n = 0

    def inc(self, instr, k=1):
        instr.then_inc(self.h, k)
        self.n += k
        return instr


class B:
    def __init__(self):
        nc = bass.Bass("TRN2", target_bir_lowering=False, debug=False, detect_race_conditions=True)
        self.nc = nc
        self.ctx = ExitStack()

        def inp(name, shape):
            return nc.dram_tensor(name, shape, F32, kind="ExternalInput").ap()

        self.u = inp("u_ext", [TE, D])
        self.w_inproj = inp("in_proj_w", [DPROJ, D])
        self.w_conv = inp("conv_w", [DXBC, 4])
        self.i_dtb = inp("dt_bias", [NH])
        self.i_alog = inp("A_log", [NH])
        self.i_dpar = inp("D_param", [NH])
        self.i_normw = inp("norm_w", [DI])
        self.w_outproj = inp("out_proj_w", [D, DI])
        self.w_router = inp("W_router", [D, 8])
        self.we_gate = inp("We_gate", [8, D, DE])
        self.we_up = inp("We_up", [8, D, DE])
        self.we_down = inp("We_down", [8, DE, D])
        self.ws_gate = inp("Ws_gate", [D, DS])
        self.ws_up = inp("Ws_up", [D, DS])
        self.ws_down = inp("Ws_down", [DS, D])
        self.i_mselT = inp("mselT", [8, 8])
        self.i_cbias = inp("cbias", [8])

        self.out = nc.dram_tensor("out", [T, D], F32, kind="ExternalOutput").ap()

        def scr(name, shape, dt=BF16):
            return nc.dram_tensor(name, shape, dt, kind="Internal").ap()

        self.d_xn = scr("d_xn", [D, TE + 1])
        self.d_z = scr("d_z", [DI, T])
        self.d_xbc = scr("d_xbc", [DXBC, TE + 1])
        self.d_xtm = scr("d_xtm", [T, DI])
        self.d_dx = scr("d_dx", [DI, T])
        self.d_prev = scr("d_prev", [NCH, 128, NPAIR * DST])
        self.d_ynorm = scr("d_ynorm", [DI, T])
        self.d_rstd = scr("d_rstd", [1, T], F32)
        self.d_ffn = scr("d_ffn", [D, T])
        self.cc_in = scr("cc_in", [128, NPAIR * DST + 1], F32)
        self.cc_out = scr("cc_out", [8 * 128, NPAIR * DST + 1], F32)
        if _DBG:
            self.dbg = {}
            for nm, shp, dt in [("xn", [D, TE + 1], BF16), ("z", [DI, T], BF16),
                                ("xbc", [DXBC, TE + 1], BF16), ("dtraw", [NH, T], F32),
                                ("combine", [8, T], F32), ("xtm", [T, DI], BF16),
                                ("dx", [DI, T], BF16), ("prevloc", [NCH, 128, NPAIR * DST], BF16),
                                ("ccout", [8 * 128, NPAIR * DST + 1], F32),
                                ("csall", [NH, T], F32), ("dtall", [NH, T], F32),
                                ("ynorm", [DI, T], BF16), ("ffn", [D, T], BF16),
                                ("gtm", [128, NCH * 2 * NH], F32),
                                ("cdcol", [128, NCH * NPAIR], F32),
                                ("dcol", [128, NCH * NPAIR], F32)]:
                self.dbg[nm] = nc.dram_tensor("dbg_" + nm, shp, dt, kind="ExternalOutput").ap()

        c = self.ctx
        self._mamba_ctx = ExitStack()
        self._tmp_ctx = ExitStack()
        mc = self._mamba_ctx
        tc = self._tmp_ctx

        def sb(name, shape, dt=F32):
            return c.enter_context(nc.sbuf_tensor(name, shape, dt))

        def sbm(name, shape, dt=F32):
            return mc.enter_context(nc.sbuf_tensor(name, shape, dt))

        def sbt(name, shape, dt=F32):
            return tc.enter_context(nc.sbuf_tensor(name, shape, dt))

        self.identb = sb("identb", [128, 128], BF16)
        self.identf = sb("identf", [128, 128], F32)
        self.ones_b = sb("ones_b", [128, 128], BF16)
        self.ones_f = sb("ones_f", [64, 128], F32)
        self.sel8 = sb("sel8", [8, 8, 128], F32)
        self.Eev = sb("Eev", [64, 32], F32)
        self.Eod = sb("Eod", [64, 32], F32)
        self.dtb_col = sb("dtb_col", [NH, 1], F32)
        self.A_col = sb("A_col", [NH, 1], F32)
        self.mselT_sb = sb("mselT_sb", [8, 8], F32)
        self.cbias_col = sb("cbias_col", [8, 1], F32)
        self.eps_col = sb("eps_col", [128, 1], F32)
        self.dummyw = sb("dummyw", [128, 512], BF16)
        self.maskT = sbm("maskT", [128, 2, CH], F32)
        self.normw = sbm("normw", [128, KDI], F32)
        self.cs_hi = sbm("cs_hi", [NH, T], BF16)
        self.cs_lo = sbm("cs_lo", [NH, T], BF16)
        self.ecs16 = sbm("ecs16", [NH, T], BF16)
        self.csldt = sbm("csldt", [128, NCH, 2, NH], F32)
        self.B_all = sbm("B_all", [128, T], BF16)
        self.C_all = sbm("C_all", [128, T], BF16)
        self.cdcol = sbm("cdcol", [128, NCH, NPAIR], F32)
        self.Dcol = sbm("Dcol", [128, NCH, NPAIR], F32)
        self.logP = sbm("logP", [NH, 1], F32)

        self.dt_all = sbt("dt_all", [NH, T], F32)
        self.cs_all = sbt("cs_all", [NH, T], F32)
        self.combine = sbt("combine", [8, T], F32)
        self.utri = sbt("utri", [128, 2, CH], F32)
        self.dt_raw = sbt("dt_raw", [NH, T], F32)
        self._prev_ctx = ExitStack()
        self.prev = self._prev_ctx.enter_context(nc.sbuf_tensor("prev", [128, NPAIR * DST], F32))
        self.g_tm = self._prev_ctx.enter_context(nc.sbuf_tensor("g_tm", [128, NCH, 2, NH], F32))
        self.D_pair = self._prev_ctx.enter_context(nc.sbuf_tensor("D_pair", [128, NPAIR], F32))
        self.wcv = self._prev_ctx.enter_context(nc.sbuf_tensor("wcv", [128, NXBC, 4], F32))

        self.bar = Ctr(nc, c, "bar")
        self.xT = Ctr(nc, c, "xT")
        self.xV = Ctr(nc, c, "xV")
        self.xA = Ctr(nc, c, "xA")
        self.xS = Ctr(nc, c, "xS")
        self.nbar = 0

    def barrier(self, blk):
        self.nbar += 1
        target = 5 * self.nbar
        for name in ("tensor", "vector", "scalar", "sync", "gpsimd"):
            @getattr(blk, name)
            def _(eng, target=target):
                eng.drain().then_inc(self.bar.h, 1)
                eng.wait_ge(self.bar.h, target)

    def warm(self, blk, ph, name, n=16):
        nc = self.nc
        pd = ph.enter_context(nc.psum_tensor(name, [128, 512], F32))
        self._pdum = pd

        @blk.tensor
        def _(t):
            for _ in range(n):
                t.matmul(pd[:, :], self.dummyw[:, 0:128], self.dummyw[:, :],
                         start=True, stop=True)
        return pd

    def build(self):
        nc = self.nc
        with self.ctx, nc.Block() as blk:
            self.p_init(blk)
            self._init_ctx.close()
            self.barrier(blk)
            self.p0_ln(blk)
            self.barrier(blk)
            with ExitStack() as ph:
                self.p1_inproj(blk, ph)
            self.barrier(blk)
            with ExitStack() as ph:
                self.p3a_router(blk, ph)
            self.barrier(blk)
            self.free_xn()
            with ExitStack() as ph:
                self.warm(blk, ph, "wm2s")
                self.p2_stats(blk, ph)
            self.barrier(blk)
            with ExitStack() as ph:
                self.warm(blk, ph, "wm2m")
                self.p2_main(blk, ph)
            self.barrier(blk)
            self._prev_ctx.close()
            with ExitStack() as ph:
                self.p3b_ffn(blk, ph)
            self.barrier(blk)
            self._tmp_ctx.close()
            with ExitStack() as ph:
                self.warm(blk, ph, "wmhn")
                self.p4_hin(blk, ph)
            self.hin_sem = self.xV
            self.hin_total = self.xV.n
            self.barrier(blk)
            with ExitStack() as ph:
                self.warm(blk, ph, "wmp4")
                self.p4_mid(blk, ph)
            self.barrier(blk)
            self.free_mamba()
            with ExitStack() as ph:
                self.warm(blk, ph, "wmp5")
                self.p5_final(blk, ph)
            if _DBG:
                self.barrier(blk)
                self.p_dbg(blk)
        return nc

    def free_xn(self):
        self._xn_ctx.close()

    def free_mamba(self):
        self._p4_ctx.close()
        self._mamba_ctx.close()

    def p_init(self, blk):
        nc, c = self.nc, self.ctx
        self._init_ctx = ExitStack()
        ic = self._init_ctx
        it128 = ic.enter_context(nc.sbuf_tensor("it128", [128, 128], I32))
        it256 = ic.enter_context(nc.sbuf_tensor("it256", [128, 2, CH], I32))
        itE = ic.enter_context(nc.sbuf_tensor("itE", [64, 2, 32], I32))
        nst = ic.enter_context(nc.sbuf_tensor("normw_st", [KDI, 128], F32))
        rowD = ic.enter_context(nc.sbuf_tensor("rowD", [1, NH], F32))
        ps_a = ic.enter_context(nc.psum_tensor("ips_a", [128, 128], F32))
        ps_b = ic.enter_context(nc.psum_tensor("ips_b", [128, 128], F32))
        g0 = Ctr(nc, c, "i_g0")
        gi = Ctr(nc, c, "i_gi")
        v0 = self.xV
        p0 = self.xT
        a0 = self.xA

        @blk.gpsimd
        def _(g):
            gi.inc(g.iota(it128[:, :], pattern=[[1, 128]], base=0, channel_multiplier=-1))
            gi.inc(g.iota(it256[:, 0, :], pattern=[[1, CH]], base=0, channel_multiplier=-1))
            gi.inc(g.iota(it256[:, 1, :], pattern=[[1, CH]], base=-128, channel_multiplier=-1))
            gi.inc(g.iota(itE[:, 0, :], pattern=[[-2, 32]], base=0, channel_multiplier=1))
            gi.inc(g.iota(itE[:, 1, :], pattern=[[-2, 32]], base=-1, channel_multiplier=1))
            g0.inc(g.dma_start(out=self.dtb_col[:, :], in_=self.i_dtb[:, None]), 16)
            g0.inc(g.dma_start(out=self.A_col[:, :], in_=self.i_alog[:, None]), 16)
            g0.inc(g.dma_start(out=rowD[0:1, :], in_=self.i_dpar[None, :]), 16)
            g0.inc(g.dma_start(out=nst[:, :], in_=self.i_normw.rearrange("(k p) -> k p", p=128)), 16)
            g0.inc(g.dma_start(out=self.wcv[:, :, :], in_=self.w_conv.rearrange("(f p) k -> p f k", p=128)), 16)
            g0.inc(g.dma_start(out=self.mselT_sb[:, :], in_=self.i_mselT[:, :]), 16)
            g0.inc(g.dma_start(out=self.cbias_col[:, :], in_=self.i_cbias[:, None]), 16)

        @blk.vector
        def _(v):
            v.wait_ge(g0.h, g0.n)
            v.wait_ge(gi.h, gi.n)
            v.memset(self.eps_col[:, :], EPS)
            v.memset(self.dummyw[:, :], 0.0)
            v.tensor_single_scalar(self.identb[:, :], it128[:, :], 0.0, ALU.is_equal)
            v.tensor_single_scalar(self.identf[:, :], it128[:, :], 0.0, ALU.is_equal)
            v.memset(self.ones_b[:, :], 1.0)
            v.memset(self.ones_f[:, :], 1.0)
            v.tensor_single_scalar(self.Eev[:, :], itE[:, 0, :], 0.0, ALU.is_equal)
            v.tensor_single_scalar(self.Eod[:, :], itE[:, 1, :], 0.0, ALU.is_equal)
            for s in range(2):
                v.tensor_single_scalar(self.utri[:, s, :], it256[:, s, :], 0.0, ALU.is_ge)
                v0.inc(v.tensor_single_scalar(self.maskT[:, s, :], it256[:, s, :], 0.0, ALU.is_lt))
            v.wait_ge(v0.h, v0.n)
            v0.inc(v.tensor_scalar_mul(self.maskT[:, :, :], self.maskT[:, :, :], -1e30))
            v.wait_ge(v0.h, v0.n)
            for e in range(8):
                v.tensor_scalar(self.sel8[:, e, :], self.ones_f[0:8, :], self.identf[0:8, e:e + 1], None, ALU.mult)
            v0.inc(v.memset(self.prev[:, :], 0.0))
            v0.inc(v.memset(self.logP[:, :], 0.0))

        @blk.scalar
        def _(sc):
            sc.wait_ge(g0.h, g0.n)
            a0.inc(sc.activation(self.A_col[:, :], self.A_col[:, :], AF.Exp))
            sc.wait_ge(a0.h, a0.n)
            a0.inc(sc.mul(self.A_col[:, :], self.A_col[:, :], -1.0))

        @blk.tensor
        def _(t):
            t.wait_ge(v0.h, v0.n)
            t.wait_ge(g0.h, g0.n)
            p0.inc(t.transpose(ps_a[:, 0:KDI], nst[:, :], self.identf[0:KDI, 0:KDI]))
            p0.inc(t.matmul(ps_b[0:64, 0:NPAIR], self.ones_f[0:1, 0:64], rowD[0:1, 0::2], start=True, stop=True))
            p0.inc(t.matmul(ps_b[64:128, 0:NPAIR], self.ones_f[0:1, 0:64], rowD[0:1, 1::2], start=True, stop=True))

        @blk.vector
        def _(v):
            v.wait_ge(p0.h, p0.n)
            v.tensor_copy(self.normw[:, :], ps_a[:, 0:KDI])
            v.tensor_copy(self.D_pair[:, :], ps_b[:, 0:NPAIR])

    def p0_ln(self, blk):
        nc, c = self.nc, self.ctx
        self._xn_ctx = ExitStack()
        self.xn = self._xn_ctx.enter_context(nc.sbuf_tensor("xn", [128, KD, TE + 1], BF16))
        ph = ExitStack()
        u_tm = ph.enter_context(nc.sbuf_tensor("u_tm", [128, 2, D], F32))
        xn_tm = ph.enter_context(nc.sbuf_tensor("xn_tm", [128, 2, D], BF16))
        lncol = ph.enter_context(nc.sbuf_tensor("lncol", [128, 2, 4], F32))
        sq = ph.enter_context(nc.sbuf_tensor("sq", [128, D], F32))
        banks = [ph.enter_context(nc.psum_tensor(f"p0_t{i}", [128, 1024], BF16)) for i in range(2)]
        s_ld = [Ctr(nc, c, "p0_sld0"), Ctr(nc, c, "p0_sld1")]
        v_var = Ctr(nc, c, "p0_vvar")
        a_sq = self.xA
        v_xn = Ctr(nc, c, "p0_vxn")
        p_tr = Ctr(nc, c, "p0_ptr")
        v_ev = Ctr(nc, c, "p0_vev")
        s_st = self.xS
        NT = 17
        with ph:
            for i in range(NT):
                nr = 128 if i < 16 else 3
                b = i % 2

                @blk.sync
                def _(s, i=i, nr=nr, b=b):
                    if i >= 2:
                        s.wait_ge(v_xn.h, i - 1)
                    s_ld[b].inc(s.dma_start(out=u_tm[0:nr, b, :], in_=self.u[i * 128:i * 128 + nr, :]), 16)

                @blk.vector
                def _(v, i=i, nr=nr, b=b):
                    v.wait_ge(s_ld[b].h, 16 * (i // 2 + 1))
                    v_var.inc(v.tensor_mul(sq[0:nr, :], u_tm[0:nr, b, :], u_tm[0:nr, b, :]))
                    v.reduce_sum(lncol[0:nr, b, 0:1], u_tm[0:nr, b, :], axis=AX.X)
                    v.wait_ge(v_var.h, v_var.n)
                    v_var.inc(v.reduce_sum(lncol[0:nr, b, 1:2], sq[0:nr, :], axis=AX.X))
                    v.wait_ge(v_var.h, v_var.n)
                    v.tensor_scalar(lncol[0:nr, b, 0:1], lncol[0:nr, b, 0:1], 1.0 / D, None, ALU.mult)
                    v_var.inc(v.tensor_scalar(lncol[0:nr, b, 1:2], lncol[0:nr, b, 1:2], 1.0 / D, None, ALU.mult))
                    v.wait_ge(v_var.h, v_var.n)
                    v_var.inc(v.tensor_mul(lncol[0:nr, b, 2:3], lncol[0:nr, b, 0:1], lncol[0:nr, b, 0:1]))
                    v.wait_ge(v_var.h, v_var.n)
                    v_var.inc(v.tensor_sub(lncol[0:nr, b, 1:2], lncol[0:nr, b, 1:2], lncol[0:nr, b, 2:3]))

                @blk.scalar
                def _(sc, i=i, nr=nr, b=b):
                    sc.wait_ge(v_var.h, v_var.n)
                    a_sq.inc(sc.activation(lncol[0:nr, b, 2:3], lncol[0:nr, b, 1:2], AF.Sqrt, bias=self.eps_col[0:nr, 0:1]))

                @blk.vector
                def _(v, i=i, nr=nr, b=b):
                    v.wait_ge(a_sq.h, a_sq.n)
                    v_var.inc(v.reciprocal(lncol[0:nr, b, 3:4], lncol[0:nr, b, 2:3]))
                    v.wait_ge(v_var.h, v_var.n)
                    v_xn.inc(v.tensor_scalar(xn_tm[0:nr, b, :], u_tm[0:nr, b, :],
                                             lncol[0:nr, b, 0:1], lncol[0:nr, b, 3:4],
                                             ALU.subtract, ALU.mult))

                @blk.tensor
                def _(t, i=i, nr=nr, b=b):
                    t.wait_ge(v_xn.h, v_xn.n)
                    for half in range(2):
                        if i >= 1:
                            t.wait_ge(v_ev.h, 2 * (i - 1) + half + 1)
                        for kk in range(8):
                            k = half * 8 + kk
                            p_tr.inc(t.transpose(banks[half][:, kk * 128:kk * 128 + nr],
                                                 xn_tm[0:nr, b, k * 128:(k + 1) * 128],
                                                 self.identb[0:nr, 0:nr]))

                @blk.vector
                def _(v, i=i, nr=nr):
                    r0 = i * 128
                    for half in range(2):
                        v.wait_ge(p_tr.h, 16 * i + 8 * (half + 1))
                        src = banks[half].ap().rearrange("p (k x) -> p k x", k=8)[:, :, 0:nr]
                        v_ev.inc(v.tensor_copy(self.xn[:, half * 8:(half + 1) * 8, r0:r0 + nr], src))

            @blk.sync
            def _(s):
                s.wait_ge(v_ev.h, v_ev.n)
                for k in range(KD):
                    s_st.inc(s.dma_start(out=self.d_xn[k * 128:(k + 1) * 128, 0:TE], in_=self.xn[:, k, 0:TE]), 16)
                s.wait_ge(s_st.h, s_st.n)

    def p1_inproj(self, blk, ph):
        nc, c = self.nc, self.ctx
        wblk = ph.enter_context(nc.sbuf_tensor("wblk", [128, 2, D], BF16))
        wT = ph.enter_context(nc.sbuf_tensor("wT", [128, 2, KD, 128], BF16))
        obuf = ph.enter_context(nc.sbuf_tensor("obuf", [128, 2, 4, 512], BF16))
        hbuf = ph.enter_context(nc.sbuf_tensor("hbuf", [128, 8], BF16))
        trb = [ph.enter_context(nc.psum_tensor(f"p1_tr{i}", [128, 1024], BF16)) for i in range(2)]
        pm = [ph.enter_context(nc.psum_tensor(f"p1_m{t}", [128, 512], F32)) for t in range(4)]
        phh = ph.enter_context(nc.psum_tensor("p1_h", [128, 8], F32))
        g_w = [Ctr(nc, c, "p1_gw0"), Ctr(nc, c, "p1_gw1")]
        p_tr = self.xT
        v_wtr = Ctr(nc, c, "p1_vwtr")
        p_mm = self.xT
        v_oe = self.xV
        s_st = self.xS
        p_hm = self.xT
        v_he = self.xV
        s_he = self.xS
        tr_done = []
        mm_done = []
        oe_done = []
        st_done = []
        he_done = []

        for m in range(M_INP):
            nr = 64 if m == M_INP - 1 else 128
            b = m % 2
            xbc = 32 <= m < 66

            @blk.gpsimd
            def _(g, m=m, nr=nr, b=b):
                if m >= 2:
                    g.wait_ge(p_tr.h, tr_done[m - 2])
                g_w[b].inc(g.dma_start(out=wblk[0:nr, b, :], in_=self.w_inproj[m * 128:m * 128 + nr, :]), 16)

            @blk.tensor
            def _(t, m=m, nr=nr, b=b):
                t.wait_ge(g_w[b].h, 16 * (m // 2 + 1))
                for half in range(2):
                    if m >= 1:
                        t.wait_ge(v_wtr.h, 2 * (m - 1) + half + 1)
                    for kk in range(8):
                        k = half * 8 + kk
                        p_tr.inc(t.transpose(trb[half][:, kk * 128:kk * 128 + nr],
                                             wblk[0:nr, b, k * 128:(k + 1) * 128],
                                             self.identb[0:nr, 0:nr]))
                tr_done.append(p_tr.n)

            @blk.vector
            def _(v, m=m, nr=nr, b=b):
                for half in range(2):
                    v.wait_ge(p_tr.h, tr_done[m] - 8 * (1 - half))
                    if m >= 2:
                        v.wait_ge(p_mm.h, mm_done[m - 1])
                    src = trb[half].ap().rearrange("p (k x) -> p k x", k=8)[:, :, 0:nr]
                    v_wtr.inc(v.tensor_copy(wT[:, b, half * 8:(half + 1) * 8, 0:nr], src))

            @blk.tensor
            def _(t, m=m, nr=nr, b=b, xbc=xbc):
                t.wait_ge(v_wtr.h, 2 * (m + 1))
                if m >= 1:
                    t.wait_ge(v_oe.h, oe_done[m - 1])
                for tt in range(4):
                    for k in range(KD):
                        mm = t.matmul(pm[tt][0:nr, :], wT[:, b, k, 0:nr],
                                      self.xn[:, k, 3 + tt * 512: 3 + (tt + 1) * 512],
                                      start=(k == 0), stop=(k == KD - 1))
                        if k == KD - 1:
                            p_mm.inc(mm)
                if xbc:
                    if m >= 33:
                        t.wait_ge(v_he.h, he_done[-1][0])
                    for k in range(KD):
                        mm = t.matmul(phh[0:nr, :], wT[:, b, k, 0:nr], self.xn[:, k, 0:8],
                                      start=(k == 0), stop=(k == KD - 1))
                        if k == KD - 1:
                            p_hm.inc(mm)
                mm_done.append(p_mm.n)

            if m < 32:
                @blk.scalar
                def _(sc, m=m, nr=nr, b=b):
                    for tt in range(4):
                        sc.wait_ge(p_mm.h, mm_done[m] - (3 - tt))
                        if m >= 2:
                            sc.wait_ge(s_st.h, st_done[m - 1])
                        v_oe.inc(sc.activation(obuf[0:nr, b, tt, :], pm[tt][0:nr, :], AF.Silu))

            @blk.vector
            def _(v, m=m, nr=nr, b=b, xbc=xbc):
                for tt in range(4):
                    if m < 32:
                        break
                    v.wait_ge(p_mm.h, mm_done[m] - (3 - tt))
                    if m >= 2:
                        v.wait_ge(s_st.h, st_done[m - 1])
                    if m == M_INP - 1:
                        v_oe.inc(v.tensor_copy(self.dt_raw[:, tt * 512:(tt + 1) * 512], pm[tt][0:64, :]))
                    else:
                        v_oe.inc(v.tensor_copy(obuf[0:nr, b, tt, :], pm[tt][0:nr, :]))
                if xbc:
                    v.wait_ge(p_hm.h, p_hm.n)
                    if m >= 34:
                        v.wait_ge(s_he.h, s_he.n)
                    v_he.inc(v.tensor_copy(hbuf[0:nr, :], phh[0:nr, :]))
                oe_done.append(v_oe.n)

            @blk.sync
            def _(s, m=m, nr=nr, b=b, xbc=xbc):
                if m < 66:
                    for tt in range(4):
                        s.wait_ge(v_oe.h, oe_done[m] - (3 - tt))
                        if m < 32:
                            dst = self.d_z[m * 128:(m + 1) * 128, tt * 512:(tt + 1) * 512]
                        else:
                            r = (m - 32) * 128
                            dst = self.d_xbc[r:r + 128, 3 + tt * 512: 3 + (tt + 1) * 512]
                        s_st.inc(s.dma_start(out=dst, in_=obuf[0:nr, b, tt, :]), 16)
                if xbc:
                    s.wait_ge(v_he.h, v_he.n)
                    r = (m - 32) * 128
                    s_he.inc(s.dma_start(out=self.d_xbc[r:r + 128, 0:3], in_=hbuf[0:128, 0:3]), 16)
                    he_done.append((v_he.n, s_he.n))
                st_done.append(s_st.n)

        @blk.sync
        def _(s):
            s.wait_ge(s_st.h, s_st.n)
            s.wait_ge(s_he.h, s_he.n)

    def p3a_router(self, blk, ph):
        nc, c = self.nc, self.ctx
        wr = ph.enter_context(nc.sbuf_tensor("wr", [128, KD, 8], BF16))
        cmb = ph.enter_context(nc.sbuf_tensor("cmb", [128, 2, 8], F32))
        rcol = ph.enter_context(nc.sbuf_tensor("rcol", [128, 2, 4], F32))
        esb = ph.enter_context(nc.sbuf_tensor("esb", [128, 2, 8], F32))
        ps_r = ph.enter_context(nc.psum_tensor("p3a_r", [128, 8], F32))
        ps_c = ph.enter_context(nc.psum_tensor("p3a_c", [8, 128], F32))
        g_w = Ctr(nc, c, "r_gw")
        p_r = self.xT
        v_s = self.xV
        a_e = self.xA
        v_c = self.xV
        p_t = self.xT
        v_o = self.xV
        sub_done = []
        cmb_done = []
        evo_done = []

        @blk.gpsimd
        def _(g):
            g_w.inc(g.dma_start(out=wr[:, :, :], in_=self.w_router.rearrange("(k p) e -> p k e", p=128)), 16)

        for i in range(KD):
            b = i % 2

            @blk.tensor
            def _(t, i=i):
                if i == 0:
                    t.wait_ge(g_w.h, 16)
                else:
                    t.wait_ge(v_s.h, sub_done[i - 1])
                for k in range(KD):
                    mm = t.matmul(ps_r[:, :], self.xn[:, k, 3 + i * 128: 3 + (i + 1) * 128],
                                  wr[:, k, :], start=(k == 0), stop=(k == KD - 1))
                    if k == KD - 1:
                        p_r.inc(mm)

            @blk.vector
            def _(v, i=i, b=b):
                v.wait_ge(p_r.h, p_r.n)
                v_s.inc(v.reduce_max(rcol[:, b, 0:1], ps_r[:, :], axis=AX.X))
                v.wait_ge(v_s.h, v_s.n)
                v_s.inc(v.tensor_scalar(esb[:, b, :], ps_r[:, :], rcol[:, b, 0:1], None, ALU.subtract))
                sub_done.append(v_s.n)

            @blk.scalar
            def _(sc, i=i, b=b):
                sc.wait_ge(v_s.h, sub_done[i])
                a_e.inc(sc.activation(esb[:, b, :], esb[:, b, :], AF.Exp))

            @blk.vector
            def _(v, i=i, b=b):
                v.wait_ge(a_e.h, a_e.n)
                v_s.inc(v.reduce_max(rcol[:, b, 0:1], esb[:, b, :], axis=AX.X))
                v_s.inc(v.reduce_sum(rcol[:, b, 1:2], esb[:, b, :], axis=AX.X))
                v.wait_ge(v_s.h, v_s.n)
                v_s.inc(v.reciprocal(rcol[:, b, 1:2], rcol[:, b, 1:2]))
                v.wait_ge(v_s.h, v_s.n)
                if i >= 2:
                    v.wait_ge(p_t.h, p_t.n - 1)
                v_s.inc(v.tensor_scalar(cmb[:, b, :], esb[:, b, :], rcol[:, b, 0:1], None, ALU.is_ge))
                v.wait_ge(v_s.h, v_s.n)
                v_s.inc(v.scalar_tensor_tensor(cmb[:, b, :], cmb[:, b, :], -1e30, esb[:, b, :], ALU.mult, ALU.add))
                v.wait_ge(v_s.h, v_s.n)
                v_s.inc(v.reduce_max(rcol[:, b, 2:3], cmb[:, b, :], axis=AX.X))
                v.wait_ge(v_s.h, v_s.n)
                v_s.inc(v.tensor_scalar(cmb[:, b, :], esb[:, b, :], rcol[:, b, 2:3], None, ALU.is_ge))
                v.wait_ge(v_s.h, v_s.n)
                v_c.inc(v.scalar_tensor_tensor(cmb[:, b, :], esb[:, b, :], rcol[:, b, 1:2], cmb[:, b, :], ALU.mult, ALU.mult))
                cmb_done.append(v_c.n)

            @blk.tensor
            def _(t, i=i, b=b):
                t.wait_ge(v_c.h, cmb_done[i])
                if i >= 1:
                    t.wait_ge(v_o.h, evo_done[i - 1])
                p_t.inc(t.transpose(ps_c[:, :], cmb[:, b, :], self.identf[:, :]))

            @blk.vector
            def _(v, i=i):
                v.wait_ge(p_t.h, p_t.n)
                v_o.inc(v.tensor_copy(self.combine[0:8, i * 128:(i + 1) * 128], ps_c[0:8, :]))
                evo_done.append(v_o.n)

    def p2_stats(self, blk, ph):
        nc, c = self.nc, self.ctx
        st = ph.enter_context(nc.sbuf_tensor("st", [NH, 5, T], F32))
        datm = ph.enter_context(nc.sbuf_tensor("datm", [128, KD, 64], F32))
        cslast = ph.enter_context(nc.sbuf_tensor("cslast", [NH, NCH], F32))
        logD = ph.enter_context(nc.sbuf_tensor("logD", [NH, NCH], F32))
        ecl = ph.enter_context(nc.sbuf_tensor("ecl", [NH, NCH], F32))
        eD = ph.enter_context(nc.sbuf_tensor("eD", [NH, NCH], F32))
        vb = ph.enter_context(nc.sbuf_tensor("vb", [64, 64], F32))
        pf = [ph.enter_context(nc.psum_tensor(f"p2s_f{i}", [128, 512], F32)) for i in range(2)]
        pc = ph.enter_context(nc.psum_tensor("p2s_c", [64, 256], F32))
        pq = ph.enter_context(nc.psum_tensor("p2s_q", [128, 32], F32))
        vx = self.xV
        ax = self.xA
        px = self.xT

        @blk.vector
        def _(v):
            vx.inc(v.tensor_scalar(st[:, 0, :], self.dt_raw[:, :], self.dtb_col[:, 0:1], None, ALU.add))

        @blk.scalar
        def _(sc):
            sc.wait_ge(vx.h, vx.n)
            ax.inc(sc.activation(st[:, 1, :], st[:, 0, :], AF.Abs))
            sc.wait_ge(ax.h, ax.n)
            ax.inc(sc.activation(st[:, 1, :], st[:, 1, :], AF.Exp, scale=-1.0))
            sc.wait_ge(ax.h, ax.n)
            ax.inc(sc.activation(st[:, 1, :], st[:, 1, :], AF.Ln, bias=1.0))

        @blk.vector
        def _(v):
            v.wait_ge(ax.h, ax.n)
            vx.inc(v.tensor_scalar_max(st[:, 2, :], st[:, 0, :], 0.0))
            v.wait_ge(vx.h, vx.n)
            vx.inc(v.tensor_add(self.dt_all[:, :], st[:, 2, :], st[:, 1, :]))
            v.wait_ge(vx.h, vx.n)
            vx.inc(v.tensor_scalar(st[:, 0, :], self.dt_all[:, :], self.A_col[:, 0:1], None, ALU.mult))

        @blk.scalar
        def _(sc):
            sc.wait_ge(vx.h, vx.n)
            ax.inc(sc.activation(st[:, 3, :], self.dt_all[:, :], AF.Ln))

        @blk.tensor
        def _(t):
            t.wait_ge(vx.h, vx.n)
            for k in range(KD):
                px.inc(t.transpose(pf[k // 8][:, (k % 8) * 64:(k % 8) * 64 + 64],
                                   st[0:64, 0, k * 128:(k + 1) * 128], self.identf[0:64, 0:64]))

        @blk.vector
        def _(v):
            v.wait_ge(px.h, px.n)
            for bk in range(2):
                src = pf[bk].ap().rearrange("p (k x) -> p k x", k=8)
                vx.inc(v.tensor_copy(datm[:, bk * 8:(bk + 1) * 8, :], src))

        cs_ev = []
        for ci in range(NCH):
            @blk.tensor
            def _(t, ci=ci):
                if ci == 0:
                    t.wait_ge(vx.h, vx.n)
                else:
                    t.wait_ge(vx.h, cs_ev[ci - 1])
                for s in range(2):
                    mm = t.matmul(pc[0:64, :], datm[:, 2 * ci + s, :], self.utri[:, s, :],
                                  start=(s == 0), stop=(s == 1))
                    if s == 1:
                        px.inc(mm)

            @blk.vector
            def _(v, ci=ci):
                v.wait_ge(px.h, px.n)
                vx.inc(v.tensor_copy(self.cs_all[:, ci * CH:(ci + 1) * CH], pc[0:64, :]))
                cs_ev.append(vx.n)

        @blk.vector
        def _(v):
            v.wait_ge(vx.h, vx.n)
            vx.inc(v.tensor_copy(self.cs_hi[:, :], self.cs_all[:, :]))
            v.wait_ge(vx.h, vx.n)
            vx.inc(v.tensor_sub(self.cs_lo[:, :], self.cs_all[:, :], self.cs_hi[:, :]))
            vx.inc(v.tensor_copy(cslast[:, :], self.cs_all[:, CH - 1::CH]))
            v.wait_ge(vx.h, vx.n)
            vx.inc(v.reduce_sum(self.logP[:, 0:1], cslast[:, :], axis=AX.X))
            vx.inc(v.memset(logD[:, 0:1], 0.0))
            v.wait_ge(vx.h, vx.n)
            for ci in range(1, NCH):
                vx.inc(v.tensor_add(logD[:, ci:ci + 1], logD[:, ci - 1:ci], cslast[:, ci - 1:ci]))
                v.wait_ge(vx.h, vx.n)
            v.wait_ge(ax.h, ax.n)
            vx.inc(v.tensor_sub(st[:, 3, :], self.cs_all[:, :], st[:, 3, :]))
            for ci in range(NCH):
                vx.inc(v.tensor_scalar(st[:, 4, ci * CH:(ci + 1) * CH],
                                       self.cs_all[:, ci * CH:(ci + 1) * CH],
                                       cslast[:, ci:ci + 1], None, ALU.subtract))

        @blk.scalar
        def _(sc):
            sc.wait_ge(vx.h, vx.n)
            sc.activation(self.ecs16[:, :], self.cs_all[:, :], AF.Exp)
            sc.activation(st[:, 4, :], st[:, 4, :], AF.Exp, scale=-1.0)
            sc.activation(ecl[:, :], cslast[:, :], AF.Exp)
            ax.inc(sc.activation(eD[:, :], logD[:, :], AF.Exp))

        @blk.vector
        def _(v):
            v.wait_ge(ax.h, ax.n)
            vx.inc(v.tensor_mul(st[:, 4, :], self.dt_all[:, :], st[:, 4, :]))

        tr_ev = []
        for ci in range(NCH):
            @blk.tensor
            def _(t, ci=ci):
                if ci == 0:
                    t.wait_ge(vx.h, vx.n)
                elif ci >= 2:
                    t.wait_ge(vx.h, tr_ev[ci - 2])
                for s in range(2):
                    px.inc(t.transpose(pf[ci % 2][:, s * 64:s * 64 + 64],
                                       st[0:64, 3, ci * CH + s * 128: ci * CH + (s + 1) * 128],
                                       self.identf[0:64, 0:64]))
                    px.inc(t.transpose(pf[ci % 2][:, 128 + s * 64: 128 + s * 64 + 64],
                                       st[0:64, 4, ci * CH + s * 128: ci * CH + (s + 1) * 128],
                                       self.identf[0:64, 0:64]))

            @blk.vector
            def _(v, ci=ci):
                v.wait_ge(px.h, px.n)
                src = pf[ci % 2].ap().rearrange("p (s x) -> p s x", x=64)
                vx.inc(v.tensor_copy(self.csldt[:, ci, :, :], src[:, 0:2, :]))
                vx.inc(v.tensor_copy(self.g_tm[:, ci, :, :], src[:, 2:4, :]))
                tr_ev.append(vx.n)

        for ci in range(NCH):
            for which in range(2):
                srccol = ecl if which == 0 else eD
                dst = self.cdcol if which == 0 else self.Dcol

                @blk.vector
                def _(v, ci=ci, srccol=srccol):
                    v.wait_ge(vx.h, vx.n)
                    v.wait_ge(px.h, px.n)
                    vx.inc(v.tensor_scalar(vb[:, :], self.ones_f[:, 0:64], srccol[:, ci:ci + 1], None, ALU.mult))

                @blk.tensor
                def _(t):
                    t.wait_ge(vx.h, vx.n)
                    t.matmul(pq[0:64, :], vb[:, :], self.Eev[:, :], start=True, stop=True)
                    px.inc(t.matmul(pq[64:128, :], vb[:, :], self.Eod[:, :], start=True, stop=True))

                @blk.vector
                def _(v, ci=ci, dst=dst):
                    v.wait_ge(px.h, px.n)
                    vx.inc(v.tensor_copy(dst[:, ci, :], pq[:, :]))

    def p2_main(self, blk, ph):
        nc, c = self.nc, self.ctx
        xbcb = ph.enter_context(nc.sbuf_tensor("xbcb", [128, NXBC, CH + 3], BF16))
        cacc = ph.enter_context(nc.sbuf_tensor("cacc", [128, 2, CH], F32))
        xsil = ph.enter_context(nc.sbuf_tensor("xsil", [128, 4, CH], BF16))
        xtmb = ph.enter_context(nc.sbuf_tensor("xtmb", [128, 2, DI], BF16))
        dxb = ph.enter_context(nc.sbuf_tensor("dxb", [128, KDI, CH], BF16))
        btm = ph.enter_context(nc.sbuf_tensor("btm", [128, 2, 128], BF16))
        bg = ph.enter_context(nc.sbuf_tensor("bg", [128, 4, 128], BF16))
        prevb = ph.enter_context(nc.sbuf_tensor("prevb", [128, NPAIR * DST], BF16))
        zc0 = ph.enter_context(nc.sbuf_tensor("zc0", [64, 1], F32))
        pxb = [ph.enter_context(nc.psum_tensor(f"p2_x{i}", [128, 1024], BF16)) for i in range(2)]
        ps_bt = ph.enter_context(nc.psum_tensor("p2_bt", [128, 256], BF16))
        pst = [ph.enter_context(nc.psum_tensor(f"p2_s{i}", [128, 128], F32)) for i in range(2)]
        s_x = Ctr(nc, c, "p2_sx")
        g_cv = Ctr(nc, c, "p2_gcv")
        v_cv = Ctr(nc, c, "p2_vcv")
        a_sl = Ctr(nc, c, "p2_asl")
        p_tx = Ctr(nc, c, "p2_ptx")
        v_xt = Ctr(nc, c, "p2_vxt")
        v_dx = Ctr(nc, c, "p2_vdx")
        v_bt = self.xV
        v_bg = Ctr(nc, c, "p2_vbg")
        v_pb = Ctr(nc, c, "p2_vpb")
        p_sm = Ctr(nc, c, "p2_psm")
        v_pu = Ctr(nc, c, "p2_vpu")
        s_xt = self.xS
        s_dx = self.xS
        s_pv = self.xS
        g_cc = Ctr(nc, c, "p2_gcc")
        self.cc_sem = g_cc
        cv_done = [(0, 0)]
        sm_done = [0]
        sxt_done = [0]
        sdx_done = [0]
        spv_done = [0]

        for ci in range(NCH):
            @blk.sync
            def _(s, ci=ci):
                if ci >= 1:
                    s.wait_ge(v_cv.h, cv_done[ci][0])
                    s.wait_ge(g_cv.h, cv_done[ci][1])
                s_x.inc(s.dma_start(
                    out=xbcb[:, :, :],
                    in_=self.d_xbc.rearrange("(f p) t -> p f t", p=128)[:, :, ci * CH: ci * CH + CH + 3]), 16)

            sil_base = a_sl.n
            tx_base = p_tx.n
            dx_base = v_dx.n
            conv_done = {}

            @blk.vector
            def _(v, ci=ci, sil_base=sil_base, tx_base=tx_base):
                v.wait_ge(s_x.h, 16 * (ci + 1))
                for f in range(NXBC + 2):
                    if f < NXBC:
                        b = f % 2
                        if sil_base + f - 1 > 0:
                            v.wait_ge(a_sl.h, sil_base + f - 1)
                        v_cv.inc(v.tensor_scalar(cacc[:, b, :], xbcb[:, f, 0:CH], self.wcv[:, f, 0:1], None, ALU.mult))
                        for kk in range(1, 4):
                            v.wait_ge(v_cv.h, v_cv.n)
                            v_cv.inc(v.scalar_tensor_tensor(cacc[:, b, :], xbcb[:, f, kk:CH + kk], self.wcv[:, f, kk:kk + 1],
                                                            cacc[:, b, :], ALU.mult, ALU.add))
                        conv_done[f] = (v_cv, v_cv.n)
                    ff = f - 2
                    if 0 <= ff < 32:
                        v.wait_ge(a_sl.h, sil_base + ff + 1)
                        if ff == 0 and ci >= 1:
                            v.wait_ge(s_pv.h, spv_done[ci])
                        v_dx.inc(v.tensor_scalar(dxb[:, ff, :], xsil[:, ff % 4, :],
                                                 self.D_pair[:, ff:ff + 1], None, ALU.mult))
                        if ff % 8 == 7:
                            g4 = ff // 8
                            v.wait_ge(p_tx.h, tx_base + 16 * (g4 + 1))
                            if g4 == 0 and ci >= 1:
                                v.wait_ge(p_sm.h, sm_done[ci])
                                v.wait_ge(s_pv.h, spv_done[ci])
                            for j in range(2):
                                src = pxb[j].ap().rearrange("p (k x) -> p k x", k=8)
                                v_xt.inc(v.tensor_copy(
                                    xtmb.ap().rearrange("p j (k x) -> p j k x", k=KDI)[:, j, g4 * 8:(g4 + 1) * 8, :], src))
                cv_done.append((v_cv.n, g_cv.n))

            @blk.scalar
            def _(sc, ci=ci, conv_done=conv_done, tx_base=tx_base, dx_base=dx_base):
                for f in range(NXBC):
                    b = f % 2
                    cctr, cval = conv_done[f]
                    sc.wait_ge(cctr.h, cval)
                    if f < 32:
                        if f >= 4:
                            sc.wait_ge(p_tx.h, tx_base + 2 * (f - 3))
                            sc.wait_ge(v_dx.h, dx_base + f - 3)
                        a_sl.inc(sc.activation(xsil[:, f % 4, :], cacc[:, b, :], AF.Silu))
                    elif f == 32:
                        a_sl.inc(sc.activation(self.B_all[:, ci * CH:(ci + 1) * CH], cacc[:, b, :], AF.Silu))
                    else:
                        a_sl.inc(sc.activation(self.C_all[:, ci * CH:(ci + 1) * CH], cacc[:, b, :], AF.Silu))

            sil2 = a_sl.n - NXBC

            @blk.tensor
            def _(t, ci=ci, sil2=sil2):
                for f in range(32):
                    g4 = f // 8
                    t.wait_ge(a_sl.h, sil2 + f + 1)
                    if f % 8 == 0 and (ci > 0 or g4 > 0):
                        t.wait_ge(v_xt.h, 2 * (4 * ci + g4))
                    for j in range(2):
                        p_tx.inc(t.transpose(pxb[j][:, (f % 8) * 128:(f % 8) * 128 + 128],
                                             xsil[:, f % 4, j * 128:(j + 1) * 128],
                                             self.identb[:, :]))
                t.wait_ge(a_sl.h, sil2 + 33)
                for s in range(2):
                    p_tx.inc(t.transpose(ps_bt[:, s * 128:(s + 1) * 128],
                                         self.B_all[:, ci * CH + s * 128: ci * CH + (s + 1) * 128],
                                         self.identb[:, :]))

            tx2 = p_tx.n - 66
            bg_base = v_bg.n
            sm_base = p_sm.n

            @blk.vector
            def _(v, ci=ci, tx2=tx2):
                v.wait_ge(p_tx.h, tx2 + 66)
                v_bt.inc(v.tensor_copy(btm.ap().rearrange("p s x -> p (s x)"), ps_bt[:, :]))
                if ci >= 1:
                    v.wait_ge(s_pv.h, spv_done[ci])
                v_pb.inc(v.tensor_copy(prevb[:, :], self.prev[:, :]))

            pu_base = v_pu.n

            @blk.vector
            def _(v, ci=ci, sm_base=sm_base):
                v.wait_ge(v_pb.h, v_pb.n)
                for j in range(NPAIR):
                    if j >= 1:
                        v.wait_ge(p_sm.h, sm_base + j)
                    for h in (2 * j, 2 * j + 1):
                        for s in range(2):
                            ins = v.tensor_scalar(bg[:, (h % 2) * 2 + s, :], btm[:, s, :],
                                                  self.g_tm[:, ci, s, h:h + 1], None, ALU.mult)
                        v_bg.inc(ins)
                    if j >= 1:
                        jj = j - 1
                        v_pu.inc(v.scalar_tensor_tensor(self.prev[:, jj * DST:(jj + 1) * DST],
                                                        self.prev[:, jj * DST:(jj + 1) * DST],
                                                        self.cdcol[:, ci, jj:jj + 1],
                                                        pst[jj % 2][:, :], ALU.mult, ALU.add))
                v.wait_ge(p_sm.h, sm_base + NPAIR)
                jj = NPAIR - 1
                v_pu.inc(v.scalar_tensor_tensor(self.prev[:, jj * DST:(jj + 1) * DST],
                                                self.prev[:, jj * DST:(jj + 1) * DST],
                                                self.cdcol[:, ci, jj:jj + 1],
                                                pst[jj % 2][:, :], ALU.mult, ALU.add))

            @blk.tensor
            def _(t, ci=ci, bg_base=bg_base, pu_base=pu_base):
                for j in range(NPAIR):
                    t.wait_ge(v_bg.h, bg_base + 2 * (j + 1))
                    if j >= 2:
                        t.wait_ge(v_pu.h, pu_base + j - 1)
                    h0, h1 = 2 * j, 2 * j + 1
                    t.matmul(pst[j % 2][0:64, :], xtmb[:, 0, 64 * h0:64 * h0 + 64], bg[:, 0, :],
                             start=True, stop=False, tile_position=(0, 0))
                    t.matmul(pst[j % 2][0:64, :], xtmb[:, 1, 64 * h0:64 * h0 + 64], bg[:, 1, :],
                             start=False, stop=True, tile_position=(0, 0))
                    t.matmul(pst[j % 2][64:128, :], xtmb[:, 0, 64 * h1:64 * h1 + 64], bg[:, 2, :],
                             start=True, stop=False, tile_position=(0, 64))
                    mm = t.matmul(pst[j % 2][64:128, :], xtmb[:, 1, 64 * h1:64 * h1 + 64], bg[:, 3, :],
                                  start=False, stop=True, tile_position=(0, 64))
                    p_sm.inc(mm)
                sm_done.append(p_sm.n)

            @blk.sync
            def _(s, ci=ci):
                s.wait_ge(v_xt.h, 8 * (ci + 1))
                for j in range(2):
                    s_xt.inc(s.dma_start(out=self.d_xtm[ci * CH + j * 128: ci * CH + (j + 1) * 128, :],
                                         in_=xtmb[:, j, :]), 16)
                sxt_done.append(s_xt.n)
                s.wait_ge(v_dx.h, 32 * (ci + 1))
                s_dx.inc(s.dma_start(out=self.d_dx.rearrange("(f p) t -> p f t", p=128)[:, :, ci * CH:(ci + 1) * CH],
                                     in_=dxb[:, :, :]), 16)
                sdx_done.append(s_dx.n)
                s.wait_ge(v_pb.h, ci + 1)
                s_pv.inc(s.dma_start(out=self.d_prev[ci, :, :], in_=prevb[:, :]), 16)
                spv_done.append(s_pv.n)

        @blk.vector
        def _(v):
            v.wait_ge(v_pu.h, v_pu.n)
            v_bt.inc(v.memset(zc0[:, 0:1], 0.0))

        @blk.sync
        def _(s):
            s.wait_ge(v_pu.h, v_pu.n)
            s.wait_ge(v_bt.h, v_bt.n)
            s_pv.inc(s.dma_start(out=self.cc_in[:, 0:NPAIR * DST], in_=self.prev[:, :]), 16)
            s_pv.inc(s.dma_start(out=self.cc_in[0:64, NPAIR * DST:NPAIR * DST + 1], in_=self.logP[:, 0:1], allow_slow_non_contiguous=True), 16)
            s_pv.inc(s.dma_start(out=self.cc_in[64:128, NPAIR * DST:NPAIR * DST + 1], in_=zc0[:, 0:1], allow_slow_non_contiguous=True), 16)

        self.s_xt_total = (s_pv, s_pv.n)

        @blk.gpsimd
        def _(g):
            g.wait_ge(s_pv.h, s_pv.n)
            g_cc.inc(g.collective_compute("AllGather", ALU.bypass,
                                          replica_groups=[list(range(8))],
                                          ins=[self.cc_in.opt()], outs=[self.cc_out.opt()]))

    def p3b_ffn(self, blk, ph):
        nc, c = self.nc, self.ctx
        xnh = ph.enter_context(nc.sbuf_tensor("xnh", [128, KD, 1024], BF16))
        acc = ph.enter_context(nc.sbuf_tensor("facc", [128, KD, 1024], BF16))
        he = ph.enter_context(nc.sbuf_tensor("he", [128, KE, 1024], BF16))
        wgu = ph.enter_context(nc.sbuf_tensor("wgu", [128, 2, 2, KD, 128], BF16))
        sgb = ph.enter_context(nc.sbuf_tensor("sgb", [128, 2, 2, 512], BF16))
        tb = ph.enter_context(nc.sbuf_tensor("tb", [128, 2, 512], BF16))
        cbe = ph.enter_context(nc.sbuf_tensor("cbe", [128, 1024], BF16))
        wd = ph.enter_context(nc.sbuf_tensor("wd", [128, 2, KE, 512], BF16))
        pp = [ph.enter_context(nc.psum_tensor(f"f_p{i}", [128, 512], F32)) for i in range(8)]
        pcb = pp[0]
        g_f = [Ctr(nc, c, "f_gf0"), Ctr(nc, c, "f_gf1")]
        g_dve = self.xV
        p_f = Ctr(nc, c, "f_pf")
        a_f = Ctr(nc, c, "f_af")
        v_f = Ctr(nc, c, "f_vf")
        p_cb = Ctr(nc, c, "f_pcb")
        v_cb = self.xV
        g_d = [Ctr(nc, c, "f_gd0"), Ctr(nc, c, "f_gd1")]
        p_d = Ctr(nc, c, "f_pd")
        v_d = Ctr(nc, c, "f_vd")
        s_h = Ctr(nc, c, "f_sh")
        s_o = self.xS

        experts = [("s", 0), ("s", 1)] + [("e", e) for e in range(8)]

        def wg_ap(kind, idx, dh):
            col = (idx * KE + dh) * 128 if kind == "s" else dh * 128
            src = self.ws_gate if kind == "s" else self.we_gate[idx]
            return src[:, col:col + 128].rearrange("(kc p) o -> p kc o", p=128)

        def wu_ap(kind, idx, dh):
            col = (idx * KE + dh) * 128 if kind == "s" else dh * 128
            src = self.ws_up if kind == "s" else self.we_up[idx]
            return src[:, col:col + 128].rearrange("(kc p) o -> p kc o", p=128)

        def wd_ap(kind, idx, q):
            r0 = idx * DE if kind == "s" else 0
            src = self.ws_down if kind == "s" else self.we_down[idx]
            return src[r0:r0 + DE, q * 512:(q + 1) * 512].rearrange("(kc p) o -> p kc o", p=128)

        for H in range(2):
            off = H * 1024
            ld_done = s_h.n

            @blk.sync
            def _(s, H=H, off=off):
                if H == 1:
                    s.wait_ge(p_f.h, p_f.n)
                    s.wait_ge(p_d.h, p_d.n)
                s_h.inc(s.dma_start(
                    out=xnh[:, :, :],
                    in_=self.d_xn.rearrange("(k p) t -> p k t", p=128)[:, :, 3 + off: 3 + off + 1024]), 16)

            for ei, (kind, idx) in enumerate(experts):
                has_cb = kind == "e"
                if has_cb:
                    for t2 in range(2):
                        @blk.tensor
                        def _(t, idx=idx, t2=t2, off=off):
                            t.wait_ge(v_cb.h, v_cb.n)
                            t.wait_ge(v_d.h, v_d.n)
                            p_cb.inc(t.matmul(pcb[:, :], self.sel8[:, idx, :],
                                              self.combine[0:8, off + t2 * 512: off + (t2 + 1) * 512],
                                              start=True, stop=True))

                        @blk.vector
                        def _(v, t2=t2):
                            v.wait_ge(p_cb.h, p_cb.n)
                            v_cb.inc(v.tensor_copy(cbe[:, t2 * 512:(t2 + 1) * 512], pcb[:, :]))

                gf_list = []
                pf_base = p_f.n
                af_base = a_f.n
                vf_base = v_f.n
                pd_prev = p_d.n

                @blk.gpsimd
                def _(g, kind=kind, idx=idx, pf_base=pf_base):
                    for dh in range(KE):
                        if dh >= 2:
                            g.wait_ge(p_f.h, pf_base + 4 * (dh - 1))
                        g_f[dh % 2].inc(g.dma_start(out=wgu[:, dh % 2, 0, :, :], in_=wg_ap(kind, idx, dh)), 16)
                        g_f[dh % 2].inc(g.dma_start(out=wgu[:, dh % 2, 1, :, :], in_=wu_ap(kind, idx, dh)), 16)
                        gf_list.append(g_f[dh % 2].n)

                vcb_now = v_cb.n
                vd_prev = v_d.n

                @blk.tensor
                def _(t, ei=ei, H=H, af_base=af_base, vf_base=vf_base, vcb_now=vcb_now, vd_prev=vd_prev):
                    for dh in range(KE):
                        t.wait_ge(g_f[dh % 2].h, gf_list[dh])
                        if dh == 0:
                            t.wait_ge(v_cb.h, vcb_now)
                            t.wait_ge(s_h.h, 16 * (H + 1))
                            t.wait_ge(v_d.h, vd_prev)
                        if dh >= 2:
                            t.wait_ge(a_f.h, af_base + 2 * (dh - 1))
                            t.wait_ge(v_f.h, vf_base + 2 * (dh - 1))
                        for gu in range(2):
                            for t2 in range(2):
                                bank = pp[4 * (dh % 2) + 2 * gu + t2]
                                for k in range(KD):
                                    mm = t.matmul(bank[:, :], wgu[:, dh % 2, gu, k, :],
                                                  xnh[:, k, t2 * 512:(t2 + 1) * 512],
                                                  start=(k == 0), stop=(k == KD - 1))
                                    if k == KD - 1:
                                        p_f.inc(mm)

                @blk.scalar
                def _(sc, pf_base=pf_base, vf_base=vf_base):
                    for dh in range(KE):
                        for t2 in range(2):
                            sc.wait_ge(p_f.h, pf_base + 4 * dh + 2 * 0 + t2 + 1)
                            if dh >= 2:
                                sc.wait_ge(v_f.h, vf_base + 2 * (dh - 1))
                            a_f.inc(sc.activation(sgb[:, dh % 2, t2, :],
                                                  pp[4 * (dh % 2) + t2][:, :], AF.Silu))

                @blk.vector
                def _(v, has_cb=has_cb, pf_base=pf_base, af_base=af_base, pd_prev=pd_prev):
                    for dh in range(KE):
                        for t2 in range(2):
                            v.wait_ge(a_f.h, af_base + 2 * dh + t2 + 1)
                            v.wait_ge(p_f.h, pf_base + 4 * dh + 2 + t2 + 1)
                            if dh == 0 and t2 == 0:
                                v.wait_ge(p_d.h, pd_prev)
                            upsum = pp[4 * (dh % 2) + 2 + t2]
                            if has_cb:
                                g_dve.inc(v.tensor_mul(tb[:, t2, :], upsum[:, :], sgb[:, dh % 2, t2, :]))
                                v.wait_ge(g_dve.h, g_dve.n)
                                v_f.inc(v.tensor_mul(he[:, dh, t2 * 512:(t2 + 1) * 512],
                                                     tb[:, t2, :], cbe[:, t2 * 512:(t2 + 1) * 512]))
                            else:
                                v_f.inc(v.tensor_mul(he[:, dh, t2 * 512:(t2 + 1) * 512],
                                                     upsum[:, :], sgb[:, dh % 2, t2, :]))

                gd_list = []
                pd_base = p_d.n
                vd_base = v_d.n
                vf_end = v_f.n

                @blk.gpsimd
                def _(g, kind=kind, idx=idx, pd_base=pd_base):
                    for q in range(4):
                        if q >= 2:
                            g.wait_ge(p_d.h, pd_base + (q - 1) * KE)
                        g_d[q % 2].inc(g.dma_start(out=wd[:, q % 2, :, :], in_=wd_ap(kind, idx, q)), 16)
                        gd_list.append(g_d[q % 2].n)

                @blk.tensor
                def _(t, ei=ei, vf_end=vf_end, vd_base=vd_base):
                    t.wait_ge(v_f.h, vf_end)
                    for q in range(4):
                        if q >= 1:
                            t.wait_ge(v_d.h, vd_base + q)
                        elif ei >= 1:
                            t.wait_ge(v_d.h, vd_base)
                        t.wait_ge(g_d[q % 2].h, gd_list[q])
                        for k in range(KE):
                            for d4 in range(4):
                                for t2 in range(2):
                                    mm = t.matmul(pp[2 * d4 + t2][:, :],
                                                  wd[:, q % 2, k, d4 * 128:(d4 + 1) * 128],
                                                  he[:, k, t2 * 512:(t2 + 1) * 512],
                                                  start=(k == 0), stop=(k == KE - 1))
                                    if d4 == 3 and t2 == 1:
                                        p_d.inc(mm)

                @blk.vector
                def _(v, ei=ei, H=H, pd_base=pd_base):
                    for q in range(4):
                        v.wait_ge(p_d.h, pd_base + (q + 1) * KE)
                        if H == 1 and ei == 0 and q == 0:
                            v.wait_ge(s_o.h, self._ffn_h0_store)
                        for d4 in range(4):
                            for t2 in range(2):
                                dd = q * 4 + d4
                                dst = acc[:, dd, t2 * 512:(t2 + 1) * 512]
                                last = d4 == 3 and t2 == 1
                                if ei == 0:
                                    ins = v.tensor_copy(dst, pp[2 * d4 + t2][:, :])
                                else:
                                    ins = v.tensor_add(dst, dst, pp[2 * d4 + t2][:, :])
                                if last:
                                    v_d.inc(ins)

            @blk.sync
            def _(s, off=off):
                s.wait_ge(v_d.h, v_d.n)
                s_o.inc(s.dma_start(
                    out=self.d_ffn.rearrange("(k p) t -> p k t", p=128)[:, :, off:off + 1024],
                    in_=acc[:, :, :]), 16)
            if H == 0:
                self._ffn_h0_store = s_o.n

        @blk.sync
        def _(s):
            s.wait_ge(s_o.h, s_o.n)

    def p4_hin(self, blk, ph):
        nc, c = self.nc, self.ctx
        self._p4_ctx = ExitStack()
        self.ccol = self._p4_ctx.enter_context(nc.sbuf_tensor("ccol", [128, 8, NPAIR], F32))
        self.sel = self._p4_ctx.enter_context(nc.sbuf_tensor("sel", [NH, NH, 128], BF16))
        self.h_in = self._p4_ctx.enter_context(nc.sbuf_tensor("h_in", [128, NPAIR * DST], F32))
        lp = ph.enter_context(nc.sbuf_tensor("lp", [64, 8], F32))
        lpT = ph.enter_context(nc.sbuf_tensor("lpT", [8, 64], F32))
        coefT = ph.enter_context(nc.sbuf_tensor("coefT", [64, 8], F32))
        vb2 = ph.enter_context(nc.sbuf_tensor("vb2", [64, 64], F32))
        hjb = ph.enter_context(nc.sbuf_tensor("hjb", [128, 2, NPAIR * DST], F32))
        pl8 = ph.enter_context(nc.psum_tensor("h_pl", [8, 64], F32))
        pq2 = ph.enter_context(nc.psum_tensor("h_pq", [128, 32], F32))
        p64 = ph.enter_context(nc.psum_tensor("h_p64", [64, 8], F32))
        s_l = self.xS
        v_h = self.xV
        p_h = self.xT
        a_h = self.xA
        s_j = [Ctr(nc, c, "h_sj0"), Ctr(nc, c, "h_sj1")]
        v_a = self.xV

        @blk.sync
        def _(s):
            s.wait_ge(self.cc_sem.h, 1)
            for j in range(8):
                s_l.inc(s.dma_start(out=lp[:, j:j + 1],
                                    in_=self.cc_out[j * 128:j * 128 + 64, NPAIR * DST:NPAIR * DST + 1],
                                    allow_slow_non_contiguous=True), 16)

        @blk.vector
        def _(v):
            v.wait_ge(s_l.h, s_l.n)
            v_h.inc(v.memset(self.h_in[:, :], 0.0))

        @blk.tensor
        def _(t):
            t.wait_ge(s_l.h, s_l.n)
            p_h.inc(t.transpose(pl8[0:8, :], lp[:, 0:8], self.identf[0:64, 0:64]))

        @blk.vector
        def _(v):
            v.wait_ge(p_h.h, p_h.n)
            v_h.inc(v.tensor_copy(lpT[:, :], pl8[0:8, :]))

        @blk.tensor
        def _(t):
            t.wait_ge(v_h.h, v_h.n)
            p_h.inc(t.matmul(pl8[0:8, :], self.mselT_sb[:, :], lpT[:, :], start=True, stop=True))

        @blk.scalar
        def _(sc):
            sc.wait_ge(p_h.h, p_h.n)
            a_h.inc(sc.activation(lpT[:, :], pl8[0:8, :], AF.Exp, bias=self.cbias_col[:, 0:1]))

        @blk.tensor
        def _(t):
            t.wait_ge(a_h.h, a_h.n)
            p_h.inc(t.transpose(p64[0:64, 0:8], lpT[0:8, 0:64], self.identf[0:8, 0:8]))

        @blk.vector
        def _(v):
            v.wait_ge(p_h.h, p_h.n)
            v_h.inc(v.tensor_copy(coefT[:, :], p64[0:64, 0:8]))

        for j in range(8):
            @blk.vector
            def _(v, j=j):
                v.wait_ge(v_h.h, v_h.n)
                v.wait_ge(p_h.h, p_h.n)
                v_h.inc(v.tensor_scalar(vb2[:, :], self.ones_f[:, 0:64], coefT[:, j:j + 1], None, ALU.mult))

            @blk.tensor
            def _(t):
                t.wait_ge(v_h.h, v_h.n)
                t.matmul(pq2[0:64, :], vb2[:, :], self.Eev[:, :], start=True, stop=True)
                p_h.inc(t.matmul(pq2[64:128, :], vb2[:, :], self.Eod[:, :], start=True, stop=True))

            @blk.vector
            def _(v, j=j):
                v.wait_ge(p_h.h, p_h.n)
                v_h.inc(v.tensor_copy(self.ccol[:, j, :], pq2[:, :]))

        va_list = []
        for j in range(8):
            @blk.sync
            def _(s, j=j):
                if j >= 2:
                    s.wait_ge(v_a.h, va_list[j - 2])
                s_j[j % 2].inc(s.dma_start(out=hjb[:, j % 2, :], in_=self.cc_out[j * 128:(j + 1) * 128, 0:NPAIR * DST]), 16)

            @blk.vector
            def _(v, j=j):
                v.wait_ge(s_j[j % 2].h, 16 * (j // 2 + 1))
                if j == 0:
                    v.wait_ge(v_h.h, v_h.n)
                else:
                    v.wait_ge(v_a.h, va_list[j - 1])
                for m in range(NPAIR):
                    ins = v.scalar_tensor_tensor(self.h_in[:, m * DST:(m + 1) * DST],
                                                 hjb[:, j % 2, m * DST:(m + 1) * DST],
                                                 self.ccol[:, j, m:m + 1],
                                                 self.h_in[:, m * DST:(m + 1) * DST], ALU.mult, ALU.add)
                v_a.inc(ins)
                va_list.append(v_a.n)

    def p4_mid(self, blk, ph):
        nc, c = self.nc, self.ctx
        NGR = NCH * 4
        NR = NCH * NPAIR
        zg = ph.enter_context(nc.sbuf_tensor("zg", [128, 2, 8, CH], BF16))
        dxg = ph.enter_context(nc.sbuf_tensor("dxg", [128, 2, 8, CH], BF16))
        xtg = ph.enter_context(nc.sbuf_tensor("xtg", [128, 2, 2, 1024], BF16))
        plg = ph.enter_context(nc.sbuf_tensor("plg", [128, 2, 1024], BF16))
        ptbg = ph.enter_context(nc.sbuf_tensor("ptbg", [128, 2, 1024], BF16))
        pTg = ph.enter_context(nc.sbuf_tensor("pTg", [128, 2, 1024], BF16))
        srt = ph.enter_context(nc.sbuf_tensor("srt", [128, 2, CH], BF16))
        segb = ph.enter_context(nc.sbuf_tensor("segb", [128, 4, CH], F32))
        lmb = ph.enter_context(nc.sbuf_tensor("lmb", [128, 2, 4, CH], BF16))
        stdb = ph.enter_context(nc.sbuf_tensor("stdb", [128, 2, 4, CH], BF16))
        ecsb = ph.enter_context(nc.sbuf_tensor("ecsb", [128, 2, CH], BF16))
        v1b = ph.enter_context(nc.sbuf_tensor("v1b", [128, 2, CH], BF16))
        t1b = ph.enter_context(nc.sbuf_tensor("t1b", [128, 2, CH], BF16))
        v2b = ph.enter_context(nc.sbuf_tensor("v2b", [128, 2, CH], BF16))
        ygb2 = ph.enter_context(nc.sbuf_tensor("ygb2", [128, 2, CH], BF16))
        sqb = ph.enter_context(nc.sbuf_tensor("sqb", [128, 2, CH], BF16))
        ygsb = ph.enter_context(nc.sbuf_tensor("ygsb", [128, 2, 8, CH], BF16))
        rr = ph.enter_context(nc.sbuf_tensor("rr", [1, 3, CH], F32))
        M0 = ph.enter_context(nc.psum_tensor("m_M0", [128, 1024], BF16))
        M1 = ph.enter_context(nc.psum_tensor("m_M1", [128, 512], F32))
        M2 = ph.enter_context(nc.psum_tensor("m_M2", [128, 512], F32))
        M3 = [ph.enter_context(nc.psum_tensor(f"m_M3{i}", [128, CH], F32)) for i in range(2)]
        M4 = ph.enter_context(nc.psum_tensor("m_M4", [128, 512], F32))

        s_c = Ctr(nc, c, "m_sc")
        s_y = Ctr(nc, c, "m_sy")
        v_m = Ctr(nc, c, "m_vm")
        a_m = Ctr(nc, c, "m_am")
        p_m = Ctr(nc, c, "m_pm")

        ld_done = [0] * (NGR + 2)
        ptb_done = [0] * (NGR + 1)
        pte_done = [0] * (NGR + 1)
        tr_done = [0] * (NGR + 1)
        st_done = [0] * (NGR + 1)
        bc_done = [0] * (NR + 1)
        yd_done = [0] * (NR + 1)
        yo_done = [0] * (NR + 1)
        ec_done = [0] * (NR + 1)
        rm_done = [0] * (NR + 1)
        sg_done = [0] * (NR + 1)
        ex_done = [0] * (NR + 1)
        sd_done = [0] * (NR + 1)
        t1_done = [0] * (NR + 1)
        v1_done = [0] * (NR + 1)
        ecs_done = [0] * (NR + 1)
        sq_done = [0] * (NR + 1)
        yn_done = [0] * (NR + 1)
        sc_done = [0] * (NCH + 1)
        sr_done = [0] * (NCH + 1)
        rr0_done = [0] * (NCH + 1)
        rsq_done = [0] * (NCH + 1)
        rcp_done = [0] * (NCH + 1)
        rs_done = [0] * (NCH + 1)

        def em_loads(s, gi):
            ci2, g2 = gi // 4, gi % 4
            c0 = ci2 * CH
            sl = gi % 2
            s_c.inc(s.dma_start(
                out=zg[:, sl, :, :],
                in_=self.d_z.rearrange("(f p) t -> p f t", p=128)[:, 8 * g2:8 * g2 + 8, c0:c0 + CH]), 16)
            s_c.inc(s.dma_start(
                out=dxg[:, sl, :, :],
                in_=self.d_dx.rearrange("(f p) t -> p f t", p=128)[:, 8 * g2:8 * g2 + 8, c0:c0 + CH]), 16)
            for s2 in range(2):
                s_c.inc(s.dma_start(
                    out=xtg[:, sl, s2, :],
                    in_=self.d_xtm[c0 + s2 * 128: c0 + (s2 + 1) * 128, 1024 * g2:1024 * (g2 + 1)]), 16)
            s_c.inc(s.dma_start(out=plg[:, sl, :], in_=self.d_prev[ci2, :, 1024 * g2:1024 * (g2 + 1)]), 16)
            ld_done[gi] = s_c.n

        @blk.vector
        def _(v):
            for h in range(NH):
                v.tensor_scalar(self.sel[:, h, :], self.ones_b[0:NH, :],
                                self.identf[0:NH, h:h + 1], None, ALU.mult)
            v_m.inc(v.engine_nop())
        selb_done = v_m.n

        @blk.sync
        def _(s):
            s.wait_ge(self.s_xt_total[0].h, self.s_xt_total[1])
            em_loads(s, 0)

        @blk.tensor
        def _(t):
            for s2 in range(2):
                mm = t.matmul(M1[:, s2 * CH:(s2 + 1) * CH],
                              self.B_all[:, s2 * 128:(s2 + 1) * 128],
                              self.C_all[:, 0:CH], start=(s2 == 0), stop=(s2 == 1))
            p_m.inc(mm)
        sc_done[0] = p_m.n

        @blk.vector
        def _(v):
            v.wait_ge(p_m.h, sc_done[0])
            v.tensor_copy(srt[:, 0, :], M1[:, 0:CH])
            v_m.inc(v.tensor_copy(srt[:, 1, :], M1[:, CH:2 * CH]))
        sr_done[0] = v_m.n

        for r in range(NR + 1):
            bnd = (r % 8 == 0) and r < NR
            gi = r // 8
            ci = r // 32
            j = r % 32
            p = r - 1
            if p >= 0:
                gp_i, cp_i, jp, jjp = p // 8, p // 32, p % 32, p % 8

            if bnd:
                @blk.vector
                def _(v, gi=gi, ci=ci):
                    v.wait_ge(s_c.h, ld_done[gi])
                    if gi == 0:
                        v.wait_ge(self.hin_sem.h, self.hin_total)
                    if gi >= 2:
                        v.wait_ge(p_m.h, tr_done[gi - 2])
                    for jj in range(8):
                        ja = (gi % 4) * 8 + jj
                        v_m.inc(v.scalar_tensor_tensor(
                            ptbg[:, gi % 2, jj * 128:(jj + 1) * 128],
                            self.h_in[:, ja * DST:(ja + 1) * DST],
                            self.Dcol[:, ci, ja:ja + 1],
                            plg[:, gi % 2, jj * 128:(jj + 1) * 128], ALU.mult, ALU.add))
                    ptb_done[gi] = v_m.n

            @blk.tensor
            def _(t, r=r, bnd=bnd, gi=gi, ci=ci, j=j, p=p):
                if bnd:
                    t.wait_ge(v_m.h, ptb_done[gi])
                    if gi >= 1:
                        t.wait_ge(v_m.h, pte_done[gi - 1])
                    for jj in range(8):
                        p_m.inc(t.transpose(M0[:, jj * 128:(jj + 1) * 128],
                                            ptbg[:, gi % 2, jj * 128:(jj + 1) * 128],
                                            self.identb[:, :]))
                    tr_done[gi] = p_m.n
                if r < NR:
                    if r >= 1:
                        t.wait_ge(v_m.h, sg_done[r - 1])
                    for dl in range(2):
                        h = 2 * j + dl
                        t.matmul(M2[:, dl * CH:(dl + 1) * CH], self.sel[0:NH, h, :],
                                 self.cs_hi[:, ci * CH:(ci + 1) * CH], start=True, stop=False)
                        mm = t.matmul(M2[:, dl * CH:(dl + 1) * CH], self.sel[0:NH, h, :],
                                      self.cs_lo[:, ci * CH:(ci + 1) * CH], start=False, stop=True)
                        p_m.inc(mm)
                    bc_done[r] = p_m.n
                if r >= 1:
                    gp_i2, cp_i2, jp2, jjp2 = p // 8, p // 32, p % 32, p % 8
                    t.wait_ge(v_m.h, sd_done[p])
                    t.wait_ge(s_c.h, ld_done[gp_i2])
                    wv = 0
                    if p >= 2:
                        wv = max(wv, t1_done[p - 2])
                    if p >= 1:
                        wv = max(wv, v1_done[p - 1], ecs_done[p - 1])
                    wv = max(wv, pte_done[gp_i2])
                    if wv:
                        t.wait_ge(v_m.h, wv)
                    for q in range(4):
                        dl, s2 = q // 2, q % 2
                        hl = 2 * jjp2 + dl
                        mm = t.matmul(M3[p % 2][64 * dl:64 * dl + 64, :],
                                      xtg[:, gp_i2 % 2, s2, 64 * hl:64 * hl + 64],
                                      stdb[:, p % 2, q, :],
                                      start=(s2 == 0), stop=(s2 == 1), tile_position=(0, 64 * dl))
                        if q == 3:
                            p_m.inc(mm)
                    yd_done[p] = p_m.n
                    for dl in range(2):
                        mm = t.matmul(M4[64 * dl:64 * dl + 64, 0:CH],
                                      pTg[:, gp_i2 % 2, jjp2 * DST + 64 * dl: jjp2 * DST + 64 * dl + 64],
                                      self.C_all[:, cp_i2 * CH:(cp_i2 + 1) * CH],
                                      start=True, stop=True, tile_position=(0, 64 * dl))
                        if dl == 1:
                            p_m.inc(mm)
                    yo_done[p] = p_m.n
                    for dl in range(2):
                        h = 2 * jp2 + dl
                        mm = t.matmul(M4[64 * dl:64 * dl + 64, CH:2 * CH],
                                      self.sel[0:NH, h, 0:64],
                                      self.ecs16[:, cp_i2 * CH:(cp_i2 + 1) * CH],
                                      start=True, stop=True, tile_position=(0, 64 * dl))
                        if dl == 1:
                            p_m.inc(mm)
                    ec_done[p] = p_m.n
                qq = r - 2
                if qq >= 0 and qq % 32 != 31:
                    t.wait_ge(v_m.h, sq_done[qq])
                    if qq % 32 == 0:
                        t.wait_ge(v_m.h, sr_done[qq // 32])
                    mm = t.matmul(M1[:, 0:CH], self.ones_b[:, :], sqb[:, qq % 2, :],
                                  start=(qq % 32 == 0), stop=False)
                    p_m.inc(mm)
                    rm_done[qq] = p_m.n

            @blk.vector
            def _(v, r=r, bnd=bnd, gi=gi, ci=ci, j=j, p=p):
                if bnd:
                    v.wait_ge(p_m.h, tr_done[gi])
                    if gi >= 2:
                        v.wait_ge(p_m.h, yo_done[(gi - 1) * 8 - 1])
                    v_m.inc(v.tensor_copy(pTg[:, gi % 2, :], M0[:, :]))
                    pte_done[gi] = v_m.n
                if r < NR:
                    v.wait_ge(p_m.h, bc_done[r])
                    if r >= 1:
                        v.wait_ge(a_m.h, ex_done[r - 1])
                    for q in range(4):
                        dl, s2 = q // 2, q % 2
                        h = 2 * j + dl
                        v_m.inc(v.scalar_tensor_tensor(
                            segb[:, q, :], M2[:, dl * CH:(dl + 1) * CH],
                            self.csldt[:, ci, s2, h:h + 1],
                            self.maskT[:, s2, :], ALU.subtract, ALU.add))
                    sg_done[r] = v_m.n
                if r >= 1:
                    v.wait_ge(p_m.h, ec_done[p])
                    v_m.inc(v.tensor_copy(ecsb[:, p % 2, :], M4[:, CH:2 * CH]))
                    ecs_done[p] = v_m.n
                    v.wait_ge(v_m.h, v_m.n)
                    v_m.inc(v.tensor_mul(v1b[:, p % 2, :], M4[:, 0:CH], ecsb[:, p % 2, :]))
                    v1_done[p] = v_m.n
                    v.wait_ge(v_m.h, v_m.n)
                    v_m.inc(v.tensor_add(t1b[:, p % 2, :], v1b[:, p % 2, :], M3[p % 2][:, :]))
                    t1_done[p] = v_m.n

            if r < NR:
                @blk.scalar
                def _(sc, r=r):
                    sc.wait_ge(v_m.h, sg_done[r])
                    if r >= 2:
                        sc.wait_ge(v_m.h, sd_done[r - 2])
                    for q in range(4):
                        a_m.inc(sc.activation(lmb[:, r % 2, q, :], segb[:, q, :], AF.Exp))
                    ex_done[r] = a_m.n

            @blk.vector
            def _(v, r=r, ci=ci, p=p):
                if r >= 1:
                    gp_i2, jp2, jjp2 = p // 8, p % 32, p % 8
                    v.wait_ge(s_c.h, ld_done[gp_i2])
                    v_m.inc(v.tensor_add(v2b[:, p % 2, :], t1b[:, p % 2, :], dxg[:, gp_i2 % 2, jjp2, :]))
                    v.wait_ge(v_m.h, v_m.n)
                    v_m.inc(v.tensor_mul(ygb2[:, p % 2, :], v2b[:, p % 2, :], zg[:, gp_i2 % 2, jjp2, :]))
                    v.wait_ge(v_m.h, v_m.n)
                    v_m.inc(v.tensor_mul(sqb[:, p % 2, :], ygb2[:, p % 2, :], ygb2[:, p % 2, :]))
                    sq_done[p] = v_m.n
                    if gp_i2 >= 2:
                        v.wait_ge(s_y.h, st_done[gp_i2 - 2])
                    v_m.inc(v.tensor_scalar(ygsb[:, gp_i2 % 2, jjp2, :], ygb2[:, p % 2, :],
                                            self.normw[:, jp2:jp2 + 1], None, ALU.mult))
                    yn_done[p] = v_m.n
                if r < NR and not (r % 32 == 0 and r >= 32):
                    v.wait_ge(a_m.h, ex_done[r])
                    if r >= 2:
                        v.wait_ge(p_m.h, yd_done[r - 2])
                    for q in range(4):
                        v_m.inc(v.tensor_mul(stdb[:, r % 2, q, :], srt[:, q % 2, :],
                                             lmb[:, r % 2, q, :]))
                    sd_done[r] = v_m.n

            if (r % 32 == 0 and r >= 32) or r == NR:
                cprev = r // 32 - 1
                q31 = r - 1

                @blk.tensor
                def _(t, q31=q31):
                    t.wait_ge(v_m.h, sq_done[q31])
                    for _ in range(12):
                        t.matmul(self._pdum[:, :], self.dummyw[:, 0:128],
                                 self.dummyw[:, :], start=True, stop=True)
                    mm = t.matmul(M1[:, 0:CH], self.ones_b[:, :], sqb[:, q31 % 2, :],
                                  start=False, stop=True)
                    p_m.inc(mm)
                    rm_done[q31] = p_m.n

                @blk.vector
                def _(v, cprev=cprev, q31=q31):
                    v.wait_ge(p_m.h, rm_done[q31])
                    if cprev >= 1:
                        v.wait_ge(s_y.h, rs_done[cprev - 1])
                    v_m.inc(v.tensor_scalar(rr[0:1, 0, :], M1[0:1, 0:CH], 1.0 / DI, None, ALU.mult))
                    rr0_done[cprev] = v_m.n

                @blk.scalar
                def _(sc, cprev=cprev):
                    sc.wait_ge(v_m.h, rr0_done[cprev])
                    a_m.inc(sc.activation(rr[0:1, 1, :], rr[0:1, 0, :], AF.Sqrt,
                                          bias=self.eps_col[0:1, 0:1]))
                    rsq_done[cprev] = a_m.n

                @blk.vector
                def _(v, cprev=cprev):
                    v.wait_ge(a_m.h, rsq_done[cprev])
                    v_m.inc(v.reciprocal(rr[0:1, 2, :], rr[0:1, 1, :]))
                    rcp_done[cprev] = v_m.n

                if r < NR:
                    ci2 = r // 32

                    @blk.tensor
                    def _(t, ci2=ci2, cprev=cprev):
                        t.wait_ge(v_m.h, rr0_done[cprev])
                        for s2 in range(2):
                            mm = t.matmul(M1[:, s2 * CH:(s2 + 1) * CH],
                                          self.B_all[:, ci2 * CH + s2 * 128: ci2 * CH + (s2 + 1) * 128],
                                          self.C_all[:, ci2 * CH:(ci2 + 1) * CH],
                                          start=(s2 == 0), stop=(s2 == 1))
                        p_m.inc(mm)
                        sc_done[ci2] = p_m.n

                    @blk.vector
                    def _(v, ci2=ci2, r=r):
                        v.wait_ge(p_m.h, sc_done[ci2])
                        v.tensor_copy(srt[:, 0, :], M1[:, 0:CH])
                        v_m.inc(v.tensor_copy(srt[:, 1, :], M1[:, CH:2 * CH]))
                        sr_done[ci2] = v_m.n

                    @blk.vector
                    def _(v, r=r, ci2=ci2):
                        v.wait_ge(a_m.h, ex_done[r])
                        if r >= 2:
                            v.wait_ge(p_m.h, yd_done[r - 2])
                        for q in range(4):
                            v_m.inc(v.tensor_mul(stdb[:, r % 2, q, :], srt[:, q % 2, :],
                                                 lmb[:, r % 2, q, :]))
                        sd_done[r] = v_m.n

            if bnd:
                @blk.sync
                def _(s, gi=gi):
                    if gi >= 1:
                        s.wait_ge(v_m.h, yn_done[gi * 8 - 1])
                        cis, gs = (gi - 1) // 4, (gi - 1) % 4
                        s_y.inc(s.dma_start(
                            out=self.d_ynorm.rearrange("(f p) t -> p f t", p=128)[
                                :, 8 * gs:8 * gs + 8, cis * CH:(cis + 1) * CH],
                            in_=ygsb[:, (gi - 1) % 2, :, :]), 16)
                        st_done[gi - 1] = s_y.n
                    if gi + 1 < NGR:
                        if gi >= 1:
                            s.wait_ge(p_m.h, yd_done[gi * 8 - 1])
                            s.wait_ge(v_m.h, ptb_done[gi - 1])
                        em_loads(s, gi + 1)
            if (r % 32 == 0 and r >= 32) or r == NR:
                cprev = r // 32 - 1

                @blk.sync
                def _(s, cprev=cprev):
                    s.wait_ge(v_m.h, rcp_done[cprev])
                    s_y.inc(s.dma_start(out=self.d_rstd[0:1, cprev * CH:(cprev + 1) * CH],
                                        in_=rr[0:1, 2, :]), 16)
                    rs_done[cprev] = s_y.n

        @blk.sync
        def _(s):
            s.wait_ge(v_m.h, yn_done[NR - 1])
            s_y.inc(s.dma_start(
                out=self.d_ynorm.rearrange("(f p) t -> p f t", p=128)[:, 24:32, 7 * CH:8 * CH],
                in_=ygsb[:, (NGR - 1) % 2, :, :]), 16)
            s.wait_ge(s_y.h, s_y.n)

    def p5_final(self, blk, ph):
        nc, c = self.nc, self.ctx
        wT = ph.enter_context(nc.sbuf_tensor("owT", [128, KDI, 2048], BF16))
        g_w = [Ctr(nc, c, "o_gw0"), Ctr(nc, c, "o_gw1")]
        p_tr = Ctr(nc, c, "o_ptr")
        v_tr = Ctr(nc, c, "o_vtr")
        with ExitStack() as bctx:
            wpb = bctx.enter_context(nc.sbuf_tensor("wpb", [128, 2, DI], BF16))
            wtr = [bctx.enter_context(nc.psum_tensor(f"o_wt{i}", [128, 1024], BF16)) for i in range(2)]
            for m in range(KD):
                @blk.gpsimd
                def _(g, m=m):
                    if m >= 2:
                        g.wait_ge(p_tr.h, 32 * (m - 1))
                    g_w[m % 2].inc(g.dma_start(out=wpb[:, m % 2, :], in_=self.w_outproj[m * 128:(m + 1) * 128, :]), 16)

                @blk.tensor
                def _(t, m=m):
                    t.wait_ge(g_w[m % 2].h, 16 * (m // 2 + 1))
                    for half in range(4):
                        if 4 * m + half >= 2:
                            t.wait_ge(v_tr.h, 4 * m + half - 1)
                        for kk in range(8):
                            k = half * 8 + kk
                            p_tr.inc(t.transpose(wtr[half % 2][:, kk * 128:(kk + 1) * 128],
                                                 wpb[:, m % 2, k * 128:(k + 1) * 128],
                                                 self.identb[:, :]))

                @blk.vector
                def _(v, m=m):
                    for half in range(4):
                        v.wait_ge(p_tr.h, 32 * m + 8 * (half + 1))
                        src = wtr[half % 2].ap().rearrange("p (k x) -> p k x", k=8)
                        v_tr.inc(v.tensor_copy(wT[:, half * 8:(half + 1) * 8, m * 128:(m + 1) * 128], src))
        wT_total = v_tr.n

        yt = ph.enter_context(nc.sbuf_tensor("yt5", [128, 2, KDI, 128], BF16))
        ffb = ph.enter_context(nc.sbuf_tensor("ffb5", [128, 2, KD, 128], BF16))
        ffT = ph.enter_context(nc.sbuf_tensor("ffT5", [128, 2, 2048], BF16))
        ub = ph.enter_context(nc.sbuf_tensor("ub5", [128, 2, 2048], F32))
        rcol = ph.enter_context(nc.sbuf_tensor("rcol5", [128, 2, 1], F32))
        obuf = ph.enter_context(nc.sbuf_tensor("obuf5", [128, 2, 2048], F32))
        pop = [ph.enter_context(nc.psum_tensor(f"o_p{i}", [128, 512], F32)) for i in range(2)]
        pft = [ph.enter_context(nc.psum_tensor(f"o_f{i}", [128, 1024], BF16)) for i in range(2)]
        s_y = Ctr(nc, c, "o_sy")
        g_u = Ctr(nc, c, "o_gu")
        p_mm = Ctr(nc, c, "o_pmm")
        v_s1 = Ctr(nc, c, "o_vs1")
        p_tf = Ctr(nc, c, "o_ptf")
        v_ev = Ctr(nc, c, "o_vev")
        g_ad = Ctr(nc, c, "o_gad")
        s_out = Ctr(nc, c, "o_sout")
        NTB = 16
        for tb in range(NTB):
            sl = tb % 2

            @blk.sync
            def _(s, tb=tb, sl=sl):
                if tb == 0:
                    s.wait_ge(v_tr.h, wT_total)
                if tb >= 2:
                    s.wait_ge(p_mm.h, 4 * (tb - 1))
                    s.wait_ge(p_tf.h, 16 * (tb - 1))
                s_y.inc(s.dma_start(
                    out=yt[:, sl, :, :],
                    in_=self.d_ynorm.rearrange("(f p) t -> p f t", p=128)[:, :, tb * 128:(tb + 1) * 128]), 16)
                s_y.inc(s.dma_start(
                    out=ffb[:, sl, :, :],
                    in_=self.d_ffn.rearrange("(k p) t -> p k t", p=128)[:, :, tb * 128:(tb + 1) * 128]), 16)

            @blk.gpsimd
            def _(g, tb=tb, sl=sl):
                if tb == 0:
                    g.wait_ge(v_tr.h, wT_total)
                if tb >= 2:
                    g.wait_ge(v_s1.h, 4 * (tb - 1))
                g_u.inc(g.dma_start(out=ub[:, sl, :], in_=self.u[3 + tb * 128: 3 + (tb + 1) * 128, :]), 16)
                g_u.inc(g.dma_start(out=rcol[:, sl, :],
                                    in_=self.d_rstd.rearrange("o t -> t o")[tb * 128:(tb + 1) * 128, :],
                                    allow_slow_non_contiguous=True), 16)

            @blk.tensor
            def _(t, tb=tb, sl=sl):
                t.wait_ge(s_y.h, 32 * (tb + 1))
                if tb == 0:
                    t.wait_ge(v_tr.h, wT_total)
                if tb >= 1:
                    t.wait_ge(v_ev.h, 2 * tb)
                for mo in range(KD):
                    p_tf.inc(t.transpose(pft[mo // 8][:, (mo % 8) * 128:(mo % 8 + 1) * 128],
                                         ffb[:, sl, mo, :], self.identb[:, :]))
                for dq in range(4):
                    if 4 * tb + dq >= 2:
                        t.wait_ge(v_s1.h, 4 * tb + dq - 1)
                    for k in range(KDI):
                        mm = t.matmul(pop[dq % 2][:, :], yt[:, sl, k, :],
                                      wT[:, k, dq * 512:(dq + 1) * 512],
                                      start=(k == 0), stop=(k == KDI - 1))
                        if k == KDI - 1:
                            p_mm.inc(mm)

            @blk.vector
            def _(v, tb=tb, sl=sl):
                for dq in range(4):
                    v.wait_ge(p_mm.h, 4 * tb + dq + 1)
                    v.wait_ge(g_u.h, 32 * (tb + 1))
                    if tb >= 2:
                        v.wait_ge(s_out.h, 16 * (tb - 1))
                        v.wait_ge(g_ad.h, 4 * (tb - 1))
                    v_s1.inc(v.scalar_tensor_tensor(obuf[:, sl, dq * 512:(dq + 1) * 512],
                                                    pop[dq % 2][:, :], rcol[:, sl, 0:1],
                                                    ub[:, sl, dq * 512:(dq + 1) * 512],
                                                    ALU.mult, ALU.add))

            @blk.scalar
            def _(sc, tb=tb, sl=sl):
                sc.wait_ge(p_tf.h, 16 * (tb + 1))
                if tb >= 1:
                    sc.wait_ge(g_ad.h, 4 * tb)
                for bk in range(2):
                    v_ev.inc(sc.activation(ffT[:, sl, bk * 1024:(bk + 1) * 1024],
                                           pft[bk][:, :], AF.Copy))

            @blk.vector
            def _(v, tb=tb, sl=sl):
                v.wait_ge(v_ev.h, 2 * (tb + 1))
                for dq in range(4):
                    v.wait_ge(v_s1.h, 4 * tb + dq + 1)
                    g_ad.inc(v.tensor_add(obuf[:, sl, dq * 512:(dq + 1) * 512],
                                          obuf[:, sl, dq * 512:(dq + 1) * 512],
                                          ffT[:, sl, dq * 512:(dq + 1) * 512]))

            @blk.sync
            def _(s, tb=tb, sl=sl):
                s.wait_ge(g_ad.h, 4 * (tb + 1))
                s_out.inc(s.dma_start(out=self.out[tb * 128:(tb + 1) * 128, :], in_=obuf[:, sl, :]), 16)

        @blk.sync
        def _(s):
            s.wait_ge(s_out.h, s_out.n)

    def p_dbg(self, blk):
        nc = self.nc
        sd = Ctr(nc, self.ctx, "dbg_s")

        @blk.sync
        def _(s):
            pairs = [("xn", self.d_xn), ("z", self.d_z), ("xbc", self.d_xbc),
                     ("xtm", self.d_xtm), ("dx", self.d_dx),
                     ("ccout", self.cc_out), ("ynorm", self.d_ynorm), ("ffn", self.d_ffn)]
            for nm, src in pairs:
                sd.inc(s.dma_start(out=self.dbg[nm], in_=src), 16)
            sd.inc(s.dma_start(out=self.dbg["prevloc"],
                               in_=self.d_prev.rearrange("c p x -> (c p) x")
                               if False else self.d_prev[:, :, :]), 16)
            s.wait_ge(sd.h, sd.n)


def _build():
    return B().build()


def kernel(L0_attn_u, L0_attn_in_proj_w, L0_attn_conv_w, L0_attn_dt_bias,
           L0_attn_A_log, L0_attn_D_param, L0_attn_norm_w, L0_attn_out_proj_w,
           L0_ffn_W_router, L0_ffn_We_gate, L0_ffn_We_up, L0_ffn_We_down,
           L0_ffn_Ws_gate, L0_ffn_Ws_up, L0_ffn_Ws_down):
    u = np.ascontiguousarray(np.asarray(L0_attn_u, dtype=np.float32))
    nc = _build()
    in_maps = []
    for k in range(8):
        bi, s = k // 4, k % 4
        if s == 0:
            u_ext = np.concatenate([np.zeros((3, D), np.float32), u[bi, 0:T]], axis=0)
        else:
            u_ext = u[bi, s * T - 3:(s + 1) * T]
        mselT = np.zeros((8, 8), np.float32)
        cbias = np.full((8,), -1e30, np.float32)
        for j in range(8):
            if j // 4 == k // 4 and j < k:
                cbias[j] = 0.0
                for m in range(j + 1, k):
                    mselT[m, j] = 1.0
        in_maps.append({
            "u_ext": np.ascontiguousarray(u_ext),
            "in_proj_w": np.ascontiguousarray(np.asarray(L0_attn_in_proj_w, dtype=np.float32)),
            "conv_w": np.ascontiguousarray(np.asarray(L0_attn_conv_w, dtype=np.float32)),
            "dt_bias": np.ascontiguousarray(np.asarray(L0_attn_dt_bias, dtype=np.float32)),
            "A_log": np.ascontiguousarray(np.asarray(L0_attn_A_log, dtype=np.float32)),
            "D_param": np.ascontiguousarray(np.asarray(L0_attn_D_param, dtype=np.float32)),
            "norm_w": np.ascontiguousarray(np.asarray(L0_attn_norm_w, dtype=np.float32)),
            "out_proj_w": np.ascontiguousarray(np.asarray(L0_attn_out_proj_w, dtype=np.float32)),
            "W_router": np.ascontiguousarray(np.asarray(L0_ffn_W_router, dtype=np.float32)),
            "We_gate": np.ascontiguousarray(np.asarray(L0_ffn_We_gate, dtype=np.float32)),
            "We_up": np.ascontiguousarray(np.asarray(L0_ffn_We_up, dtype=np.float32)),
            "We_down": np.ascontiguousarray(np.asarray(L0_ffn_We_down, dtype=np.float32)),
            "Ws_gate": np.ascontiguousarray(np.asarray(L0_ffn_Ws_gate, dtype=np.float32)),
            "Ws_up": np.ascontiguousarray(np.asarray(L0_ffn_Ws_up, dtype=np.float32)),
            "Ws_down": np.ascontiguousarray(np.asarray(L0_ffn_Ws_down, dtype=np.float32)),
            "mselT": mselT, "cbias": cbias,
        })
    res = run_bass_kernel_spmd(nc, in_maps, core_ids=list(range(8)))
    out = np.empty((2, 8192, D), np.float32)
    for k in range(8):
        bi, s = k // 4, k % 4
        out[bi, s * T:(s + 1) * T] = res.results[k]["out"]
    if _DBG:
        kernel._last = res
    return out


# version 12
# speedup vs baseline: 1.0005x; 1.0005x over previous
import numpy as np
from contextlib import ExitStack

import concourse.bass as bass
import concourse.mybir as mybir
from concourse.bass_utils import run_bass_kernel_spmd

F32 = mybir.dt.float32
BF16 = mybir.dt.bfloat16
I32 = mybir.dt.int32
AF = mybir.ActivationFunctionType
ALU = mybir.AluOpType
AX = mybir.AxisListType

D = 2048
DI = 4096
NH = 64
DST = 128
DXBC = DI + 2 * DST
DPROJ = 2 * DI + 2 * DST + NH
T = 2048
CH = 256
NCH = T // CH
TE = T + 3
EPS = 1e-5
KD = D // 128
KDI = DI // 128
NPAIR = NH // 2
NXBC = DXBC // 128
M_INP = 67
DE = 1408
DS = 2816
KE = DE // 128
KS = DS // 128

_DBG = False


class Ctr:
    def __init__(self, nc, ctx, name):
        self.h = ctx.enter_context(nc.semaphore(name))
        self.n = 0

    def inc(self, instr, k=1):
        instr.then_inc(self.h, k)
        self.n += k
        return instr


class B:
    def __init__(self):
        nc = bass.Bass("TRN2", target_bir_lowering=False, debug=False, detect_race_conditions=True)
        self.nc = nc
        self.ctx = ExitStack()

        def inp(name, shape):
            return nc.dram_tensor(name, shape, F32, kind="ExternalInput").ap()

        self.u = inp("u_ext", [TE, D])
        self.w_inproj = inp("in_proj_w", [DPROJ, D])
        self.w_conv = inp("conv_w", [DXBC, 4])
        self.i_dtb = inp("dt_bias", [NH])
        self.i_alog = inp("A_log", [NH])
        self.i_dpar = inp("D_param", [NH])
        self.i_normw = inp("norm_w", [DI])
        self.w_outproj = inp("out_proj_w", [D, DI])
        self.w_router = inp("W_router", [D, 8])
        self.we_gate = inp("We_gate", [8, D, DE])
        self.we_up = inp("We_up", [8, D, DE])
        self.we_down = inp("We_down", [8, DE, D])
        self.ws_gate = inp("Ws_gate", [D, DS])
        self.ws_up = inp("Ws_up", [D, DS])
        self.ws_down = inp("Ws_down", [DS, D])
        self.i_mselT = inp("mselT", [8, 8])
        self.i_cbias = inp("cbias", [8])

        self.out = nc.dram_tensor("out", [T, D], F32, kind="ExternalOutput").ap()

        def scr(name, shape, dt=BF16):
            return nc.dram_tensor(name, shape, dt, kind="Internal").ap()

        self.d_xn = scr("d_xn", [D, TE + 1])
        self.d_z = scr("d_z", [DI, T])
        self.d_xbc = scr("d_xbc", [DXBC, TE + 1])
        self.d_xtm = scr("d_xtm", [T, DI])
        self.d_dx = scr("d_dx", [DI, T])
        self.d_prev = scr("d_prev", [NCH, 128, NPAIR * DST])
        self.d_ynorm = scr("d_ynorm", [DI, T])
        self.d_rstd = scr("d_rstd", [1, T], F32)
        self.d_ffn = scr("d_ffn", [D, T])
        self.cc_in = scr("cc_in", [128, NPAIR * DST + 1], F32)
        self.cc_out = scr("cc_out", [8 * 128, NPAIR * DST + 1], F32)
        if _DBG:
            self.dbg = {}
            for nm, shp, dt in [("xn", [D, TE + 1], BF16), ("z", [DI, T], BF16),
                                ("xbc", [DXBC, TE + 1], BF16), ("dtraw", [NH, T], F32),
                                ("combine", [8, T], F32), ("xtm", [T, DI], BF16),
                                ("dx", [DI, T], BF16), ("prevloc", [NCH, 128, NPAIR * DST], BF16),
                                ("ccout", [8 * 128, NPAIR * DST + 1], F32),
                                ("csall", [NH, T], F32), ("dtall", [NH, T], F32),
                                ("ynorm", [DI, T], BF16), ("ffn", [D, T], BF16),
                                ("gtm", [128, NCH * 2 * NH], F32),
                                ("cdcol", [128, NCH * NPAIR], F32),
                                ("dcol", [128, NCH * NPAIR], F32)]:
                self.dbg[nm] = nc.dram_tensor("dbg_" + nm, shp, dt, kind="ExternalOutput").ap()

        c = self.ctx
        self._mamba_ctx = ExitStack()
        self._tmp_ctx = ExitStack()
        mc = self._mamba_ctx
        tc = self._tmp_ctx

        def sb(name, shape, dt=F32):
            return c.enter_context(nc.sbuf_tensor(name, shape, dt))

        def sbm(name, shape, dt=F32):
            return mc.enter_context(nc.sbuf_tensor(name, shape, dt))

        def sbt(name, shape, dt=F32):
            return tc.enter_context(nc.sbuf_tensor(name, shape, dt))

        self.identb = sb("identb", [128, 128], BF16)
        self.identf = sb("identf", [128, 128], F32)
        self.ones_b = sb("ones_b", [128, 128], BF16)
        self.ones_f = sb("ones_f", [64, 128], F32)
        self.sel8 = sb("sel8", [8, 8, 128], F32)
        self.Eev = sb("Eev", [64, 32], F32)
        self.Eod = sb("Eod", [64, 32], F32)
        self.dtb_col = sb("dtb_col", [NH, 1], F32)
        self.A_col = sb("A_col", [NH, 1], F32)
        self.mselT_sb = sb("mselT_sb", [8, 8], F32)
        self.cbias_col = sb("cbias_col", [8, 1], F32)
        self.eps_col = sb("eps_col", [128, 1], F32)
        self.dummyw = sb("dummyw", [128, 512], BF16)
        self.maskT = sbm("maskT", [128, 2, CH], F32)
        self.normw = sbm("normw", [128, KDI], F32)
        self.cs_hi = sbm("cs_hi", [NH, T], BF16)
        self.cs_lo = sbm("cs_lo", [NH, T], BF16)
        self.ecs16 = sbm("ecs16", [NH, T], BF16)
        self.csldt = sbm("csldt", [128, NCH, 2, NH], F32)
        self.B_all = sbm("B_all", [128, T], BF16)
        self.C_all = sbm("C_all", [128, T], BF16)
        self.cdcol = sbm("cdcol", [128, NCH, NPAIR], F32)
        self.Dcol = sbm("Dcol", [128, NCH, NPAIR], F32)
        self.logP = sbm("logP", [NH, 1], F32)

        self.dt_all = sbt("dt_all", [NH, T], F32)
        self.cs_all = sbt("cs_all", [NH, T], F32)
        self.combine = sbt("combine", [8, T], F32)
        self.utri = sbt("utri", [128, 2, CH], F32)
        self.dt_raw = sbt("dt_raw", [NH, T], F32)
        self._prev_ctx = ExitStack()
        self.prev = self._prev_ctx.enter_context(nc.sbuf_tensor("prev", [128, NPAIR * DST], F32))
        self.g_tm = self._prev_ctx.enter_context(nc.sbuf_tensor("g_tm", [128, NCH, 2, NH], F32))
        self.D_pair = self._prev_ctx.enter_context(nc.sbuf_tensor("D_pair", [128, NPAIR], F32))
        self.wcv = self._prev_ctx.enter_context(nc.sbuf_tensor("wcv", [128, NXBC, 4], F32))

        self.bar = Ctr(nc, c, "bar")
        self.xT = Ctr(nc, c, "xT")
        self.xV = Ctr(nc, c, "xV")
        self.xA = Ctr(nc, c, "xA")
        self.xS = Ctr(nc, c, "xS")
        self.nbar = 0

    def barrier(self, blk):
        self.nbar += 1
        target = 5 * self.nbar
        for name in ("tensor", "vector", "scalar", "sync", "gpsimd"):
            @getattr(blk, name)
            def _(eng, target=target):
                eng.drain().then_inc(self.bar.h, 1)
                eng.wait_ge(self.bar.h, target)

    def warm(self, blk, ph, name, n=16):
        nc = self.nc
        pd = ph.enter_context(nc.psum_tensor(name, [128, 512], F32))
        self._pdum = pd

        @blk.tensor
        def _(t):
            for _ in range(n):
                t.matmul(pd[:, :], self.dummyw[:, 0:128], self.dummyw[:, :],
                         start=True, stop=True)
        return pd

    def build(self):
        nc = self.nc
        with self.ctx, nc.Block() as blk:
            self.p_init(blk)
            self._init_ctx.close()
            self.barrier(blk)
            self.p0_ln(blk)
            self.barrier(blk)
            with ExitStack() as ph:
                self.p1_inproj(blk, ph)
            self.barrier(blk)
            with ExitStack() as ph:
                self.p3a_router(blk, ph)
            self.barrier(blk)
            self.free_xn()
            with ExitStack() as ph:
                self.warm(blk, ph, "wm2s")
                self.p2_stats(blk, ph)
            self.barrier(blk)
            with ExitStack() as ph:
                self.warm(blk, ph, "wm2m")
                self.p2_main(blk, ph)
            self.barrier(blk)
            self._prev_ctx.close()
            with ExitStack() as ph:
                self.p3b_ffn(blk, ph)
            self.barrier(blk)
            self._tmp_ctx.close()
            with ExitStack() as ph:
                self.warm(blk, ph, "wmhn")
                self.p4_hin(blk, ph)
            self.hin_sem = self.xV
            self.hin_total = self.xV.n
            self.barrier(blk)
            with ExitStack() as ph:
                self.warm(blk, ph, "wmp4")
                self.p4_mid(blk, ph)
            self.barrier(blk)
            self.free_mamba()
            with ExitStack() as ph:
                self.warm(blk, ph, "wmp5")
                self.p5_final(blk, ph)
            if _DBG:
                self.barrier(blk)
                self.p_dbg(blk)
        return nc

    def free_xn(self):
        self._xn_ctx.close()

    def free_mamba(self):
        self._p4_ctx.close()
        self._mamba_ctx.close()

    def p_init(self, blk):
        nc, c = self.nc, self.ctx
        self._init_ctx = ExitStack()
        ic = self._init_ctx
        it128 = ic.enter_context(nc.sbuf_tensor("it128", [128, 128], I32))
        it256 = ic.enter_context(nc.sbuf_tensor("it256", [128, 2, CH], I32))
        itE = ic.enter_context(nc.sbuf_tensor("itE", [64, 2, 32], I32))
        nst = ic.enter_context(nc.sbuf_tensor("normw_st", [KDI, 128], F32))
        rowD = ic.enter_context(nc.sbuf_tensor("rowD", [1, NH], F32))
        ps_a = ic.enter_context(nc.psum_tensor("ips_a", [128, 128], F32))
        ps_b = ic.enter_context(nc.psum_tensor("ips_b", [128, 128], F32))
        g0 = Ctr(nc, c, "i_g0")
        gi = Ctr(nc, c, "i_gi")
        v0 = self.xV
        p0 = self.xT
        a0 = self.xA

        @blk.gpsimd
        def _(g):
            gi.inc(g.iota(it128[:, :], pattern=[[1, 128]], base=0, channel_multiplier=-1))
            gi.inc(g.iota(it256[:, 0, :], pattern=[[1, CH]], base=0, channel_multiplier=-1))
            gi.inc(g.iota(it256[:, 1, :], pattern=[[1, CH]], base=-128, channel_multiplier=-1))
            gi.inc(g.iota(itE[:, 0, :], pattern=[[-2, 32]], base=0, channel_multiplier=1))
            gi.inc(g.iota(itE[:, 1, :], pattern=[[-2, 32]], base=-1, channel_multiplier=1))
            g0.inc(g.dma_start(out=self.dtb_col[:, :], in_=self.i_dtb[:, None]), 16)
            g0.inc(g.dma_start(out=self.A_col[:, :], in_=self.i_alog[:, None]), 16)
            g0.inc(g.dma_start(out=rowD[0:1, :], in_=self.i_dpar[None, :]), 16)
            g0.inc(g.dma_start(out=nst[:, :], in_=self.i_normw.rearrange("(k p) -> k p", p=128)), 16)
            g0.inc(g.dma_start(out=self.wcv[:, :, :], in_=self.w_conv.rearrange("(f p) k -> p f k", p=128)), 16)
            g0.inc(g.dma_start(out=self.mselT_sb[:, :], in_=self.i_mselT[:, :]), 16)
            g0.inc(g.dma_start(out=self.cbias_col[:, :], in_=self.i_cbias[:, None]), 16)

        @blk.vector
        def _(v):
            v.wait_ge(g0.h, g0.n)
            v.wait_ge(gi.h, gi.n)
            v.memset(self.eps_col[:, :], EPS)
            v.memset(self.dummyw[:, :], 0.0)
            v.tensor_single_scalar(self.identb[:, :], it128[:, :], 0.0, ALU.is_equal)
            v.tensor_single_scalar(self.identf[:, :], it128[:, :], 0.0, ALU.is_equal)
            v.memset(self.ones_b[:, :], 1.0)
            v.memset(self.ones_f[:, :], 1.0)
            v.tensor_single_scalar(self.Eev[:, :], itE[:, 0, :], 0.0, ALU.is_equal)
            v.tensor_single_scalar(self.Eod[:, :], itE[:, 1, :], 0.0, ALU.is_equal)
            for s in range(2):
                v.tensor_single_scalar(self.utri[:, s, :], it256[:, s, :], 0.0, ALU.is_ge)
                v0.inc(v.tensor_single_scalar(self.maskT[:, s, :], it256[:, s, :], 0.0, ALU.is_lt))
            v.wait_ge(v0.h, v0.n)
            v0.inc(v.tensor_scalar_mul(self.maskT[:, :, :], self.maskT[:, :, :], -1e30))
            v.wait_ge(v0.h, v0.n)
            for e in range(8):
                v.tensor_scalar(self.sel8[:, e, :], self.ones_f[0:8, :], self.identf[0:8, e:e + 1], None, ALU.mult)
            v0.inc(v.memset(self.prev[:, :], 0.0))
            v0.inc(v.memset(self.logP[:, :], 0.0))

        @blk.scalar
        def _(sc):
            sc.wait_ge(g0.h, g0.n)
            a0.inc(sc.activation(self.A_col[:, :], self.A_col[:, :], AF.Exp))
            sc.wait_ge(a0.h, a0.n)
            a0.inc(sc.mul(self.A_col[:, :], self.A_col[:, :], -1.0))

        @blk.tensor
        def _(t):
            t.wait_ge(v0.h, v0.n)
            t.wait_ge(g0.h, g0.n)
            p0.inc(t.transpose(ps_a[:, 0:KDI], nst[:, :], self.identf[0:KDI, 0:KDI]))
            p0.inc(t.matmul(ps_b[0:64, 0:NPAIR], self.ones_f[0:1, 0:64], rowD[0:1, 0::2], start=True, stop=True))
            p0.inc(t.matmul(ps_b[64:128, 0:NPAIR], self.ones_f[0:1, 0:64], rowD[0:1, 1::2], start=True, stop=True))

        @blk.vector
        def _(v):
            v.wait_ge(p0.h, p0.n)
            v.tensor_copy(self.normw[:, :], ps_a[:, 0:KDI])
            v.tensor_copy(self.D_pair[:, :], ps_b[:, 0:NPAIR])

    def p0_ln(self, blk):
        nc, c = self.nc, self.ctx
        self._xn_ctx = ExitStack()
        self.xn = self._xn_ctx.enter_context(nc.sbuf_tensor("xn", [128, KD, TE + 1], BF16))
        ph = ExitStack()
        u_tm = ph.enter_context(nc.sbuf_tensor("u_tm", [128, 2, D], F32))
        xn_tm = ph.enter_context(nc.sbuf_tensor("xn_tm", [128, 2, D], BF16))
        lncol = ph.enter_context(nc.sbuf_tensor("lncol", [128, 2, 4], F32))
        sq = ph.enter_context(nc.sbuf_tensor("sq", [128, D], F32))
        banks = [ph.enter_context(nc.psum_tensor(f"p0_t{i}", [128, 1024], BF16)) for i in range(2)]
        s_ld = [Ctr(nc, c, "p0_sld0"), Ctr(nc, c, "p0_sld1")]
        v_var = Ctr(nc, c, "p0_vvar")
        a_sq = self.xA
        v_xn = Ctr(nc, c, "p0_vxn")
        p_tr = Ctr(nc, c, "p0_ptr")
        v_ev = Ctr(nc, c, "p0_vev")
        s_st = self.xS
        NT = 17
        with ph:
            for i in range(NT):
                nr = 128 if i < 16 else 3
                b = i % 2

                @blk.sync
                def _(s, i=i, nr=nr, b=b):
                    if i >= 2:
                        s.wait_ge(v_xn.h, i - 1)
                    s_ld[b].inc(s.dma_start(out=u_tm[0:nr, b, :], in_=self.u[i * 128:i * 128 + nr, :]), 16)

                @blk.vector
                def _(v, i=i, nr=nr, b=b):
                    v.wait_ge(s_ld[b].h, 16 * (i // 2 + 1))
                    v_var.inc(v.tensor_mul(sq[0:nr, :], u_tm[0:nr, b, :], u_tm[0:nr, b, :]))
                    v.reduce_sum(lncol[0:nr, b, 0:1], u_tm[0:nr, b, :], axis=AX.X)
                    v.wait_ge(v_var.h, v_var.n)
                    v_var.inc(v.reduce_sum(lncol[0:nr, b, 1:2], sq[0:nr, :], axis=AX.X))
                    v.wait_ge(v_var.h, v_var.n)
                    v.tensor_scalar(lncol[0:nr, b, 0:1], lncol[0:nr, b, 0:1], 1.0 / D, None, ALU.mult)
                    v_var.inc(v.tensor_scalar(lncol[0:nr, b, 1:2], lncol[0:nr, b, 1:2], 1.0 / D, None, ALU.mult))
                    v.wait_ge(v_var.h, v_var.n)
                    v_var.inc(v.tensor_mul(lncol[0:nr, b, 2:3], lncol[0:nr, b, 0:1], lncol[0:nr, b, 0:1]))
                    v.wait_ge(v_var.h, v_var.n)
                    v_var.inc(v.tensor_sub(lncol[0:nr, b, 1:2], lncol[0:nr, b, 1:2], lncol[0:nr, b, 2:3]))

                @blk.scalar
                def _(sc, i=i, nr=nr, b=b):
                    sc.wait_ge(v_var.h, v_var.n)
                    a_sq.inc(sc.activation(lncol[0:nr, b, 2:3], lncol[0:nr, b, 1:2], AF.Sqrt, bias=self.eps_col[0:nr, 0:1]))

                @blk.vector
                def _(v, i=i, nr=nr, b=b):
                    v.wait_ge(a_sq.h, a_sq.n)
                    v_var.inc(v.reciprocal(lncol[0:nr, b, 3:4], lncol[0:nr, b, 2:3]))
                    v.wait_ge(v_var.h, v_var.n)
                    v_xn.inc(v.tensor_scalar(xn_tm[0:nr, b, :], u_tm[0:nr, b, :],
                                             lncol[0:nr, b, 0:1], lncol[0:nr, b, 3:4],
                                             ALU.subtract, ALU.mult))

                @blk.tensor
                def _(t, i=i, nr=nr, b=b):
                    t.wait_ge(v_xn.h, v_xn.n)
                    for half in range(2):
                        if i >= 1:
                            t.wait_ge(v_ev.h, 2 * (i - 1) + half + 1)
                        for kk in range(8):
                            k = half * 8 + kk
                            p_tr.inc(t.transpose(banks[half][:, kk * 128:kk * 128 + nr],
                                                 xn_tm[0:nr, b, k * 128:(k + 1) * 128],
                                                 self.identb[0:nr, 0:nr]))

                @blk.vector
                def _(v, i=i, nr=nr):
                    r0 = i * 128
                    for half in range(2):
                        v.wait_ge(p_tr.h, 16 * i + 8 * (half + 1))
                        src = banks[half].ap().rearrange("p (k x) -> p k x", k=8)[:, :, 0:nr]
                        v_ev.inc(v.tensor_copy(self.xn[:, half * 8:(half + 1) * 8, r0:r0 + nr], src))

            @blk.sync
            def _(s):
                s.wait_ge(v_ev.h, v_ev.n)
                for k in range(KD):
                    s_st.inc(s.dma_start(out=self.d_xn[k * 128:(k + 1) * 128, 0:TE], in_=self.xn[:, k, 0:TE]), 16)
                s.wait_ge(s_st.h, s_st.n)

    def p1_inproj(self, blk, ph):
        nc, c = self.nc, self.ctx
        wblk = ph.enter_context(nc.sbuf_tensor("wblk", [128, 2, D], BF16))
        wT = ph.enter_context(nc.sbuf_tensor("wT", [128, 2, KD, 128], BF16))
        obuf = ph.enter_context(nc.sbuf_tensor("obuf", [128, 2, 4, 512], BF16))
        hbuf = ph.enter_context(nc.sbuf_tensor("hbuf", [128, 8], BF16))
        trb = [ph.enter_context(nc.psum_tensor(f"p1_tr{i}", [128, 1024], BF16)) for i in range(2)]
        pm = [ph.enter_context(nc.psum_tensor(f"p1_m{t}", [128, 512], F32)) for t in range(4)]
        phh = ph.enter_context(nc.psum_tensor("p1_h", [128, 8], F32))
        g_w = [Ctr(nc, c, "p1_gw0"), Ctr(nc, c, "p1_gw1")]
        p_tr = self.xT
        v_wtr = Ctr(nc, c, "p1_vwtr")
        p_mm = self.xT
        v_oe = self.xV
        s_st = self.xS
        p_hm = self.xT
        v_he = self.xV
        s_he = self.xS
        tr_done = []
        mm_done = []
        oe_done = []
        st_done = []
        he_done = []

        for m in range(M_INP):
            nr = 64 if m == M_INP - 1 else 128
            b = m % 2
            xbc = 32 <= m < 66

            @blk.gpsimd
            def _(g, m=m, nr=nr, b=b):
                if m >= 2:
                    g.wait_ge(p_tr.h, tr_done[m - 2])
                g_w[b].inc(g.dma_start(out=wblk[0:nr, b, :], in_=self.w_inproj[m * 128:m * 128 + nr, :]), 16)

            @blk.tensor
            def _(t, m=m, nr=nr, b=b):
                t.wait_ge(g_w[b].h, 16 * (m // 2 + 1))
                for half in range(2):
                    if m >= 1:
                        t.wait_ge(v_wtr.h, 2 * (m - 1) + half + 1)
                    for kk in range(8):
                        k = half * 8 + kk
                        p_tr.inc(t.transpose(trb[half][:, kk * 128:kk * 128 + nr],
                                             wblk[0:nr, b, k * 128:(k + 1) * 128],
                                             self.identb[0:nr, 0:nr]))
                tr_done.append(p_tr.n)

            @blk.vector
            def _(v, m=m, nr=nr, b=b):
                for half in range(2):
                    v.wait_ge(p_tr.h, tr_done[m] - 8 * (1 - half))
                    if m >= 2:
                        v.wait_ge(p_mm.h, mm_done[m - 1])
                    src = trb[half].ap().rearrange("p (k x) -> p k x", k=8)[:, :, 0:nr]
                    v_wtr.inc(v.tensor_copy(wT[:, b, half * 8:(half + 1) * 8, 0:nr], src))

            @blk.tensor
            def _(t, m=m, nr=nr, b=b, xbc=xbc):
                t.wait_ge(v_wtr.h, 2 * (m + 1))
                if m >= 1:
                    t.wait_ge(v_oe.h, oe_done[m - 1])
                for tt in range(4):
                    for k in range(KD):
                        mm = t.matmul(pm[tt][0:nr, :], wT[:, b, k, 0:nr],
                                      self.xn[:, k, 3 + tt * 512: 3 + (tt + 1) * 512],
                                      start=(k == 0), stop=(k == KD - 1))
                        if k == KD - 1:
                            p_mm.inc(mm)
                if xbc:
                    if m >= 33:
                        t.wait_ge(v_he.h, he_done[-1][0])
                    for k in range(KD):
                        mm = t.matmul(phh[0:nr, :], wT[:, b, k, 0:nr], self.xn[:, k, 0:8],
                                      start=(k == 0), stop=(k == KD - 1))
                        if k == KD - 1:
                            p_hm.inc(mm)
                mm_done.append(p_mm.n)

            if m < 32:
                @blk.scalar
                def _(sc, m=m, nr=nr, b=b):
                    for tt in range(4):
                        sc.wait_ge(p_mm.h, mm_done[m] - (3 - tt))
                        if m >= 2:
                            sc.wait_ge(s_st.h, st_done[m - 1])
                        v_oe.inc(sc.activation(obuf[0:nr, b, tt, :], pm[tt][0:nr, :], AF.Silu))

            @blk.vector
            def _(v, m=m, nr=nr, b=b, xbc=xbc):
                for tt in range(4):
                    if m < 32:
                        break
                    v.wait_ge(p_mm.h, mm_done[m] - (3 - tt))
                    if m >= 2:
                        v.wait_ge(s_st.h, st_done[m - 1])
                    if m == M_INP - 1:
                        v_oe.inc(v.tensor_copy(self.dt_raw[:, tt * 512:(tt + 1) * 512], pm[tt][0:64, :]))
                    else:
                        v_oe.inc(v.tensor_copy(obuf[0:nr, b, tt, :], pm[tt][0:nr, :]))
                if xbc:
                    v.wait_ge(p_hm.h, p_hm.n)
                    if m >= 34:
                        v.wait_ge(s_he.h, s_he.n)
                    v_he.inc(v.tensor_copy(hbuf[0:nr, :], phh[0:nr, :]))
                oe_done.append(v_oe.n)

            @blk.sync
            def _(s, m=m, nr=nr, b=b, xbc=xbc):
                if m < 66:
                    for tt in range(4):
                        s.wait_ge(v_oe.h, oe_done[m] - (3 - tt))
                        if m < 32:
                            dst = self.d_z[m * 128:(m + 1) * 128, tt * 512:(tt + 1) * 512]
                        else:
                            r = (m - 32) * 128
                            dst = self.d_xbc[r:r + 128, 3 + tt * 512: 3 + (tt + 1) * 512]
                        s_st.inc(s.dma_start(out=dst, in_=obuf[0:nr, b, tt, :]), 16)
                if xbc:
                    s.wait_ge(v_he.h, v_he.n)
                    r = (m - 32) * 128
                    s_he.inc(s.dma_start(out=self.d_xbc[r:r + 128, 0:3], in_=hbuf[0:128, 0:3]), 16)
                    he_done.append((v_he.n, s_he.n))
                st_done.append(s_st.n)

        @blk.sync
        def _(s):
            s.wait_ge(s_st.h, s_st.n)
            s.wait_ge(s_he.h, s_he.n)

    def p3a_router(self, blk, ph):
        nc, c = self.nc, self.ctx
        wr = ph.enter_context(nc.sbuf_tensor("wr", [128, KD, 8], BF16))
        cmb = ph.enter_context(nc.sbuf_tensor("cmb", [128, 2, 8], F32))
        rcol = ph.enter_context(nc.sbuf_tensor("rcol", [128, 2, 4], F32))
        esb = ph.enter_context(nc.sbuf_tensor("esb", [128, 2, 8], F32))
        ps_r = ph.enter_context(nc.psum_tensor("p3a_r", [128, 8], F32))
        ps_c = ph.enter_context(nc.psum_tensor("p3a_c", [8, 128], F32))
        g_w = Ctr(nc, c, "r_gw")
        p_r = self.xT
        v_s = self.xV
        a_e = self.xA
        v_c = self.xV
        p_t = self.xT
        v_o = self.xV
        sub_done = []
        cmb_done = []
        evo_done = []

        @blk.gpsimd
        def _(g):
            g_w.inc(g.dma_start(out=wr[:, :, :], in_=self.w_router.rearrange("(k p) e -> p k e", p=128)), 16)

        for i in range(KD):
            b = i % 2

            @blk.tensor
            def _(t, i=i):
                if i == 0:
                    t.wait_ge(g_w.h, 16)
                else:
                    t.wait_ge(v_s.h, sub_done[i - 1])
                for k in range(KD):
                    mm = t.matmul(ps_r[:, :], self.xn[:, k, 3 + i * 128: 3 + (i + 1) * 128],
                                  wr[:, k, :], start=(k == 0), stop=(k == KD - 1))
                    if k == KD - 1:
                        p_r.inc(mm)

            @blk.vector
            def _(v, i=i, b=b):
                v.wait_ge(p_r.h, p_r.n)
                v_s.inc(v.reduce_max(rcol[:, b, 0:1], ps_r[:, :], axis=AX.X))
                v.wait_ge(v_s.h, v_s.n)
                v_s.inc(v.tensor_scalar(esb[:, b, :], ps_r[:, :], rcol[:, b, 0:1], None, ALU.subtract))
                sub_done.append(v_s.n)

            @blk.scalar
            def _(sc, i=i, b=b):
                sc.wait_ge(v_s.h, sub_done[i])
                a_e.inc(sc.activation(esb[:, b, :], esb[:, b, :], AF.Exp))

            @blk.vector
            def _(v, i=i, b=b):
                v.wait_ge(a_e.h, a_e.n)
                v_s.inc(v.reduce_max(rcol[:, b, 0:1], esb[:, b, :], axis=AX.X))
                v_s.inc(v.reduce_sum(rcol[:, b, 1:2], esb[:, b, :], axis=AX.X))
                v.wait_ge(v_s.h, v_s.n)
                v_s.inc(v.reciprocal(rcol[:, b, 1:2], rcol[:, b, 1:2]))
                v.wait_ge(v_s.h, v_s.n)
                if i >= 2:
                    v.wait_ge(p_t.h, p_t.n - 1)
                v_s.inc(v.tensor_scalar(cmb[:, b, :], esb[:, b, :], rcol[:, b, 0:1], None, ALU.is_ge))
                v.wait_ge(v_s.h, v_s.n)
                v_s.inc(v.scalar_tensor_tensor(cmb[:, b, :], cmb[:, b, :], -1e30, esb[:, b, :], ALU.mult, ALU.add))
                v.wait_ge(v_s.h, v_s.n)
                v_s.inc(v.reduce_max(rcol[:, b, 2:3], cmb[:, b, :], axis=AX.X))
                v.wait_ge(v_s.h, v_s.n)
                v_s.inc(v.tensor_scalar(cmb[:, b, :], esb[:, b, :], rcol[:, b, 2:3], None, ALU.is_ge))
                v.wait_ge(v_s.h, v_s.n)
                v_c.inc(v.scalar_tensor_tensor(cmb[:, b, :], esb[:, b, :], rcol[:, b, 1:2], cmb[:, b, :], ALU.mult, ALU.mult))
                cmb_done.append(v_c.n)

            @blk.tensor
            def _(t, i=i, b=b):
                t.wait_ge(v_c.h, cmb_done[i])
                if i >= 1:
                    t.wait_ge(v_o.h, evo_done[i - 1])
                p_t.inc(t.transpose(ps_c[:, :], cmb[:, b, :], self.identf[:, :]))

            @blk.vector
            def _(v, i=i):
                v.wait_ge(p_t.h, p_t.n)
                v_o.inc(v.tensor_copy(self.combine[0:8, i * 128:(i + 1) * 128], ps_c[0:8, :]))
                evo_done.append(v_o.n)

    def p2_stats(self, blk, ph):
        nc, c = self.nc, self.ctx
        st = ph.enter_context(nc.sbuf_tensor("st", [NH, 5, T], F32))
        datm = ph.enter_context(nc.sbuf_tensor("datm", [128, KD, 64], F32))
        cslast = ph.enter_context(nc.sbuf_tensor("cslast", [NH, NCH], F32))
        logD = ph.enter_context(nc.sbuf_tensor("logD", [NH, NCH], F32))
        ecl = ph.enter_context(nc.sbuf_tensor("ecl", [NH, NCH], F32))
        eD = ph.enter_context(nc.sbuf_tensor("eD", [NH, NCH], F32))
        vb = ph.enter_context(nc.sbuf_tensor("vb", [64, 64], F32))
        pf = [ph.enter_context(nc.psum_tensor(f"p2s_f{i}", [128, 512], F32)) for i in range(2)]
        pc = ph.enter_context(nc.psum_tensor("p2s_c", [64, 256], F32))
        pq = ph.enter_context(nc.psum_tensor("p2s_q", [128, 32], F32))
        vx = self.xV
        ax = self.xA
        px = self.xT

        @blk.vector
        def _(v):
            vx.inc(v.tensor_scalar(st[:, 0, :], self.dt_raw[:, :], self.dtb_col[:, 0:1], None, ALU.add))

        @blk.scalar
        def _(sc):
            sc.wait_ge(vx.h, vx.n)
            ax.inc(sc.activation(st[:, 1, :], st[:, 0, :], AF.Abs))
            sc.wait_ge(ax.h, ax.n)
            ax.inc(sc.activation(st[:, 1, :], st[:, 1, :], AF.Exp, scale=-1.0))
            sc.wait_ge(ax.h, ax.n)
            ax.inc(sc.activation(st[:, 1, :], st[:, 1, :], AF.Ln, bias=1.0))

        @blk.vector
        def _(v):
            v.wait_ge(ax.h, ax.n)
            vx.inc(v.tensor_scalar_max(st[:, 2, :], st[:, 0, :], 0.0))
            v.wait_ge(vx.h, vx.n)
            vx.inc(v.tensor_add(self.dt_all[:, :], st[:, 2, :], st[:, 1, :]))
            v.wait_ge(vx.h, vx.n)
            vx.inc(v.tensor_scalar(st[:, 0, :], self.dt_all[:, :], self.A_col[:, 0:1], None, ALU.mult))

        @blk.scalar
        def _(sc):
            sc.wait_ge(vx.h, vx.n)
            ax.inc(sc.activation(st[:, 3, :], self.dt_all[:, :], AF.Ln))

        @blk.tensor
        def _(t):
            t.wait_ge(vx.h, vx.n)
            for k in range(KD):
                px.inc(t.transpose(pf[k // 8][:, (k % 8) * 64:(k % 8) * 64 + 64],
                                   st[0:64, 0, k * 128:(k + 1) * 128], self.identf[0:64, 0:64]))

        @blk.vector
        def _(v):
            v.wait_ge(px.h, px.n)
            for bk in range(2):
                src = pf[bk].ap().rearrange("p (k x) -> p k x", k=8)
                vx.inc(v.tensor_copy(datm[:, bk * 8:(bk + 1) * 8, :], src))

        cs_ev = []
        for ci in range(NCH):
            @blk.tensor
            def _(t, ci=ci):
                if ci == 0:
                    t.wait_ge(vx.h, vx.n)
                else:
                    t.wait_ge(vx.h, cs_ev[ci - 1])
                for s in range(2):
                    mm = t.matmul(pc[0:64, :], datm[:, 2 * ci + s, :], self.utri[:, s, :],
                                  start=(s == 0), stop=(s == 1))
                    if s == 1:
                        px.inc(mm)

            @blk.vector
            def _(v, ci=ci):
                v.wait_ge(px.h, px.n)
                vx.inc(v.tensor_copy(self.cs_all[:, ci * CH:(ci + 1) * CH], pc[0:64, :]))
                cs_ev.append(vx.n)

        @blk.vector
        def _(v):
            v.wait_ge(vx.h, vx.n)
            vx.inc(v.tensor_copy(self.cs_hi[:, :], self.cs_all[:, :]))
            v.wait_ge(vx.h, vx.n)
            vx.inc(v.tensor_sub(self.cs_lo[:, :], self.cs_all[:, :], self.cs_hi[:, :]))
            vx.inc(v.tensor_copy(cslast[:, :], self.cs_all[:, CH - 1::CH]))
            v.wait_ge(vx.h, vx.n)
            vx.inc(v.reduce_sum(self.logP[:, 0:1], cslast[:, :], axis=AX.X))
            vx.inc(v.memset(logD[:, 0:1], 0.0))
            v.wait_ge(vx.h, vx.n)
            for ci in range(1, NCH):
                vx.inc(v.tensor_add(logD[:, ci:ci + 1], logD[:, ci - 1:ci], cslast[:, ci - 1:ci]))
                v.wait_ge(vx.h, vx.n)
            v.wait_ge(ax.h, ax.n)
            vx.inc(v.tensor_sub(st[:, 3, :], self.cs_all[:, :], st[:, 3, :]))
            for ci in range(NCH):
                vx.inc(v.tensor_scalar(st[:, 4, ci * CH:(ci + 1) * CH],
                                       self.cs_all[:, ci * CH:(ci + 1) * CH],
                                       cslast[:, ci:ci + 1], None, ALU.subtract))

        @blk.scalar
        def _(sc):
            sc.wait_ge(vx.h, vx.n)
            sc.activation(self.ecs16[:, :], self.cs_all[:, :], AF.Exp)
            sc.activation(st[:, 4, :], st[:, 4, :], AF.Exp, scale=-1.0)
            sc.activation(ecl[:, :], cslast[:, :], AF.Exp)
            ax.inc(sc.activation(eD[:, :], logD[:, :], AF.Exp))

        @blk.vector
        def _(v):
            v.wait_ge(ax.h, ax.n)
            vx.inc(v.tensor_mul(st[:, 4, :], self.dt_all[:, :], st[:, 4, :]))

        tr_ev = []
        for ci in range(NCH):
            @blk.tensor
            def _(t, ci=ci):
                if ci == 0:
                    t.wait_ge(vx.h, vx.n)
                elif ci >= 2:
                    t.wait_ge(vx.h, tr_ev[ci - 2])
                for s in range(2):
                    px.inc(t.transpose(pf[ci % 2][:, s * 64:s * 64 + 64],
                                       st[0:64, 3, ci * CH + s * 128: ci * CH + (s + 1) * 128],
                                       self.identf[0:64, 0:64]))
                    px.inc(t.transpose(pf[ci % 2][:, 128 + s * 64: 128 + s * 64 + 64],
                                       st[0:64, 4, ci * CH + s * 128: ci * CH + (s + 1) * 128],
                                       self.identf[0:64, 0:64]))

            @blk.vector
            def _(v, ci=ci):
                v.wait_ge(px.h, px.n)
                src = pf[ci % 2].ap().rearrange("p (s x) -> p s x", x=64)
                vx.inc(v.tensor_copy(self.csldt[:, ci, :, :], src[:, 0:2, :]))
                vx.inc(v.tensor_copy(self.g_tm[:, ci, :, :], src[:, 2:4, :]))
                tr_ev.append(vx.n)

        for ci in range(NCH):
            for which in range(2):
                srccol = ecl if which == 0 else eD
                dst = self.cdcol if which == 0 else self.Dcol

                @blk.vector
                def _(v, ci=ci, srccol=srccol):
                    v.wait_ge(vx.h, vx.n)
                    v.wait_ge(px.h, px.n)
                    vx.inc(v.tensor_scalar(vb[:, :], self.ones_f[:, 0:64], srccol[:, ci:ci + 1], None, ALU.mult))

                @blk.tensor
                def _(t):
                    t.wait_ge(vx.h, vx.n)
                    t.matmul(pq[0:64, :], vb[:, :], self.Eev[:, :], start=True, stop=True)
                    px.inc(t.matmul(pq[64:128, :], vb[:, :], self.Eod[:, :], start=True, stop=True))

                @blk.vector
                def _(v, ci=ci, dst=dst):
                    v.wait_ge(px.h, px.n)
                    vx.inc(v.tensor_copy(dst[:, ci, :], pq[:, :]))

    def p2_main(self, blk, ph):
        nc, c = self.nc, self.ctx
        xbcb = ph.enter_context(nc.sbuf_tensor("xbcb", [128, NXBC, CH + 3], BF16))
        cacc = ph.enter_context(nc.sbuf_tensor("cacc", [128, 2, CH], F32))
        xsil = ph.enter_context(nc.sbuf_tensor("xsil", [128, 4, CH], BF16))
        xtmb = ph.enter_context(nc.sbuf_tensor("xtmb", [128, 2, DI], BF16))
        dxb = ph.enter_context(nc.sbuf_tensor("dxb", [128, KDI, CH], BF16))
        btm = ph.enter_context(nc.sbuf_tensor("btm", [128, 2, 128], BF16))
        bg = ph.enter_context(nc.sbuf_tensor("bg", [128, 4, 128], BF16))
        prevb = ph.enter_context(nc.sbuf_tensor("prevb", [128, NPAIR * DST], BF16))
        zc0 = ph.enter_context(nc.sbuf_tensor("zc0", [64, 1], F32))
        pxb = [ph.enter_context(nc.psum_tensor(f"p2_x{i}", [128, 1024], BF16)) for i in range(2)]
        ps_bt = ph.enter_context(nc.psum_tensor("p2_bt", [128, 256], BF16))
        pst = [ph.enter_context(nc.psum_tensor(f"p2_s{i}", [128, 128], F32)) for i in range(2)]
        s_x = Ctr(nc, c, "p2_sx")
        g_cv = Ctr(nc, c, "p2_gcv")
        v_cv = Ctr(nc, c, "p2_vcv")
        a_sl = Ctr(nc, c, "p2_asl")
        p_tx = Ctr(nc, c, "p2_ptx")
        v_xt = Ctr(nc, c, "p2_vxt")
        v_dx = Ctr(nc, c, "p2_vdx")
        v_bt = self.xV
        v_bg = Ctr(nc, c, "p2_vbg")
        v_pb = Ctr(nc, c, "p2_vpb")
        p_sm = Ctr(nc, c, "p2_psm")
        v_pu = Ctr(nc, c, "p2_vpu")
        s_xt = self.xS
        s_dx = self.xS
        s_pv = self.xS
        g_cc = Ctr(nc, c, "p2_gcc")
        self.cc_sem = g_cc
        cv_done = [(0, 0)]
        sm_done = [0]
        sxt_done = [0]
        sdx_done = [0]
        spv_done = [0]

        for ci in range(NCH):
            @blk.sync
            def _(s, ci=ci):
                if ci >= 1:
                    s.wait_ge(v_cv.h, cv_done[ci][0])
                    s.wait_ge(g_cv.h, cv_done[ci][1])
                s_x.inc(s.dma_start(
                    out=xbcb[:, :, :],
                    in_=self.d_xbc.rearrange("(f p) t -> p f t", p=128)[:, :, ci * CH: ci * CH + CH + 3]), 16)

            sil_base = a_sl.n
            tx_base = p_tx.n
            dx_base = v_dx.n
            conv_done = {}

            @blk.vector
            def _(v, ci=ci, sil_base=sil_base, tx_base=tx_base):
                v.wait_ge(s_x.h, 16 * (ci + 1))
                for f in range(NXBC + 2):
                    if f < NXBC:
                        b = f % 2
                        if sil_base + f - 1 > 0:
                            v.wait_ge(a_sl.h, sil_base + f - 1)
                        v_cv.inc(v.tensor_scalar(cacc[:, b, :], xbcb[:, f, 0:CH], self.wcv[:, f, 0:1], None, ALU.mult))
                        for kk in range(1, 4):
                            v.wait_ge(v_cv.h, v_cv.n)
                            v_cv.inc(v.scalar_tensor_tensor(cacc[:, b, :], xbcb[:, f, kk:CH + kk], self.wcv[:, f, kk:kk + 1],
                                                            cacc[:, b, :], ALU.mult, ALU.add))
                        conv_done[f] = (v_cv, v_cv.n)
                    ff = f - 2
                    if 0 <= ff < 32:
                        v.wait_ge(a_sl.h, sil_base + ff + 1)
                        if ff == 0 and ci >= 1:
                            v.wait_ge(s_pv.h, spv_done[ci])
                        v_dx.inc(v.tensor_scalar(dxb[:, ff, :], xsil[:, ff % 4, :],
                                                 self.D_pair[:, ff:ff + 1], None, ALU.mult))
                        if ff % 8 == 7:
                            g4 = ff // 8
                            v.wait_ge(p_tx.h, tx_base + 16 * (g4 + 1))
                            if g4 == 0 and ci >= 1:
                                v.wait_ge(p_sm.h, sm_done[ci])
                                v.wait_ge(s_pv.h, spv_done[ci])
                            for j in range(2):
                                src = pxb[j].ap().rearrange("p (k x) -> p k x", k=8)
                                v_xt.inc(v.tensor_copy(
                                    xtmb.ap().rearrange("p j (k x) -> p j k x", k=KDI)[:, j, g4 * 8:(g4 + 1) * 8, :], src))
                cv_done.append((v_cv.n, g_cv.n))

            @blk.scalar
            def _(sc, ci=ci, conv_done=conv_done, tx_base=tx_base, dx_base=dx_base):
                for f in range(NXBC):
                    b = f % 2
                    cctr, cval = conv_done[f]
                    sc.wait_ge(cctr.h, cval)
                    if f < 32:
                        if f >= 4:
                            sc.wait_ge(p_tx.h, tx_base + 2 * (f - 3))
                            sc.wait_ge(v_dx.h, dx_base + f - 3)
                        a_sl.inc(sc.activation(xsil[:, f % 4, :], cacc[:, b, :], AF.Silu))
                    elif f == 32:
                        a_sl.inc(sc.activation(self.B_all[:, ci * CH:(ci + 1) * CH], cacc[:, b, :], AF.Silu))
                    else:
                        a_sl.inc(sc.activation(self.C_all[:, ci * CH:(ci + 1) * CH], cacc[:, b, :], AF.Silu))

            sil2 = a_sl.n - NXBC

            @blk.tensor
            def _(t, ci=ci, sil2=sil2):
                for f in range(32):
                    g4 = f // 8
                    t.wait_ge(a_sl.h, sil2 + f + 1)
                    if f % 8 == 0 and (ci > 0 or g4 > 0):
                        t.wait_ge(v_xt.h, 2 * (4 * ci + g4))
                    for j in range(2):
                        p_tx.inc(t.transpose(pxb[j][:, (f % 8) * 128:(f % 8) * 128 + 128],
                                             xsil[:, f % 4, j * 128:(j + 1) * 128],
                                             self.identb[:, :]))
                t.wait_ge(a_sl.h, sil2 + 33)
                for s in range(2):
                    p_tx.inc(t.transpose(ps_bt[:, s * 128:(s + 1) * 128],
                                         self.B_all[:, ci * CH + s * 128: ci * CH + (s + 1) * 128],
                                         self.identb[:, :]))

            tx2 = p_tx.n - 66
            bg_base = v_bg.n
            sm_base = p_sm.n

            @blk.vector
            def _(v, ci=ci, tx2=tx2):
                v.wait_ge(p_tx.h, tx2 + 66)
                v_bt.inc(v.tensor_copy(btm.ap().rearrange("p s x -> p (s x)"), ps_bt[:, :]))
                if ci >= 1:
                    v.wait_ge(s_pv.h, spv_done[ci])
                v_pb.inc(v.tensor_copy(prevb[:, :], self.prev[:, :]))

            pu_base = v_pu.n

            @blk.vector
            def _(v, ci=ci, sm_base=sm_base):
                v.wait_ge(v_pb.h, v_pb.n)
                for j in range(NPAIR):
                    if j >= 1:
                        v.wait_ge(p_sm.h, sm_base + j)
                    for h in (2 * j, 2 * j + 1):
                        for s in range(2):
                            ins = v.tensor_scalar(bg[:, (h % 2) * 2 + s, :], btm[:, s, :],
                                                  self.g_tm[:, ci, s, h:h + 1], None, ALU.mult)
                        v_bg.inc(ins)
                    if j >= 1:
                        jj = j - 1
                        v_pu.inc(v.scalar_tensor_tensor(self.prev[:, jj * DST:(jj + 1) * DST],
                                                        self.prev[:, jj * DST:(jj + 1) * DST],
                                                        self.cdcol[:, ci, jj:jj + 1],
                                                        pst[jj % 2][:, :], ALU.mult, ALU.add))
                v.wait_ge(p_sm.h, sm_base + NPAIR)
                jj = NPAIR - 1
                v_pu.inc(v.scalar_tensor_tensor(self.prev[:, jj * DST:(jj + 1) * DST],
                                                self.prev[:, jj * DST:(jj + 1) * DST],
                                                self.cdcol[:, ci, jj:jj + 1],
                                                pst[jj % 2][:, :], ALU.mult, ALU.add))

            @blk.tensor
            def _(t, ci=ci, bg_base=bg_base, pu_base=pu_base):
                for j in range(NPAIR):
                    t.wait_ge(v_bg.h, bg_base + 2 * (j + 1))
                    if j >= 2:
                        t.wait_ge(v_pu.h, pu_base + j - 1)
                    h0, h1 = 2 * j, 2 * j + 1
                    t.matmul(pst[j % 2][0:64, :], xtmb[:, 0, 64 * h0:64 * h0 + 64], bg[:, 0, :],
                             start=True, stop=False, tile_position=(0, 0))
                    t.matmul(pst[j % 2][0:64, :], xtmb[:, 1, 64 * h0:64 * h0 + 64], bg[:, 1, :],
                             start=False, stop=True, tile_position=(0, 0))
                    t.matmul(pst[j % 2][64:128, :], xtmb[:, 0, 64 * h1:64 * h1 + 64], bg[:, 2, :],
                             start=True, stop=False, tile_position=(0, 64))
                    mm = t.matmul(pst[j % 2][64:128, :], xtmb[:, 1, 64 * h1:64 * h1 + 64], bg[:, 3, :],
                                  start=False, stop=True, tile_position=(0, 64))
                    p_sm.inc(mm)
                sm_done.append(p_sm.n)

            @blk.sync
            def _(s, ci=ci):
                s.wait_ge(v_xt.h, 8 * (ci + 1))
                for j in range(2):
                    s_xt.inc(s.dma_start(out=self.d_xtm[ci * CH + j * 128: ci * CH + (j + 1) * 128, :],
                                         in_=xtmb[:, j, :]), 16)
                sxt_done.append(s_xt.n)
                s.wait_ge(v_dx.h, 32 * (ci + 1))
                s_dx.inc(s.dma_start(out=self.d_dx.rearrange("(f p) t -> p f t", p=128)[:, :, ci * CH:(ci + 1) * CH],
                                     in_=dxb[:, :, :]), 16)
                sdx_done.append(s_dx.n)
                s.wait_ge(v_pb.h, ci + 1)
                s_pv.inc(s.dma_start(out=self.d_prev[ci, :, :], in_=prevb[:, :]), 16)
                spv_done.append(s_pv.n)

        @blk.vector
        def _(v):
            v.wait_ge(v_pu.h, v_pu.n)
            v_bt.inc(v.memset(zc0[:, 0:1], 0.0))

        @blk.sync
        def _(s):
            s.wait_ge(v_pu.h, v_pu.n)
            s.wait_ge(v_bt.h, v_bt.n)
            s_pv.inc(s.dma_start(out=self.cc_in[:, 0:NPAIR * DST], in_=self.prev[:, :]), 16)
            s_pv.inc(s.dma_start(out=self.cc_in[0:64, NPAIR * DST:NPAIR * DST + 1], in_=self.logP[:, 0:1], allow_slow_non_contiguous=True), 16)
            s_pv.inc(s.dma_start(out=self.cc_in[64:128, NPAIR * DST:NPAIR * DST + 1], in_=zc0[:, 0:1], allow_slow_non_contiguous=True), 16)

        self.s_xt_total = (s_pv, s_pv.n)

        @blk.gpsimd
        def _(g):
            g.wait_ge(s_pv.h, s_pv.n)
            g_cc.inc(g.collective_compute("AllGather", ALU.bypass,
                                          replica_groups=[list(range(8))],
                                          ins=[self.cc_in.opt()], outs=[self.cc_out.opt()]))

    def p3b_ffn(self, blk, ph):
        nc, c = self.nc, self.ctx
        xnh = ph.enter_context(nc.sbuf_tensor("xnh", [128, KD, 1024], BF16))
        acc = ph.enter_context(nc.sbuf_tensor("facc", [128, KD, 1024], BF16))
        he = ph.enter_context(nc.sbuf_tensor("he", [128, KE, 1024], BF16))
        wgu = ph.enter_context(nc.sbuf_tensor("wgu", [128, 2, 2, KD, 128], BF16))
        sgb = ph.enter_context(nc.sbuf_tensor("sgb", [128, 2, 2, 512], BF16))
        tb = ph.enter_context(nc.sbuf_tensor("tb", [128, 2, 512], BF16))
        cbe = ph.enter_context(nc.sbuf_tensor("cbe", [128, 1024], BF16))
        wd = ph.enter_context(nc.sbuf_tensor("wd", [128, 2, KE, 512], BF16))
        pp = [ph.enter_context(nc.psum_tensor(f"f_p{i}", [128, 512], F32)) for i in range(8)]
        pcb = pp[0]
        g_f = [Ctr(nc, c, "f_gf0"), Ctr(nc, c, "f_gf1")]
        g_dve = self.xV
        p_f = Ctr(nc, c, "f_pf")
        a_f = Ctr(nc, c, "f_af")
        v_f = Ctr(nc, c, "f_vf")
        p_cb = Ctr(nc, c, "f_pcb")
        v_cb = self.xV
        g_d = [Ctr(nc, c, "f_gd0"), Ctr(nc, c, "f_gd1")]
        p_d = Ctr(nc, c, "f_pd")
        v_d = Ctr(nc, c, "f_vd")
        s_h = Ctr(nc, c, "f_sh")
        s_o = self.xS

        experts = [("s", 0), ("s", 1)] + [("e", e) for e in range(8)]

        def wg_ap(kind, idx, dh):
            col = (idx * KE + dh) * 128 if kind == "s" else dh * 128
            src = self.ws_gate if kind == "s" else self.we_gate[idx]
            return src[:, col:col + 128].rearrange("(kc p) o -> p kc o", p=128)

        def wu_ap(kind, idx, dh):
            col = (idx * KE + dh) * 128 if kind == "s" else dh * 128
            src = self.ws_up if kind == "s" else self.we_up[idx]
            return src[:, col:col + 128].rearrange("(kc p) o -> p kc o", p=128)

        def wd_ap(kind, idx, q):
            r0 = idx * DE if kind == "s" else 0
            src = self.ws_down if kind == "s" else self.we_down[idx]
            return src[r0:r0 + DE, q * 512:(q + 1) * 512].rearrange("(kc p) o -> p kc o", p=128)

        for H in range(2):
            off = H * 1024
            ld_done = s_h.n

            @blk.sync
            def _(s, H=H, off=off):
                if H == 1:
                    s.wait_ge(p_f.h, p_f.n)
                    s.wait_ge(p_d.h, p_d.n)
                s_h.inc(s.dma_start(
                    out=xnh[:, :, :],
                    in_=self.d_xn.rearrange("(k p) t -> p k t", p=128)[:, :, 3 + off: 3 + off + 1024]), 16)

            for ei, (kind, idx) in enumerate(experts):
                has_cb = kind == "e"
                if has_cb:
                    for t2 in range(2):
                        @blk.tensor
                        def _(t, idx=idx, t2=t2, off=off):
                            t.wait_ge(v_cb.h, v_cb.n)
                            t.wait_ge(v_d.h, v_d.n)
                            p_cb.inc(t.matmul(pcb[:, :], self.sel8[:, idx, :],
                                              self.combine[0:8, off + t2 * 512: off + (t2 + 1) * 512],
                                              start=True, stop=True))

                        @blk.vector
                        def _(v, t2=t2):
                            v.wait_ge(p_cb.h, p_cb.n)
                            v_cb.inc(v.tensor_copy(cbe[:, t2 * 512:(t2 + 1) * 512], pcb[:, :]))

                gf_list = []
                pf_base = p_f.n
                af_base = a_f.n
                vf_base = v_f.n
                pd_prev = p_d.n

                @blk.gpsimd
                def _(g, kind=kind, idx=idx, pf_base=pf_base):
                    for dh in range(KE):
                        if dh >= 2:
                            g.wait_ge(p_f.h, pf_base + 4 * (dh - 1))
                        g_f[dh % 2].inc(g.dma_start(out=wgu[:, dh % 2, 0, :, :], in_=wg_ap(kind, idx, dh)), 16)
                        g_f[dh % 2].inc(g.dma_start(out=wgu[:, dh % 2, 1, :, :], in_=wu_ap(kind, idx, dh)), 16)
                        gf_list.append(g_f[dh % 2].n)

                vcb_now = v_cb.n
                vd_prev = v_d.n

                @blk.tensor
                def _(t, ei=ei, H=H, af_base=af_base, vf_base=vf_base, vcb_now=vcb_now, vd_prev=vd_prev):
                    for dh in range(KE):
                        t.wait_ge(g_f[dh % 2].h, gf_list[dh])
                        if dh == 0:
                            t.wait_ge(v_cb.h, vcb_now)
                            t.wait_ge(s_h.h, 16 * (H + 1))
                            t.wait_ge(v_d.h, vd_prev)
                        if dh >= 2:
                            t.wait_ge(a_f.h, af_base + 2 * (dh - 1))
                            t.wait_ge(v_f.h, vf_base + 2 * (dh - 1))
                        for gu in range(2):
                            for t2 in range(2):
                                bank = pp[4 * (dh % 2) + 2 * gu + t2]
                                for k in range(KD):
                                    mm = t.matmul(bank[:, :], wgu[:, dh % 2, gu, k, :],
                                                  xnh[:, k, t2 * 512:(t2 + 1) * 512],
                                                  start=(k == 0), stop=(k == KD - 1))
                                    if k == KD - 1:
                                        p_f.inc(mm)

                @blk.scalar
                def _(sc, pf_base=pf_base, vf_base=vf_base):
                    for dh in range(KE):
                        for t2 in range(2):
                            sc.wait_ge(p_f.h, pf_base + 4 * dh + 2 * 0 + t2 + 1)
                            if dh >= 2:
                                sc.wait_ge(v_f.h, vf_base + 2 * (dh - 1))
                            a_f.inc(sc.activation(sgb[:, dh % 2, t2, :],
                                                  pp[4 * (dh % 2) + t2][:, :], AF.Silu))

                @blk.vector
                def _(v, has_cb=has_cb, pf_base=pf_base, af_base=af_base, pd_prev=pd_prev):
                    for dh in range(KE):
                        for t2 in range(2):
                            v.wait_ge(a_f.h, af_base + 2 * dh + t2 + 1)
                            v.wait_ge(p_f.h, pf_base + 4 * dh + 2 + t2 + 1)
                            if dh == 0 and t2 == 0:
                                v.wait_ge(p_d.h, pd_prev)
                            upsum = pp[4 * (dh % 2) + 2 + t2]
                            if has_cb:
                                g_dve.inc(v.tensor_mul(tb[:, t2, :], upsum[:, :], sgb[:, dh % 2, t2, :]))
                                v.wait_ge(g_dve.h, g_dve.n)
                                v_f.inc(v.tensor_mul(he[:, dh, t2 * 512:(t2 + 1) * 512],
                                                     tb[:, t2, :], cbe[:, t2 * 512:(t2 + 1) * 512]))
                            else:
                                v_f.inc(v.tensor_mul(he[:, dh, t2 * 512:(t2 + 1) * 512],
                                                     upsum[:, :], sgb[:, dh % 2, t2, :]))

                gd_list = []
                pd_base = p_d.n
                vd_base = v_d.n
                vf_end = v_f.n

                @blk.gpsimd
                def _(g, kind=kind, idx=idx, pd_base=pd_base):
                    for q in range(4):
                        if q >= 2:
                            g.wait_ge(p_d.h, pd_base + (q - 1) * KE)
                        g_d[q % 2].inc(g.dma_start(out=wd[:, q % 2, :, :], in_=wd_ap(kind, idx, q)), 16)
                        gd_list.append(g_d[q % 2].n)

                @blk.tensor
                def _(t, ei=ei, vf_end=vf_end, vd_base=vd_base):
                    t.wait_ge(v_f.h, vf_end)
                    for q in range(4):
                        if q >= 1:
                            t.wait_ge(v_d.h, vd_base + q)
                        elif ei >= 1:
                            t.wait_ge(v_d.h, vd_base)
                        t.wait_ge(g_d[q % 2].h, gd_list[q])
                        for k in range(KE):
                            for d4 in range(4):
                                for t2 in range(2):
                                    mm = t.matmul(pp[2 * d4 + t2][:, :],
                                                  wd[:, q % 2, k, d4 * 128:(d4 + 1) * 128],
                                                  he[:, k, t2 * 512:(t2 + 1) * 512],
                                                  start=(k == 0), stop=(k == KE - 1))
                                    if d4 == 3 and t2 == 1:
                                        p_d.inc(mm)

                @blk.vector
                def _(v, ei=ei, H=H, pd_base=pd_base):
                    for q in range(4):
                        v.wait_ge(p_d.h, pd_base + (q + 1) * KE)
                        if H == 1 and ei == 0 and q == 0:
                            v.wait_ge(s_o.h, self._ffn_h0_store)
                        for d4 in range(4):
                            for t2 in range(2):
                                dd = q * 4 + d4
                                dst = acc[:, dd, t2 * 512:(t2 + 1) * 512]
                                last = d4 == 3 and t2 == 1
                                if ei == 0:
                                    ins = v.tensor_copy(dst, pp[2 * d4 + t2][:, :])
                                else:
                                    ins = v.tensor_add(dst, dst, pp[2 * d4 + t2][:, :])
                                if last:
                                    v_d.inc(ins)

            @blk.sync
            def _(s, off=off):
                s.wait_ge(v_d.h, v_d.n)
                s_o.inc(s.dma_start(
                    out=self.d_ffn.rearrange("(k p) t -> p k t", p=128)[:, :, off:off + 1024],
                    in_=acc[:, :, :]), 16)
            if H == 0:
                self._ffn_h0_store = s_o.n

        @blk.sync
        def _(s):
            s.wait_ge(s_o.h, s_o.n)

    def p4_hin(self, blk, ph):
        nc, c = self.nc, self.ctx
        self._p4_ctx = ExitStack()
        self.ccol = self._p4_ctx.enter_context(nc.sbuf_tensor("ccol", [128, 8, NPAIR], F32))
        self.sel = self._p4_ctx.enter_context(nc.sbuf_tensor("sel", [NH, NH, 128], BF16))
        self.h_in = self._p4_ctx.enter_context(nc.sbuf_tensor("h_in", [128, NPAIR * DST], F32))
        lp = ph.enter_context(nc.sbuf_tensor("lp", [64, 8], F32))
        lpT = ph.enter_context(nc.sbuf_tensor("lpT", [8, 64], F32))
        coefT = ph.enter_context(nc.sbuf_tensor("coefT", [64, 8], F32))
        vb2 = ph.enter_context(nc.sbuf_tensor("vb2", [64, 64], F32))
        hjb = ph.enter_context(nc.sbuf_tensor("hjb", [128, 2, NPAIR * DST], F32))
        pl8 = ph.enter_context(nc.psum_tensor("h_pl", [8, 64], F32))
        pq2 = ph.enter_context(nc.psum_tensor("h_pq", [128, 32], F32))
        p64 = ph.enter_context(nc.psum_tensor("h_p64", [64, 8], F32))
        s_l = self.xS
        v_h = self.xV
        p_h = self.xT
        a_h = self.xA
        s_j = [Ctr(nc, c, "h_sj0"), Ctr(nc, c, "h_sj1")]
        v_a = self.xV

        @blk.sync
        def _(s):
            s.wait_ge(self.cc_sem.h, 1)
            for j in range(8):
                s_l.inc(s.dma_start(out=lp[:, j:j + 1],
                                    in_=self.cc_out[j * 128:j * 128 + 64, NPAIR * DST:NPAIR * DST + 1],
                                    allow_slow_non_contiguous=True), 16)

        @blk.vector
        def _(v):
            v.wait_ge(s_l.h, s_l.n)
            v_h.inc(v.memset(self.h_in[:, :], 0.0))

        @blk.tensor
        def _(t):
            t.wait_ge(s_l.h, s_l.n)
            p_h.inc(t.transpose(pl8[0:8, :], lp[:, 0:8], self.identf[0:64, 0:64]))

        @blk.vector
        def _(v):
            v.wait_ge(p_h.h, p_h.n)
            v_h.inc(v.tensor_copy(lpT[:, :], pl8[0:8, :]))

        @blk.tensor
        def _(t):
            t.wait_ge(v_h.h, v_h.n)
            p_h.inc(t.matmul(pl8[0:8, :], self.mselT_sb[:, :], lpT[:, :], start=True, stop=True))

        @blk.scalar
        def _(sc):
            sc.wait_ge(p_h.h, p_h.n)
            a_h.inc(sc.activation(lpT[:, :], pl8[0:8, :], AF.Exp, bias=self.cbias_col[:, 0:1]))

        @blk.tensor
        def _(t):
            t.wait_ge(a_h.h, a_h.n)
            p_h.inc(t.transpose(p64[0:64, 0:8], lpT[0:8, 0:64], self.identf[0:8, 0:8]))

        @blk.vector
        def _(v):
            v.wait_ge(p_h.h, p_h.n)
            v_h.inc(v.tensor_copy(coefT[:, :], p64[0:64, 0:8]))

        for j in range(8):
            @blk.vector
            def _(v, j=j):
                v.wait_ge(v_h.h, v_h.n)
                v.wait_ge(p_h.h, p_h.n)
                v_h.inc(v.tensor_scalar(vb2[:, :], self.ones_f[:, 0:64], coefT[:, j:j + 1], None, ALU.mult))

            @blk.tensor
            def _(t):
                t.wait_ge(v_h.h, v_h.n)
                t.matmul(pq2[0:64, :], vb2[:, :], self.Eev[:, :], start=True, stop=True)
                p_h.inc(t.matmul(pq2[64:128, :], vb2[:, :], self.Eod[:, :], start=True, stop=True))

            @blk.vector
            def _(v, j=j):
                v.wait_ge(p_h.h, p_h.n)
                v_h.inc(v.tensor_copy(self.ccol[:, j, :], pq2[:, :]))

        va_list = []
        for j in range(8):
            @blk.sync
            def _(s, j=j):
                if j >= 2:
                    s.wait_ge(v_a.h, va_list[j - 2])
                s_j[j % 2].inc(s.dma_start(out=hjb[:, j % 2, :], in_=self.cc_out[j * 128:(j + 1) * 128, 0:NPAIR * DST]), 16)

            @blk.vector
            def _(v, j=j):
                v.wait_ge(s_j[j % 2].h, 16 * (j // 2 + 1))
                if j == 0:
                    v.wait_ge(v_h.h, v_h.n)
                else:
                    v.wait_ge(v_a.h, va_list[j - 1])
                for m in range(NPAIR):
                    ins = v.scalar_tensor_tensor(self.h_in[:, m * DST:(m + 1) * DST],
                                                 hjb[:, j % 2, m * DST:(m + 1) * DST],
                                                 self.ccol[:, j, m:m + 1],
                                                 self.h_in[:, m * DST:(m + 1) * DST], ALU.mult, ALU.add)
                v_a.inc(ins)
                va_list.append(v_a.n)

    def p4_mid(self, blk, ph):
        nc, c = self.nc, self.ctx
        NGR = NCH * 4
        NR = NCH * NPAIR
        zg = ph.enter_context(nc.sbuf_tensor("zg", [128, 2, 8, CH], BF16))
        dxg = ph.enter_context(nc.sbuf_tensor("dxg", [128, 2, 8, CH], BF16))
        xtg = ph.enter_context(nc.sbuf_tensor("xtg", [128, 2, 2, 1024], BF16))
        plg = ph.enter_context(nc.sbuf_tensor("plg", [128, 2, 1024], BF16))
        ptbg = ph.enter_context(nc.sbuf_tensor("ptbg", [128, 2, 1024], BF16))
        pTg = ph.enter_context(nc.sbuf_tensor("pTg", [128, 2, 1024], BF16))
        srt = ph.enter_context(nc.sbuf_tensor("srt", [128, 2, CH], BF16))
        segb = ph.enter_context(nc.sbuf_tensor("segb", [128, 4, CH], F32))
        lmb = ph.enter_context(nc.sbuf_tensor("lmb", [128, 2, 4, CH], BF16))
        stdb = ph.enter_context(nc.sbuf_tensor("stdb", [128, 2, 4, CH], BF16))
        ecsb = ph.enter_context(nc.sbuf_tensor("ecsb", [128, 2, CH], BF16))
        v1b = ph.enter_context(nc.sbuf_tensor("v1b", [128, 2, CH], BF16))
        t1b = ph.enter_context(nc.sbuf_tensor("t1b", [128, 2, CH], BF16))
        v2b = ph.enter_context(nc.sbuf_tensor("v2b", [128, 2, CH], BF16))
        ygb2 = ph.enter_context(nc.sbuf_tensor("ygb2", [128, 2, CH], BF16))
        sqb = ph.enter_context(nc.sbuf_tensor("sqb", [128, 2, CH], BF16))
        ygsb = ph.enter_context(nc.sbuf_tensor("ygsb", [128, 2, 8, CH], BF16))
        rr = ph.enter_context(nc.sbuf_tensor("rr", [1, 3, CH], F32))
        M0 = ph.enter_context(nc.psum_tensor("m_M0", [128, 1024], BF16))
        M1 = ph.enter_context(nc.psum_tensor("m_M1", [128, 512], F32))
        M2 = ph.enter_context(nc.psum_tensor("m_M2", [128, 512], F32))
        M3 = [ph.enter_context(nc.psum_tensor(f"m_M3{i}", [128, CH], F32)) for i in range(2)]
        M4 = ph.enter_context(nc.psum_tensor("m_M4", [128, 512], F32))

        s_c = Ctr(nc, c, "m_sc")
        s_y = Ctr(nc, c, "m_sy")
        v_m = Ctr(nc, c, "m_vm")
        a_m = Ctr(nc, c, "m_am")
        p_m = Ctr(nc, c, "m_pm")

        ld_done = [0] * (NGR + 2)
        ptb_done = [0] * (NGR + 1)
        pte_done = [0] * (NGR + 1)
        tr_done = [0] * (NGR + 1)
        st_done = [0] * (NGR + 1)
        bc_done = [0] * (NR + 1)
        yd_done = [0] * (NR + 1)
        yo_done = [0] * (NR + 1)
        ec_done = [0] * (NR + 1)
        rm_done = [0] * (NR + 1)
        sg_done = [0] * (NR + 1)
        ex_done = [0] * (NR + 1)
        sd_done = [0] * (NR + 1)
        t1_done = [0] * (NR + 1)
        v1_done = [0] * (NR + 1)
        ecs_done = [0] * (NR + 1)
        sq_done = [0] * (NR + 1)
        yn_done = [0] * (NR + 1)
        sc_done = [0] * (NCH + 1)
        sr_done = [0] * (NCH + 1)
        rr0_done = [0] * (NCH + 1)
        rsq_done = [0] * (NCH + 1)
        rcp_done = [0] * (NCH + 1)
        rs_done = [0] * (NCH + 1)

        def em_loads(s, gi):
            ci2, g2 = gi // 4, gi % 4
            c0 = ci2 * CH
            sl = gi % 2
            s_c.inc(s.dma_start(
                out=zg[:, sl, :, :],
                in_=self.d_z.rearrange("(f p) t -> p f t", p=128)[:, 8 * g2:8 * g2 + 8, c0:c0 + CH]), 16)
            s_c.inc(s.dma_start(
                out=dxg[:, sl, :, :],
                in_=self.d_dx.rearrange("(f p) t -> p f t", p=128)[:, 8 * g2:8 * g2 + 8, c0:c0 + CH]), 16)
            for s2 in range(2):
                s_c.inc(s.dma_start(
                    out=xtg[:, sl, s2, :],
                    in_=self.d_xtm[c0 + s2 * 128: c0 + (s2 + 1) * 128, 1024 * g2:1024 * (g2 + 1)]), 16)
            s_c.inc(s.dma_start(out=plg[:, sl, :], in_=self.d_prev[ci2, :, 1024 * g2:1024 * (g2 + 1)]), 16)
            ld_done[gi] = s_c.n

        @blk.vector
        def _(v):
            for h in range(NH):
                v.tensor_scalar(self.sel[:, h, :], self.ones_b[0:NH, :],
                                self.identf[0:NH, h:h + 1], None, ALU.mult)
            v_m.inc(v.engine_nop())
        selb_done = v_m.n

        @blk.sync
        def _(s):
            s.wait_ge(self.s_xt_total[0].h, self.s_xt_total[1])
            em_loads(s, 0)

        @blk.tensor
        def _(t):
            for s2 in range(2):
                mm = t.matmul(M1[:, s2 * CH:(s2 + 1) * CH],
                              self.B_all[:, s2 * 128:(s2 + 1) * 128],
                              self.C_all[:, 0:CH], start=(s2 == 0), stop=(s2 == 1))
            p_m.inc(mm)
        sc_done[0] = p_m.n

        @blk.vector
        def _(v):
            v.wait_ge(p_m.h, sc_done[0])
            v.tensor_copy(srt[:, 0, :], M1[:, 0:CH])
            v_m.inc(v.tensor_copy(srt[:, 1, :], M1[:, CH:2 * CH]))
        sr_done[0] = v_m.n

        for r in range(NR + 1):
            bnd = (r % 8 == 0) and r < NR
            gi = r // 8
            ci = r // 32
            j = r % 32
            p = r - 1
            if p >= 0:
                gp_i, cp_i, jp, jjp = p // 8, p // 32, p % 32, p % 8

            if bnd:
                @blk.vector
                def _(v, gi=gi, ci=ci):
                    v.wait_ge(s_c.h, ld_done[gi])
                    if gi == 0:
                        v.wait_ge(self.hin_sem.h, self.hin_total)
                    if gi >= 2:
                        v.wait_ge(p_m.h, tr_done[gi - 2])
                    for jj in range(8):
                        ja = (gi % 4) * 8 + jj
                        v_m.inc(v.scalar_tensor_tensor(
                            ptbg[:, gi % 2, jj * 128:(jj + 1) * 128],
                            self.h_in[:, ja * DST:(ja + 1) * DST],
                            self.Dcol[:, ci, ja:ja + 1],
                            plg[:, gi % 2, jj * 128:(jj + 1) * 128], ALU.mult, ALU.add))
                    ptb_done[gi] = v_m.n

            @blk.tensor
            def _(t, r=r, bnd=bnd, gi=gi, ci=ci, j=j, p=p):
                if bnd:
                    t.wait_ge(v_m.h, ptb_done[gi])
                    if gi >= 1:
                        t.wait_ge(v_m.h, pte_done[gi - 1])
                    for jj in range(8):
                        p_m.inc(t.transpose(M0[:, jj * 128:(jj + 1) * 128],
                                            ptbg[:, gi % 2, jj * 128:(jj + 1) * 128],
                                            self.identb[:, :]))
                    tr_done[gi] = p_m.n
                if r < NR:
                    if r >= 1:
                        t.wait_ge(v_m.h, sg_done[r - 1])
                    for dl in range(2):
                        h = 2 * j + dl
                        t.matmul(M2[:, dl * CH:(dl + 1) * CH], self.sel[0:NH, h, :],
                                 self.cs_hi[:, ci * CH:(ci + 1) * CH], start=True, stop=False)
                        mm = t.matmul(M2[:, dl * CH:(dl + 1) * CH], self.sel[0:NH, h, :],
                                      self.cs_lo[:, ci * CH:(ci + 1) * CH], start=False, stop=True)
                        p_m.inc(mm)
                    bc_done[r] = p_m.n
                if r >= 1:
                    gp_i2, cp_i2, jp2, jjp2 = p // 8, p // 32, p % 32, p % 8
                    t.wait_ge(v_m.h, sd_done[p])
                    t.wait_ge(s_c.h, ld_done[gp_i2])
                    wv = 0
                    if p >= 2:
                        wv = max(wv, t1_done[p - 2])
                    if p >= 1:
                        wv = max(wv, v1_done[p - 1], ecs_done[p - 1])
                    wv = max(wv, pte_done[gp_i2])
                    if wv:
                        t.wait_ge(v_m.h, wv)
                    for q in range(4):
                        dl, s2 = q // 2, q % 2
                        hl = 2 * jjp2 + dl
                        mm = t.matmul(M3[p % 2][64 * dl:64 * dl + 64, :],
                                      xtg[:, gp_i2 % 2, s2, 64 * hl:64 * hl + 64],
                                      stdb[:, p % 2, q, :],
                                      start=(s2 == 0), stop=(s2 == 1), tile_position=(0, 64 * dl))
                        if q == 3:
                            p_m.inc(mm)
                    yd_done[p] = p_m.n
                    for dl in range(2):
                        mm = t.matmul(M4[64 * dl:64 * dl + 64, 0:CH],
                                      pTg[:, gp_i2 % 2, jjp2 * DST + 64 * dl: jjp2 * DST + 64 * dl + 64],
                                      self.C_all[:, cp_i2 * CH:(cp_i2 + 1) * CH],
                                      start=True, stop=True, tile_position=(0, 64 * dl))
                        if dl == 1:
                            p_m.inc(mm)
                    yo_done[p] = p_m.n
                    for dl in range(2):
                        h = 2 * jp2 + dl
                        mm = t.matmul(M4[64 * dl:64 * dl + 64, CH:2 * CH],
                                      self.sel[0:NH, h, 0:64],
                                      self.ecs16[:, cp_i2 * CH:(cp_i2 + 1) * CH],
                                      start=True, stop=True, tile_position=(0, 64 * dl))
                        if dl == 1:
                            p_m.inc(mm)
                    ec_done[p] = p_m.n
                qq = r - 2
                if qq >= 0 and qq % 32 != 31:
                    t.wait_ge(v_m.h, sq_done[qq])
                    if qq % 32 == 0:
                        t.wait_ge(v_m.h, sr_done[qq // 32])
                    mm = t.matmul(M1[:, 0:CH], self.ones_b[:, :], sqb[:, qq % 2, :],
                                  start=(qq % 32 == 0), stop=False)
                    p_m.inc(mm)
                    rm_done[qq] = p_m.n

            @blk.vector
            def _(v, r=r, bnd=bnd, gi=gi, ci=ci, j=j, p=p):
                if bnd:
                    v.wait_ge(p_m.h, tr_done[gi])
                    if gi >= 2:
                        v.wait_ge(p_m.h, yo_done[(gi - 1) * 8 - 1])
                    v_m.inc(v.tensor_copy(pTg[:, gi % 2, :], M0[:, :]))
                    pte_done[gi] = v_m.n
                if r < NR:
                    v.wait_ge(p_m.h, bc_done[r])
                    if r >= 1:
                        v.wait_ge(a_m.h, ex_done[r - 1])
                    for q in range(4):
                        dl, s2 = q // 2, q % 2
                        h = 2 * j + dl
                        v_m.inc(v.scalar_tensor_tensor(
                            segb[:, q, :], M2[:, dl * CH:(dl + 1) * CH],
                            self.csldt[:, ci, s2, h:h + 1],
                            self.maskT[:, s2, :], ALU.subtract, ALU.add))
                    sg_done[r] = v_m.n
                if r >= 1:
                    v.wait_ge(p_m.h, ec_done[p])
                    v_m.inc(v.tensor_copy(ecsb[:, p % 2, :], M4[:, CH:2 * CH]))
                    ecs_done[p] = v_m.n
                    v.wait_ge(v_m.h, v_m.n)
                    v_m.inc(v.tensor_mul(v1b[:, p % 2, :], M4[:, 0:CH], ecsb[:, p % 2, :]))
                    v1_done[p] = v_m.n
                    v.wait_ge(v_m.h, v_m.n)
                    v_m.inc(v.tensor_add(t1b[:, p % 2, :], v1b[:, p % 2, :], M3[p % 2][:, :]))
                    t1_done[p] = v_m.n

            if r < NR:
                @blk.scalar
                def _(sc, r=r):
                    sc.wait_ge(v_m.h, sg_done[r])
                    if r >= 2:
                        sc.wait_ge(v_m.h, sd_done[r - 2])
                    for q in range(4):
                        a_m.inc(sc.activation(lmb[:, r % 2, q, :], segb[:, q, :], AF.Exp))
                    ex_done[r] = a_m.n

            @blk.vector
            def _(v, r=r, ci=ci, p=p):
                if r >= 1:
                    gp_i2, jp2, jjp2 = p // 8, p % 32, p % 8
                    v.wait_ge(s_c.h, ld_done[gp_i2])
                    v_m.inc(v.tensor_add(v2b[:, p % 2, :], t1b[:, p % 2, :], dxg[:, gp_i2 % 2, jjp2, :]))
                    v.wait_ge(v_m.h, v_m.n)
                    v_m.inc(v.tensor_mul(ygb2[:, p % 2, :], v2b[:, p % 2, :], zg[:, gp_i2 % 2, jjp2, :]))
                    v.wait_ge(v_m.h, v_m.n)
                    v_m.inc(v.tensor_mul(sqb[:, p % 2, :], ygb2[:, p % 2, :], ygb2[:, p % 2, :]))
                    sq_done[p] = v_m.n
                    if gp_i2 >= 2:
                        v.wait_ge(s_y.h, st_done[gp_i2 - 2])
                    v_m.inc(v.tensor_scalar(ygsb[:, gp_i2 % 2, jjp2, :], ygb2[:, p % 2, :],
                                            self.normw[:, jp2:jp2 + 1], None, ALU.mult))
                    yn_done[p] = v_m.n
                if r < NR and not (r % 32 == 0 and r >= 32):
                    v.wait_ge(a_m.h, ex_done[r])
                    if r >= 2:
                        v.wait_ge(p_m.h, yd_done[r - 2])
                    for q in range(4):
                        v_m.inc(v.tensor_mul(stdb[:, r % 2, q, :], srt[:, q % 2, :],
                                             lmb[:, r % 2, q, :]))
                    sd_done[r] = v_m.n

            if (r % 32 == 0 and r >= 32) or r == NR:
                cprev = r // 32 - 1
                q31 = r - 1

                @blk.tensor
                def _(t, q31=q31):
                    t.wait_ge(v_m.h, sq_done[q31])
                    for _ in range(12):
                        t.matmul(self._pdum[:, :], self.dummyw[:, 0:128],
                                 self.dummyw[:, :], start=True, stop=True)
                    mm = t.matmul(M1[:, 0:CH], self.ones_b[:, :], sqb[:, q31 % 2, :],
                                  start=False, stop=True)
                    p_m.inc(mm)
                    rm_done[q31] = p_m.n

                @blk.vector
                def _(v, cprev=cprev, q31=q31):
                    v.wait_ge(p_m.h, rm_done[q31])
                    if cprev >= 1:
                        v.wait_ge(s_y.h, rs_done[cprev - 1])
                    v_m.inc(v.tensor_scalar(rr[0:1, 0, :], M1[0:1, 0:CH], 1.0 / DI, None, ALU.mult))
                    rr0_done[cprev] = v_m.n

                @blk.scalar
                def _(sc, cprev=cprev):
                    sc.wait_ge(v_m.h, rr0_done[cprev])
                    a_m.inc(sc.activation(rr[0:1, 1, :], rr[0:1, 0, :], AF.Sqrt,
                                          bias=self.eps_col[0:1, 0:1]))
                    rsq_done[cprev] = a_m.n

                @blk.vector
                def _(v, cprev=cprev):
                    v.wait_ge(a_m.h, rsq_done[cprev])
                    v_m.inc(v.reciprocal(rr[0:1, 2, :], rr[0:1, 1, :]))
                    rcp_done[cprev] = v_m.n

                if r < NR:
                    ci2 = r // 32

                    @blk.tensor
                    def _(t, ci2=ci2, cprev=cprev):
                        t.wait_ge(v_m.h, rr0_done[cprev])
                        for s2 in range(2):
                            mm = t.matmul(M1[:, s2 * CH:(s2 + 1) * CH],
                                          self.B_all[:, ci2 * CH + s2 * 128: ci2 * CH + (s2 + 1) * 128],
                                          self.C_all[:, ci2 * CH:(ci2 + 1) * CH],
                                          start=(s2 == 0), stop=(s2 == 1))
                        p_m.inc(mm)
                        sc_done[ci2] = p_m.n

                    @blk.vector
                    def _(v, ci2=ci2, r=r):
                        v.wait_ge(p_m.h, sc_done[ci2])
                        v.tensor_copy(srt[:, 0, :], M1[:, 0:CH])
                        v_m.inc(v.tensor_copy(srt[:, 1, :], M1[:, CH:2 * CH]))
                        sr_done[ci2] = v_m.n

                    @blk.vector
                    def _(v, r=r, ci2=ci2):
                        v.wait_ge(a_m.h, ex_done[r])
                        if r >= 2:
                            v.wait_ge(p_m.h, yd_done[r - 2])
                        for q in range(4):
                            v_m.inc(v.tensor_mul(stdb[:, r % 2, q, :], srt[:, q % 2, :],
                                                 lmb[:, r % 2, q, :]))
                        sd_done[r] = v_m.n

            if bnd:
                @blk.sync
                def _(s, gi=gi):
                    if gi >= 1:
                        s.wait_ge(v_m.h, yn_done[gi * 8 - 1])
                        cis, gs = (gi - 1) // 4, (gi - 1) % 4
                        s_y.inc(s.dma_start(
                            out=self.d_ynorm.rearrange("(f p) t -> p f t", p=128)[
                                :, 8 * gs:8 * gs + 8, cis * CH:(cis + 1) * CH],
                            in_=ygsb[:, (gi - 1) % 2, :, :]), 16)
                        st_done[gi - 1] = s_y.n
                    if gi + 1 < NGR:
                        if gi >= 1:
                            s.wait_ge(p_m.h, yd_done[gi * 8 - 1])
                            s.wait_ge(v_m.h, ptb_done[gi - 1])
                        em_loads(s, gi + 1)
            if (r % 32 == 0 and r >= 32) or r == NR:
                cprev = r // 32 - 1

                @blk.sync
                def _(s, cprev=cprev):
                    s.wait_ge(v_m.h, rcp_done[cprev])
                    s_y.inc(s.dma_start(out=self.d_rstd[0:1, cprev * CH:(cprev + 1) * CH],
                                        in_=rr[0:1, 2, :]), 16)
                    rs_done[cprev] = s_y.n

        @blk.sync
        def _(s):
            s.wait_ge(v_m.h, yn_done[NR - 1])
            s_y.inc(s.dma_start(
                out=self.d_ynorm.rearrange("(f p) t -> p f t", p=128)[:, 24:32, 7 * CH:8 * CH],
                in_=ygsb[:, (NGR - 1) % 2, :, :]), 16)
            s.wait_ge(s_y.h, s_y.n)

    def p5_final(self, blk, ph):
        nc, c = self.nc, self.ctx
        wT = ph.enter_context(nc.sbuf_tensor("owT", [128, KDI, 2048], BF16))
        g_w = [Ctr(nc, c, "o_gw0"), Ctr(nc, c, "o_gw1")]
        p_tr = Ctr(nc, c, "o_ptr")
        v_tr = Ctr(nc, c, "o_vtr")
        with ExitStack() as bctx:
            wpb = bctx.enter_context(nc.sbuf_tensor("wpb", [128, 2, DI], BF16))
            wtr = [bctx.enter_context(nc.psum_tensor(f"o_wt{i}", [128, 1024], BF16)) for i in range(2)]
            for m in range(KD):
                @blk.gpsimd
                def _(g, m=m):
                    if m >= 2:
                        g.wait_ge(p_tr.h, 32 * (m - 1))
                    g_w[m % 2].inc(g.dma_start(out=wpb[:, m % 2, :], in_=self.w_outproj[m * 128:(m + 1) * 128, :]), 16)

                @blk.tensor
                def _(t, m=m):
                    t.wait_ge(g_w[m % 2].h, 16 * (m // 2 + 1))
                    for half in range(4):
                        if 4 * m + half >= 2:
                            t.wait_ge(v_tr.h, 4 * m + half - 1)
                        for kk in range(8):
                            k = half * 8 + kk
                            p_tr.inc(t.transpose(wtr[half % 2][:, kk * 128:(kk + 1) * 128],
                                                 wpb[:, m % 2, k * 128:(k + 1) * 128],
                                                 self.identb[:, :]))

                @blk.scalar
                def _(sc, m=m):
                    for half in range(4):
                        sc.wait_ge(p_tr.h, 32 * m + 8 * (half + 1))
                        src = wtr[half % 2].ap().rearrange("p (k x) -> p k x", k=8)
                        v_tr.inc(sc.activation(wT[:, half * 8:(half + 1) * 8, m * 128:(m + 1) * 128],
                                               src, AF.Copy))
        wT_total = v_tr.n

        yt = ph.enter_context(nc.sbuf_tensor("yt5", [128, 2, KDI, 128], BF16))
        ffb = ph.enter_context(nc.sbuf_tensor("ffb5", [128, 2, KD, 128], BF16))
        ffT = ph.enter_context(nc.sbuf_tensor("ffT5", [128, 2, 2048], BF16))
        ub = ph.enter_context(nc.sbuf_tensor("ub5", [128, 2, 2048], F32))
        rcol = ph.enter_context(nc.sbuf_tensor("rcol5", [128, 2, 1], F32))
        obuf = ph.enter_context(nc.sbuf_tensor("obuf5", [128, 2, 2048], F32))
        pop = [ph.enter_context(nc.psum_tensor(f"o_p{i}", [128, 512], F32)) for i in range(2)]
        pft = [ph.enter_context(nc.psum_tensor(f"o_f{i}", [128, 1024], BF16)) for i in range(2)]
        s_y = Ctr(nc, c, "o_sy")
        g_u = Ctr(nc, c, "o_gu")
        p_mm = Ctr(nc, c, "o_pmm")
        v_s1 = Ctr(nc, c, "o_vs1")
        p_tf = Ctr(nc, c, "o_ptf")
        v_ev = Ctr(nc, c, "o_vev")
        g_ad = Ctr(nc, c, "o_gad")
        s_out = Ctr(nc, c, "o_sout")
        NTB = 16
        for tb in range(NTB):
            sl = tb % 2

            @blk.sync
            def _(s, tb=tb, sl=sl):
                if tb == 0:
                    s.wait_ge(v_tr.h, wT_total)
                if tb >= 2:
                    s.wait_ge(p_mm.h, 4 * (tb - 1))
                    s.wait_ge(p_tf.h, 16 * (tb - 1))
                s_y.inc(s.dma_start(
                    out=yt[:, sl, :, :],
                    in_=self.d_ynorm.rearrange("(f p) t -> p f t", p=128)[:, :, tb * 128:(tb + 1) * 128]), 16)
                s_y.inc(s.dma_start(
                    out=ffb[:, sl, :, :],
                    in_=self.d_ffn.rearrange("(k p) t -> p k t", p=128)[:, :, tb * 128:(tb + 1) * 128]), 16)

            @blk.gpsimd
            def _(g, tb=tb, sl=sl):
                if tb == 0:
                    g.wait_ge(v_tr.h, wT_total)
                if tb >= 2:
                    g.wait_ge(v_s1.h, 4 * (tb - 1))
                g_u.inc(g.dma_start(out=ub[:, sl, :], in_=self.u[3 + tb * 128: 3 + (tb + 1) * 128, :]), 16)
                g_u.inc(g.dma_start(out=rcol[:, sl, :],
                                    in_=self.d_rstd.rearrange("o t -> t o")[tb * 128:(tb + 1) * 128, :],
                                    allow_slow_non_contiguous=True), 16)

            @blk.tensor
            def _(t, tb=tb, sl=sl):
                t.wait_ge(s_y.h, 32 * (tb + 1))
                if tb == 0:
                    t.wait_ge(v_tr.h, wT_total)
                if tb >= 1:
                    t.wait_ge(v_ev.h, 2 * tb)
                for mo in range(KD):
                    p_tf.inc(t.transpose(pft[mo // 8][:, (mo % 8) * 128:(mo % 8 + 1) * 128],
                                         ffb[:, sl, mo, :], self.identb[:, :]))
                for dq in range(4):
                    if 4 * tb + dq >= 2:
                        t.wait_ge(v_s1.h, 4 * tb + dq - 1)
                    for k in range(KDI):
                        mm = t.matmul(pop[dq % 2][:, :], yt[:, sl, k, :],
                                      wT[:, k, dq * 512:(dq + 1) * 512],
                                      start=(k == 0), stop=(k == KDI - 1))
                        if k == KDI - 1:
                            p_mm.inc(mm)

            @blk.vector
            def _(v, tb=tb, sl=sl):
                for dq in range(4):
                    v.wait_ge(p_mm.h, 4 * tb + dq + 1)
                    v.wait_ge(g_u.h, 32 * (tb + 1))
                    if tb >= 2:
                        v.wait_ge(s_out.h, 16 * (tb - 1))
                        v.wait_ge(g_ad.h, 4 * (tb - 1))
                    v_s1.inc(v.scalar_tensor_tensor(obuf[:, sl, dq * 512:(dq + 1) * 512],
                                                    pop[dq % 2][:, :], rcol[:, sl, 0:1],
                                                    ub[:, sl, dq * 512:(dq + 1) * 512],
                                                    ALU.mult, ALU.add))

            @blk.scalar
            def _(sc, tb=tb, sl=sl):
                sc.wait_ge(p_tf.h, 16 * (tb + 1))
                if tb >= 1:
                    sc.wait_ge(g_ad.h, 4 * tb)
                for bk in range(2):
                    v_ev.inc(sc.activation(ffT[:, sl, bk * 1024:(bk + 1) * 1024],
                                           pft[bk][:, :], AF.Copy))

            @blk.vector
            def _(v, tb=tb, sl=sl):
                v.wait_ge(v_ev.h, 2 * (tb + 1))
                for dq in range(4):
                    v.wait_ge(v_s1.h, 4 * tb + dq + 1)
                    g_ad.inc(v.tensor_add(obuf[:, sl, dq * 512:(dq + 1) * 512],
                                          obuf[:, sl, dq * 512:(dq + 1) * 512],
                                          ffT[:, sl, dq * 512:(dq + 1) * 512]))

            @blk.sync
            def _(s, tb=tb, sl=sl):
                s.wait_ge(g_ad.h, 4 * (tb + 1))
                s_out.inc(s.dma_start(out=self.out[tb * 128:(tb + 1) * 128, :], in_=obuf[:, sl, :]), 16)

        @blk.sync
        def _(s):
            s.wait_ge(s_out.h, s_out.n)

    def p_dbg(self, blk):
        nc = self.nc
        sd = Ctr(nc, self.ctx, "dbg_s")

        @blk.sync
        def _(s):
            pairs = [("xn", self.d_xn), ("z", self.d_z), ("xbc", self.d_xbc),
                     ("xtm", self.d_xtm), ("dx", self.d_dx),
                     ("ccout", self.cc_out), ("ynorm", self.d_ynorm), ("ffn", self.d_ffn)]
            for nm, src in pairs:
                sd.inc(s.dma_start(out=self.dbg[nm], in_=src), 16)
            sd.inc(s.dma_start(out=self.dbg["prevloc"],
                               in_=self.d_prev.rearrange("c p x -> (c p) x")
                               if False else self.d_prev[:, :, :]), 16)
            s.wait_ge(sd.h, sd.n)


def _build():
    return B().build()


def kernel(L0_attn_u, L0_attn_in_proj_w, L0_attn_conv_w, L0_attn_dt_bias,
           L0_attn_A_log, L0_attn_D_param, L0_attn_norm_w, L0_attn_out_proj_w,
           L0_ffn_W_router, L0_ffn_We_gate, L0_ffn_We_up, L0_ffn_We_down,
           L0_ffn_Ws_gate, L0_ffn_Ws_up, L0_ffn_Ws_down):
    u = np.ascontiguousarray(np.asarray(L0_attn_u, dtype=np.float32))
    nc = _build()
    in_maps = []
    for k in range(8):
        bi, s = k // 4, k % 4
        if s == 0:
            u_ext = np.concatenate([np.zeros((3, D), np.float32), u[bi, 0:T]], axis=0)
        else:
            u_ext = u[bi, s * T - 3:(s + 1) * T]
        mselT = np.zeros((8, 8), np.float32)
        cbias = np.full((8,), -1e30, np.float32)
        for j in range(8):
            if j // 4 == k // 4 and j < k:
                cbias[j] = 0.0
                for m in range(j + 1, k):
                    mselT[m, j] = 1.0
        in_maps.append({
            "u_ext": np.ascontiguousarray(u_ext),
            "in_proj_w": np.ascontiguousarray(np.asarray(L0_attn_in_proj_w, dtype=np.float32)),
            "conv_w": np.ascontiguousarray(np.asarray(L0_attn_conv_w, dtype=np.float32)),
            "dt_bias": np.ascontiguousarray(np.asarray(L0_attn_dt_bias, dtype=np.float32)),
            "A_log": np.ascontiguousarray(np.asarray(L0_attn_A_log, dtype=np.float32)),
            "D_param": np.ascontiguousarray(np.asarray(L0_attn_D_param, dtype=np.float32)),
            "norm_w": np.ascontiguousarray(np.asarray(L0_attn_norm_w, dtype=np.float32)),
            "out_proj_w": np.ascontiguousarray(np.asarray(L0_attn_out_proj_w, dtype=np.float32)),
            "W_router": np.ascontiguousarray(np.asarray(L0_ffn_W_router, dtype=np.float32)),
            "We_gate": np.ascontiguousarray(np.asarray(L0_ffn_We_gate, dtype=np.float32)),
            "We_up": np.ascontiguousarray(np.asarray(L0_ffn_We_up, dtype=np.float32)),
            "We_down": np.ascontiguousarray(np.asarray(L0_ffn_We_down, dtype=np.float32)),
            "Ws_gate": np.ascontiguousarray(np.asarray(L0_ffn_Ws_gate, dtype=np.float32)),
            "Ws_up": np.ascontiguousarray(np.asarray(L0_ffn_Ws_up, dtype=np.float32)),
            "Ws_down": np.ascontiguousarray(np.asarray(L0_ffn_Ws_down, dtype=np.float32)),
            "mselT": mselT, "cbias": cbias,
        })
    res = run_bass_kernel_spmd(nc, in_maps, core_ids=list(range(8)))
    out = np.empty((2, 8192, D), np.float32)
    for k in range(8):
        bi, s = k // 4, k % 4
        out[bi, s * T:(s + 1) * T] = res.results[k]["out"]
    if _DBG:
        kernel._last = res
    return out


# version 13
# speedup vs baseline: 1.0016x; 1.0010x over previous
import numpy as np
from contextlib import ExitStack

import concourse.bass as bass
import concourse.mybir as mybir
from concourse.bass_utils import run_bass_kernel_spmd

F32 = mybir.dt.float32
BF16 = mybir.dt.bfloat16
I32 = mybir.dt.int32
AF = mybir.ActivationFunctionType
ALU = mybir.AluOpType
AX = mybir.AxisListType

D = 2048
DI = 4096
NH = 64
DST = 128
DXBC = DI + 2 * DST
DPROJ = 2 * DI + 2 * DST + NH
T = 2048
CH = 256
NCH = T // CH
TE = T + 3
EPS = 1e-5
KD = D // 128
KDI = DI // 128
NPAIR = NH // 2
NXBC = DXBC // 128
M_INP = 67
DE = 1408
DS = 2816
KE = DE // 128
KS = DS // 128

_DBG = False


class Ctr:
    def __init__(self, nc, ctx, name):
        self.h = ctx.enter_context(nc.semaphore(name))
        self.n = 0

    def inc(self, instr, k=1):
        instr.then_inc(self.h, k)
        self.n += k
        return instr


class B:
    def __init__(self):
        nc = bass.Bass("TRN2", target_bir_lowering=False, debug=False, detect_race_conditions=True)
        self.nc = nc
        self.ctx = ExitStack()

        def inp(name, shape):
            return nc.dram_tensor(name, shape, F32, kind="ExternalInput").ap()

        self.u = inp("u_ext", [TE, D])
        self.w_inproj = inp("in_proj_w", [DPROJ, D])
        self.w_conv = inp("conv_w", [DXBC, 4])
        self.i_dtb = inp("dt_bias", [NH])
        self.i_alog = inp("A_log", [NH])
        self.i_dpar = inp("D_param", [NH])
        self.i_normw = inp("norm_w", [DI])
        self.w_outproj = inp("out_proj_w", [D, DI])
        self.w_router = inp("W_router", [D, 8])
        self.we_gate = inp("We_gate", [8, D, DE])
        self.we_up = inp("We_up", [8, D, DE])
        self.we_down = inp("We_down", [8, DE, D])
        self.ws_gate = inp("Ws_gate", [D, DS])
        self.ws_up = inp("Ws_up", [D, DS])
        self.ws_down = inp("Ws_down", [DS, D])
        self.i_mselT = inp("mselT", [8, 8])
        self.i_cbias = inp("cbias", [8])

        self.out = nc.dram_tensor("out", [T, D], F32, kind="ExternalOutput").ap()

        def scr(name, shape, dt=BF16):
            return nc.dram_tensor(name, shape, dt, kind="Internal").ap()

        self.d_xn = scr("d_xn", [D, TE + 1])
        self.d_z = scr("d_z", [DI, T])
        self.d_xbc = scr("d_xbc", [DXBC, TE + 1])
        self.d_xtm = scr("d_xtm", [T, DI])
        self.d_dx = scr("d_dx", [DI, T])
        self.d_prev = scr("d_prev", [NCH, 128, NPAIR * DST])
        self.d_ynorm = scr("d_ynorm", [DI, T])
        self.d_rstd = scr("d_rstd", [1, T], F32)
        self.d_ffn = scr("d_ffn", [D, T])
        self.cc_in = scr("cc_in", [128, NPAIR * DST + 1], F32)
        self.cc_out = scr("cc_out", [8 * 128, NPAIR * DST + 1], F32)
        if _DBG:
            self.dbg = {}
            for nm, shp, dt in [("xn", [D, TE + 1], BF16), ("z", [DI, T], BF16),
                                ("xbc", [DXBC, TE + 1], BF16), ("dtraw", [NH, T], F32),
                                ("combine", [8, T], F32), ("xtm", [T, DI], BF16),
                                ("dx", [DI, T], BF16), ("prevloc", [NCH, 128, NPAIR * DST], BF16),
                                ("ccout", [8 * 128, NPAIR * DST + 1], F32),
                                ("csall", [NH, T], F32), ("dtall", [NH, T], F32),
                                ("ynorm", [DI, T], BF16), ("ffn", [D, T], BF16),
                                ("gtm", [128, NCH * 2 * NH], F32),
                                ("cdcol", [128, NCH * NPAIR], F32),
                                ("dcol", [128, NCH * NPAIR], F32)]:
                self.dbg[nm] = nc.dram_tensor("dbg_" + nm, shp, dt, kind="ExternalOutput").ap()

        c = self.ctx
        self._mamba_ctx = ExitStack()
        self._tmp_ctx = ExitStack()
        mc = self._mamba_ctx
        tc = self._tmp_ctx

        def sb(name, shape, dt=F32):
            return c.enter_context(nc.sbuf_tensor(name, shape, dt))

        def sbm(name, shape, dt=F32):
            return mc.enter_context(nc.sbuf_tensor(name, shape, dt))

        def sbt(name, shape, dt=F32):
            return tc.enter_context(nc.sbuf_tensor(name, shape, dt))

        self.identb = sb("identb", [128, 128], BF16)
        self.identf = sb("identf", [128, 128], F32)
        self.ones_b = sb("ones_b", [128, 128], BF16)
        self.ones_f = sb("ones_f", [64, 128], F32)
        self.sel8 = sb("sel8", [8, 8, 128], F32)
        self.Eev = sb("Eev", [64, 32], F32)
        self.Eod = sb("Eod", [64, 32], F32)
        self.dtb_col = sb("dtb_col", [NH, 1], F32)
        self.A_col = sb("A_col", [NH, 1], F32)
        self.mselT_sb = sb("mselT_sb", [8, 8], F32)
        self.cbias_col = sb("cbias_col", [8, 1], F32)
        self.eps_col = sb("eps_col", [128, 1], F32)
        self.dummyw = sb("dummyw", [128, 512], BF16)
        self.maskT = sbm("maskT", [128, 2, CH], F32)
        self.normw = sbm("normw", [128, KDI], F32)
        self.cs_hi = sbm("cs_hi", [NH, T], BF16)
        self.cs_lo = sbm("cs_lo", [NH, T], BF16)
        self.ecs16 = sbm("ecs16", [NH, T], BF16)
        self.csldt = sbm("csldt", [128, NCH, 2, NH], F32)
        self.B_all = sbm("B_all", [128, T], BF16)
        self.C_all = sbm("C_all", [128, T], BF16)
        self.cdcol = sbm("cdcol", [128, NCH, NPAIR], F32)
        self.Dcol = sbm("Dcol", [128, NCH, NPAIR], F32)
        self.logP = sbm("logP", [NH, 1], F32)

        self.dt_all = sbt("dt_all", [NH, T], F32)
        self.cs_all = sbt("cs_all", [NH, T], F32)
        self.combine = sbt("combine", [8, T], F32)
        self.utri = sbt("utri", [128, 2, CH], F32)
        self.dt_raw = sbt("dt_raw", [NH, T], F32)
        self._prev_ctx = ExitStack()
        self.prev = self._prev_ctx.enter_context(nc.sbuf_tensor("prev", [128, NPAIR * DST], F32))
        self.g_tm = self._prev_ctx.enter_context(nc.sbuf_tensor("g_tm", [128, NCH, 2, NH], F32))
        self.D_pair = self._prev_ctx.enter_context(nc.sbuf_tensor("D_pair", [128, NPAIR], F32))
        self.wcv = self._prev_ctx.enter_context(nc.sbuf_tensor("wcv", [128, NXBC, 4], F32))

        self.bar = Ctr(nc, c, "bar")
        self.xT = Ctr(nc, c, "xT")
        self.xV = Ctr(nc, c, "xV")
        self.xA = Ctr(nc, c, "xA")
        self.xS = Ctr(nc, c, "xS")
        self.nbar = 0

    def barrier(self, blk):
        self.nbar += 1
        target = 5 * self.nbar
        for name in ("tensor", "vector", "scalar", "sync", "gpsimd"):
            @getattr(blk, name)
            def _(eng, target=target):
                eng.drain().then_inc(self.bar.h, 1)
                eng.wait_ge(self.bar.h, target)

    def warm(self, blk, ph, name, n=16):
        nc = self.nc
        pd = ph.enter_context(nc.psum_tensor(name, [128, 512], F32))
        self._pdum = pd

        @blk.tensor
        def _(t):
            for _ in range(n):
                t.matmul(pd[:, :], self.dummyw[:, 0:128], self.dummyw[:, :],
                         start=True, stop=True)
        return pd

    def build(self):
        nc = self.nc
        with self.ctx, nc.Block() as blk:
            self.p_init(blk)
            self._init_ctx.close()
            self.barrier(blk)
            self.p0_ln(blk)
            self.barrier(blk)
            with ExitStack() as ph:
                self.p1_inproj(blk, ph)
            self.barrier(blk)
            with ExitStack() as ph:
                self.p3a_router(blk, ph)
            self.barrier(blk)
            self.free_xn()
            with ExitStack() as ph:
                self.p2_stats(blk, ph)
            self.barrier(blk)
            with ExitStack() as ph:
                self.p2_main(blk, ph)
            self.barrier(blk)
            self._prev_ctx.close()
            with ExitStack() as ph:
                self.p3b_ffn(blk, ph)
            self.barrier(blk)
            self._tmp_ctx.close()
            with ExitStack() as ph:
                self.p4_hin(blk, ph)
            self.hin_sem = self.xV
            self.hin_total = self.xV.n
            self.barrier(blk)
            with ExitStack() as ph:
                self.p4_mid(blk, ph)
            self.barrier(blk)
            self.free_mamba()
            with ExitStack() as ph:
                self.p5_final(blk, ph)
            if _DBG:
                self.barrier(blk)
                self.p_dbg(blk)
        return nc

    def free_xn(self):
        self._xn_ctx.close()

    def free_mamba(self):
        self._p4_ctx.close()
        self._mamba_ctx.close()

    def p_init(self, blk):
        nc, c = self.nc, self.ctx
        self._init_ctx = ExitStack()
        ic = self._init_ctx
        it128 = ic.enter_context(nc.sbuf_tensor("it128", [128, 128], I32))
        it256 = ic.enter_context(nc.sbuf_tensor("it256", [128, 2, CH], I32))
        itE = ic.enter_context(nc.sbuf_tensor("itE", [64, 2, 32], I32))
        nst = ic.enter_context(nc.sbuf_tensor("normw_st", [KDI, 128], F32))
        rowD = ic.enter_context(nc.sbuf_tensor("rowD", [1, NH], F32))
        ps_a = ic.enter_context(nc.psum_tensor("ips_a", [128, 128], F32))
        ps_b = ic.enter_context(nc.psum_tensor("ips_b", [128, 128], F32))
        g0 = Ctr(nc, c, "i_g0")
        gi = Ctr(nc, c, "i_gi")
        v0 = self.xV
        p0 = self.xT
        a0 = self.xA

        @blk.gpsimd
        def _(g):
            gi.inc(g.iota(it128[:, :], pattern=[[1, 128]], base=0, channel_multiplier=-1))
            gi.inc(g.iota(it256[:, 0, :], pattern=[[1, CH]], base=0, channel_multiplier=-1))
            gi.inc(g.iota(it256[:, 1, :], pattern=[[1, CH]], base=-128, channel_multiplier=-1))
            gi.inc(g.iota(itE[:, 0, :], pattern=[[-2, 32]], base=0, channel_multiplier=1))
            gi.inc(g.iota(itE[:, 1, :], pattern=[[-2, 32]], base=-1, channel_multiplier=1))
            g0.inc(g.dma_start(out=self.dtb_col[:, :], in_=self.i_dtb[:, None]), 16)
            g0.inc(g.dma_start(out=self.A_col[:, :], in_=self.i_alog[:, None]), 16)
            g0.inc(g.dma_start(out=rowD[0:1, :], in_=self.i_dpar[None, :]), 16)
            g0.inc(g.dma_start(out=nst[:, :], in_=self.i_normw.rearrange("(k p) -> k p", p=128)), 16)
            g0.inc(g.dma_start(out=self.wcv[:, :, :], in_=self.w_conv.rearrange("(f p) k -> p f k", p=128)), 16)
            g0.inc(g.dma_start(out=self.mselT_sb[:, :], in_=self.i_mselT[:, :]), 16)
            g0.inc(g.dma_start(out=self.cbias_col[:, :], in_=self.i_cbias[:, None]), 16)

        @blk.vector
        def _(v):
            v.wait_ge(g0.h, g0.n)
            v.wait_ge(gi.h, gi.n)
            v.memset(self.eps_col[:, :], EPS)
            v.memset(self.dummyw[:, :], 0.0)
            v.tensor_single_scalar(self.identb[:, :], it128[:, :], 0.0, ALU.is_equal)
            v.tensor_single_scalar(self.identf[:, :], it128[:, :], 0.0, ALU.is_equal)
            v.memset(self.ones_b[:, :], 1.0)
            v.memset(self.ones_f[:, :], 1.0)
            v.tensor_single_scalar(self.Eev[:, :], itE[:, 0, :], 0.0, ALU.is_equal)
            v.tensor_single_scalar(self.Eod[:, :], itE[:, 1, :], 0.0, ALU.is_equal)
            for s in range(2):
                v.tensor_single_scalar(self.utri[:, s, :], it256[:, s, :], 0.0, ALU.is_ge)
                v0.inc(v.tensor_single_scalar(self.maskT[:, s, :], it256[:, s, :], 0.0, ALU.is_lt))
            v.wait_ge(v0.h, v0.n)
            v0.inc(v.tensor_scalar_mul(self.maskT[:, :, :], self.maskT[:, :, :], -1e30))
            v.wait_ge(v0.h, v0.n)
            for e in range(8):
                v.tensor_scalar(self.sel8[:, e, :], self.ones_f[0:8, :], self.identf[0:8, e:e + 1], None, ALU.mult)
            v0.inc(v.memset(self.prev[:, :], 0.0))
            v0.inc(v.memset(self.logP[:, :], 0.0))

        @blk.scalar
        def _(sc):
            sc.wait_ge(g0.h, g0.n)
            a0.inc(sc.activation(self.A_col[:, :], self.A_col[:, :], AF.Exp))
            sc.wait_ge(a0.h, a0.n)
            a0.inc(sc.mul(self.A_col[:, :], self.A_col[:, :], -1.0))

        @blk.tensor
        def _(t):
            t.wait_ge(v0.h, v0.n)
            t.wait_ge(g0.h, g0.n)
            p0.inc(t.transpose(ps_a[:, 0:KDI], nst[:, :], self.identf[0:KDI, 0:KDI]))
            p0.inc(t.matmul(ps_b[0:64, 0:NPAIR], self.ones_f[0:1, 0:64], rowD[0:1, 0::2], start=True, stop=True))
            p0.inc(t.matmul(ps_b[64:128, 0:NPAIR], self.ones_f[0:1, 0:64], rowD[0:1, 1::2], start=True, stop=True))

        @blk.vector
        def _(v):
            v.wait_ge(p0.h, p0.n)
            v.tensor_copy(self.normw[:, :], ps_a[:, 0:KDI])
            v.tensor_copy(self.D_pair[:, :], ps_b[:, 0:NPAIR])

    def p0_ln(self, blk):
        nc, c = self.nc, self.ctx
        self._xn_ctx = ExitStack()
        self.xn = self._xn_ctx.enter_context(nc.sbuf_tensor("xn", [128, KD, TE + 1], BF16))
        ph = ExitStack()
        u_tm = ph.enter_context(nc.sbuf_tensor("u_tm", [128, 2, D], F32))
        xn_tm = ph.enter_context(nc.sbuf_tensor("xn_tm", [128, 2, D], BF16))
        lncol = ph.enter_context(nc.sbuf_tensor("lncol", [128, 2, 4], F32))
        sq = ph.enter_context(nc.sbuf_tensor("sq", [128, D], F32))
        banks = [ph.enter_context(nc.psum_tensor(f"p0_t{i}", [128, 1024], BF16)) for i in range(2)]
        s_ld = [Ctr(nc, c, "p0_sld0"), Ctr(nc, c, "p0_sld1")]
        v_var = Ctr(nc, c, "p0_vvar")
        a_sq = self.xA
        v_xn = Ctr(nc, c, "p0_vxn")
        p_tr = Ctr(nc, c, "p0_ptr")
        v_ev = Ctr(nc, c, "p0_vev")
        s_st = self.xS
        NT = 17
        with ph:
            for i in range(NT):
                nr = 128 if i < 16 else 3
                b = i % 2

                @blk.sync
                def _(s, i=i, nr=nr, b=b):
                    if i >= 2:
                        s.wait_ge(v_xn.h, i - 1)
                    s_ld[b].inc(s.dma_start(out=u_tm[0:nr, b, :], in_=self.u[i * 128:i * 128 + nr, :]), 16)

                @blk.vector
                def _(v, i=i, nr=nr, b=b):
                    v.wait_ge(s_ld[b].h, 16 * (i // 2 + 1))
                    v_var.inc(v.tensor_mul(sq[0:nr, :], u_tm[0:nr, b, :], u_tm[0:nr, b, :]))
                    v.reduce_sum(lncol[0:nr, b, 0:1], u_tm[0:nr, b, :], axis=AX.X)
                    v.wait_ge(v_var.h, v_var.n)
                    v_var.inc(v.reduce_sum(lncol[0:nr, b, 1:2], sq[0:nr, :], axis=AX.X))
                    v.wait_ge(v_var.h, v_var.n)
                    v.tensor_scalar(lncol[0:nr, b, 0:1], lncol[0:nr, b, 0:1], 1.0 / D, None, ALU.mult)
                    v_var.inc(v.tensor_scalar(lncol[0:nr, b, 1:2], lncol[0:nr, b, 1:2], 1.0 / D, None, ALU.mult))
                    v.wait_ge(v_var.h, v_var.n)
                    v_var.inc(v.tensor_mul(lncol[0:nr, b, 2:3], lncol[0:nr, b, 0:1], lncol[0:nr, b, 0:1]))
                    v.wait_ge(v_var.h, v_var.n)
                    v_var.inc(v.tensor_sub(lncol[0:nr, b, 1:2], lncol[0:nr, b, 1:2], lncol[0:nr, b, 2:3]))

                @blk.scalar
                def _(sc, i=i, nr=nr, b=b):
                    sc.wait_ge(v_var.h, v_var.n)
                    a_sq.inc(sc.activation(lncol[0:nr, b, 2:3], lncol[0:nr, b, 1:2], AF.Sqrt, bias=self.eps_col[0:nr, 0:1]))

                @blk.vector
                def _(v, i=i, nr=nr, b=b):
                    v.wait_ge(a_sq.h, a_sq.n)
                    v_var.inc(v.reciprocal(lncol[0:nr, b, 3:4], lncol[0:nr, b, 2:3]))
                    v.wait_ge(v_var.h, v_var.n)
                    v_xn.inc(v.tensor_scalar(xn_tm[0:nr, b, :], u_tm[0:nr, b, :],
                                             lncol[0:nr, b, 0:1], lncol[0:nr, b, 3:4],
                                             ALU.subtract, ALU.mult))

                @blk.tensor
                def _(t, i=i, nr=nr, b=b):
                    t.wait_ge(v_xn.h, v_xn.n)
                    for half in range(2):
                        if i >= 1:
                            t.wait_ge(v_ev.h, 2 * (i - 1) + half + 1)
                        for kk in range(8):
                            k = half * 8 + kk
                            p_tr.inc(t.transpose(banks[half][:, kk * 128:kk * 128 + nr],
                                                 xn_tm[0:nr, b, k * 128:(k + 1) * 128],
                                                 self.identb[0:nr, 0:nr]))

                @blk.vector
                def _(v, i=i, nr=nr):
                    r0 = i * 128
                    for half in range(2):
                        v.wait_ge(p_tr.h, 16 * i + 8 * (half + 1))
                        src = banks[half].ap().rearrange("p (k x) -> p k x", k=8)[:, :, 0:nr]
                        v_ev.inc(v.tensor_copy(self.xn[:, half * 8:(half + 1) * 8, r0:r0 + nr], src))

            @blk.sync
            def _(s):
                s.wait_ge(v_ev.h, v_ev.n)
                for k in range(KD):
                    s_st.inc(s.dma_start(out=self.d_xn[k * 128:(k + 1) * 128, 0:TE], in_=self.xn[:, k, 0:TE]), 16)
                s.wait_ge(s_st.h, s_st.n)

    def p1_inproj(self, blk, ph):
        nc, c = self.nc, self.ctx
        wblk = ph.enter_context(nc.sbuf_tensor("wblk", [128, 2, D], BF16))
        wT = ph.enter_context(nc.sbuf_tensor("wT", [128, 2, KD, 128], BF16))
        obuf = ph.enter_context(nc.sbuf_tensor("obuf", [128, 2, 4, 512], BF16))
        hbuf = ph.enter_context(nc.sbuf_tensor("hbuf", [128, 8], BF16))
        trb = [ph.enter_context(nc.psum_tensor(f"p1_tr{i}", [128, 1024], BF16)) for i in range(2)]
        pm = [ph.enter_context(nc.psum_tensor(f"p1_m{t}", [128, 512], F32)) for t in range(4)]
        phh = ph.enter_context(nc.psum_tensor("p1_h", [128, 8], F32))
        g_w = [Ctr(nc, c, "p1_gw0"), Ctr(nc, c, "p1_gw1")]
        p_tr = self.xT
        v_wtr = Ctr(nc, c, "p1_vwtr")
        p_mm = self.xT
        v_oe = self.xV
        s_st = self.xS
        p_hm = self.xT
        v_he = self.xV
        s_he = self.xS
        tr_done = []
        mm_done = []
        oe_done = []
        st_done = []
        he_done = []

        for m in range(M_INP):
            nr = 64 if m == M_INP - 1 else 128
            b = m % 2
            xbc = 32 <= m < 66

            @blk.gpsimd
            def _(g, m=m, nr=nr, b=b):
                if m >= 2:
                    g.wait_ge(p_tr.h, tr_done[m - 2])
                g_w[b].inc(g.dma_start(out=wblk[0:nr, b, :], in_=self.w_inproj[m * 128:m * 128 + nr, :]), 16)

            @blk.tensor
            def _(t, m=m, nr=nr, b=b):
                t.wait_ge(g_w[b].h, 16 * (m // 2 + 1))
                for half in range(2):
                    if m >= 1:
                        t.wait_ge(v_wtr.h, 2 * (m - 1) + half + 1)
                    for kk in range(8):
                        k = half * 8 + kk
                        p_tr.inc(t.transpose(trb[half][:, kk * 128:kk * 128 + nr],
                                             wblk[0:nr, b, k * 128:(k + 1) * 128],
                                             self.identb[0:nr, 0:nr]))
                tr_done.append(p_tr.n)

            @blk.vector
            def _(v, m=m, nr=nr, b=b):
                for half in range(2):
                    v.wait_ge(p_tr.h, tr_done[m] - 8 * (1 - half))
                    if m >= 2:
                        v.wait_ge(p_mm.h, mm_done[m - 1])
                    src = trb[half].ap().rearrange("p (k x) -> p k x", k=8)[:, :, 0:nr]
                    v_wtr.inc(v.tensor_copy(wT[:, b, half * 8:(half + 1) * 8, 0:nr], src))

            @blk.tensor
            def _(t, m=m, nr=nr, b=b, xbc=xbc):
                t.wait_ge(v_wtr.h, 2 * (m + 1))
                if m >= 1:
                    t.wait_ge(v_oe.h, oe_done[m - 1])
                for tt in range(4):
                    for k in range(KD):
                        mm = t.matmul(pm[tt][0:nr, :], wT[:, b, k, 0:nr],
                                      self.xn[:, k, 3 + tt * 512: 3 + (tt + 1) * 512],
                                      start=(k == 0), stop=(k == KD - 1))
                        if k == KD - 1:
                            p_mm.inc(mm)
                if xbc:
                    if m >= 33:
                        t.wait_ge(v_he.h, he_done[-1][0])
                    for k in range(KD):
                        mm = t.matmul(phh[0:nr, :], wT[:, b, k, 0:nr], self.xn[:, k, 0:8],
                                      start=(k == 0), stop=(k == KD - 1))
                        if k == KD - 1:
                            p_hm.inc(mm)
                mm_done.append(p_mm.n)

            if m < 32:
                @blk.scalar
                def _(sc, m=m, nr=nr, b=b):
                    for tt in range(4):
                        sc.wait_ge(p_mm.h, mm_done[m] - (3 - tt))
                        if m >= 2:
                            sc.wait_ge(s_st.h, st_done[m - 1])
                        v_oe.inc(sc.activation(obuf[0:nr, b, tt, :], pm[tt][0:nr, :], AF.Silu))

            @blk.vector
            def _(v, m=m, nr=nr, b=b, xbc=xbc):
                for tt in range(4):
                    if m < 32:
                        break
                    v.wait_ge(p_mm.h, mm_done[m] - (3 - tt))
                    if m >= 2:
                        v.wait_ge(s_st.h, st_done[m - 1])
                    if m == M_INP - 1:
                        v_oe.inc(v.tensor_copy(self.dt_raw[:, tt * 512:(tt + 1) * 512], pm[tt][0:64, :]))
                    else:
                        v_oe.inc(v.tensor_copy(obuf[0:nr, b, tt, :], pm[tt][0:nr, :]))
                if xbc:
                    v.wait_ge(p_hm.h, p_hm.n)
                    if m >= 34:
                        v.wait_ge(s_he.h, s_he.n)
                    v_he.inc(v.tensor_copy(hbuf[0:nr, :], phh[0:nr, :]))
                oe_done.append(v_oe.n)

            @blk.sync
            def _(s, m=m, nr=nr, b=b, xbc=xbc):
                if m < 66:
                    for tt in range(4):
                        s.wait_ge(v_oe.h, oe_done[m] - (3 - tt))
                        if m < 32:
                            dst = self.d_z[m * 128:(m + 1) * 128, tt * 512:(tt + 1) * 512]
                        else:
                            r = (m - 32) * 128
                            dst = self.d_xbc[r:r + 128, 3 + tt * 512: 3 + (tt + 1) * 512]
                        s_st.inc(s.dma_start(out=dst, in_=obuf[0:nr, b, tt, :]), 16)
                if xbc:
                    s.wait_ge(v_he.h, v_he.n)
                    r = (m - 32) * 128
                    s_he.inc(s.dma_start(out=self.d_xbc[r:r + 128, 0:3], in_=hbuf[0:128, 0:3]), 16)
                    he_done.append((v_he.n, s_he.n))
                st_done.append(s_st.n)

        @blk.sync
        def _(s):
            s.wait_ge(s_st.h, s_st.n)
            s.wait_ge(s_he.h, s_he.n)

    def p3a_router(self, blk, ph):
        nc, c = self.nc, self.ctx
        wr = ph.enter_context(nc.sbuf_tensor("wr", [128, KD, 8], BF16))
        cmb = ph.enter_context(nc.sbuf_tensor("cmb", [128, 2, 8], F32))
        rcol = ph.enter_context(nc.sbuf_tensor("rcol", [128, 2, 4], F32))
        esb = ph.enter_context(nc.sbuf_tensor("esb", [128, 2, 8], F32))
        ps_r = ph.enter_context(nc.psum_tensor("p3a_r", [128, 8], F32))
        ps_c = ph.enter_context(nc.psum_tensor("p3a_c", [8, 128], F32))
        g_w = Ctr(nc, c, "r_gw")
        p_r = self.xT
        v_s = self.xV
        a_e = self.xA
        v_c = self.xV
        p_t = self.xT
        v_o = self.xV
        sub_done = []
        cmb_done = []
        evo_done = []

        @blk.gpsimd
        def _(g):
            g_w.inc(g.dma_start(out=wr[:, :, :], in_=self.w_router.rearrange("(k p) e -> p k e", p=128)), 16)

        for i in range(KD):
            b = i % 2

            @blk.tensor
            def _(t, i=i):
                if i == 0:
                    t.wait_ge(g_w.h, 16)
                else:
                    t.wait_ge(v_s.h, sub_done[i - 1])
                for k in range(KD):
                    mm = t.matmul(ps_r[:, :], self.xn[:, k, 3 + i * 128: 3 + (i + 1) * 128],
                                  wr[:, k, :], start=(k == 0), stop=(k == KD - 1))
                    if k == KD - 1:
                        p_r.inc(mm)

            @blk.vector
            def _(v, i=i, b=b):
                v.wait_ge(p_r.h, p_r.n)
                v_s.inc(v.reduce_max(rcol[:, b, 0:1], ps_r[:, :], axis=AX.X))
                v.wait_ge(v_s.h, v_s.n)
                v_s.inc(v.tensor_scalar(esb[:, b, :], ps_r[:, :], rcol[:, b, 0:1], None, ALU.subtract))
                sub_done.append(v_s.n)

            @blk.scalar
            def _(sc, i=i, b=b):
                sc.wait_ge(v_s.h, sub_done[i])
                a_e.inc(sc.activation(esb[:, b, :], esb[:, b, :], AF.Exp))

            @blk.vector
            def _(v, i=i, b=b):
                v.wait_ge(a_e.h, a_e.n)
                v_s.inc(v.reduce_max(rcol[:, b, 0:1], esb[:, b, :], axis=AX.X))
                v_s.inc(v.reduce_sum(rcol[:, b, 1:2], esb[:, b, :], axis=AX.X))
                v.wait_ge(v_s.h, v_s.n)
                v_s.inc(v.reciprocal(rcol[:, b, 1:2], rcol[:, b, 1:2]))
                v.wait_ge(v_s.h, v_s.n)
                if i >= 2:
                    v.wait_ge(p_t.h, p_t.n - 1)
                v_s.inc(v.tensor_scalar(cmb[:, b, :], esb[:, b, :], rcol[:, b, 0:1], None, ALU.is_ge))
                v.wait_ge(v_s.h, v_s.n)
                v_s.inc(v.scalar_tensor_tensor(cmb[:, b, :], cmb[:, b, :], -1e30, esb[:, b, :], ALU.mult, ALU.add))
                v.wait_ge(v_s.h, v_s.n)
                v_s.inc(v.reduce_max(rcol[:, b, 2:3], cmb[:, b, :], axis=AX.X))
                v.wait_ge(v_s.h, v_s.n)
                v_s.inc(v.tensor_scalar(cmb[:, b, :], esb[:, b, :], rcol[:, b, 2:3], None, ALU.is_ge))
                v.wait_ge(v_s.h, v_s.n)
                v_c.inc(v.scalar_tensor_tensor(cmb[:, b, :], esb[:, b, :], rcol[:, b, 1:2], cmb[:, b, :], ALU.mult, ALU.mult))
                cmb_done.append(v_c.n)

            @blk.tensor
            def _(t, i=i, b=b):
                t.wait_ge(v_c.h, cmb_done[i])
                if i >= 1:
                    t.wait_ge(v_o.h, evo_done[i - 1])
                p_t.inc(t.transpose(ps_c[:, :], cmb[:, b, :], self.identf[:, :]))

            @blk.vector
            def _(v, i=i):
                v.wait_ge(p_t.h, p_t.n)
                v_o.inc(v.tensor_copy(self.combine[0:8, i * 128:(i + 1) * 128], ps_c[0:8, :]))
                evo_done.append(v_o.n)

    def p2_stats(self, blk, ph):
        nc, c = self.nc, self.ctx
        st = ph.enter_context(nc.sbuf_tensor("st", [NH, 5, T], F32))
        datm = ph.enter_context(nc.sbuf_tensor("datm", [128, KD, 64], F32))
        cslast = ph.enter_context(nc.sbuf_tensor("cslast", [NH, NCH], F32))
        logD = ph.enter_context(nc.sbuf_tensor("logD", [NH, NCH], F32))
        ecl = ph.enter_context(nc.sbuf_tensor("ecl", [NH, NCH], F32))
        eD = ph.enter_context(nc.sbuf_tensor("eD", [NH, NCH], F32))
        vb = ph.enter_context(nc.sbuf_tensor("vb", [64, 64], F32))
        pf = [ph.enter_context(nc.psum_tensor(f"p2s_f{i}", [128, 512], F32)) for i in range(2)]
        pc = ph.enter_context(nc.psum_tensor("p2s_c", [64, 256], F32))
        pq = ph.enter_context(nc.psum_tensor("p2s_q", [128, 32], F32))
        vx = self.xV
        ax = self.xA
        px = self.xT

        @blk.vector
        def _(v):
            vx.inc(v.tensor_scalar(st[:, 0, :], self.dt_raw[:, :], self.dtb_col[:, 0:1], None, ALU.add))

        @blk.scalar
        def _(sc):
            sc.wait_ge(vx.h, vx.n)
            ax.inc(sc.activation(st[:, 1, :], st[:, 0, :], AF.Abs))
            sc.wait_ge(ax.h, ax.n)
            ax.inc(sc.activation(st[:, 1, :], st[:, 1, :], AF.Exp, scale=-1.0))
            sc.wait_ge(ax.h, ax.n)
            ax.inc(sc.activation(st[:, 1, :], st[:, 1, :], AF.Ln, bias=1.0))

        @blk.vector
        def _(v):
            v.wait_ge(ax.h, ax.n)
            vx.inc(v.tensor_scalar_max(st[:, 2, :], st[:, 0, :], 0.0))
            v.wait_ge(vx.h, vx.n)
            vx.inc(v.tensor_add(self.dt_all[:, :], st[:, 2, :], st[:, 1, :]))
            v.wait_ge(vx.h, vx.n)
            vx.inc(v.tensor_scalar(st[:, 0, :], self.dt_all[:, :], self.A_col[:, 0:1], None, ALU.mult))

        @blk.scalar
        def _(sc):
            sc.wait_ge(vx.h, vx.n)
            ax.inc(sc.activation(st[:, 3, :], self.dt_all[:, :], AF.Ln))

        @blk.tensor
        def _(t):
            t.wait_ge(vx.h, vx.n)
            for k in range(KD):
                px.inc(t.transpose(pf[k // 8][:, (k % 8) * 64:(k % 8) * 64 + 64],
                                   st[0:64, 0, k * 128:(k + 1) * 128], self.identf[0:64, 0:64]))

        @blk.vector
        def _(v):
            v.wait_ge(px.h, px.n)
            for bk in range(2):
                src = pf[bk].ap().rearrange("p (k x) -> p k x", k=8)
                vx.inc(v.tensor_copy(datm[:, bk * 8:(bk + 1) * 8, :], src))

        cs_ev = []
        for ci in range(NCH):
            @blk.tensor
            def _(t, ci=ci):
                if ci == 0:
                    t.wait_ge(vx.h, vx.n)
                else:
                    t.wait_ge(vx.h, cs_ev[ci - 1])
                for s in range(2):
                    mm = t.matmul(pc[0:64, :], datm[:, 2 * ci + s, :], self.utri[:, s, :],
                                  start=(s == 0), stop=(s == 1))
                    if s == 1:
                        px.inc(mm)

            @blk.vector
            def _(v, ci=ci):
                v.wait_ge(px.h, px.n)
                vx.inc(v.tensor_copy(self.cs_all[:, ci * CH:(ci + 1) * CH], pc[0:64, :]))
                cs_ev.append(vx.n)

        @blk.vector
        def _(v):
            v.wait_ge(vx.h, vx.n)
            vx.inc(v.tensor_copy(self.cs_hi[:, :], self.cs_all[:, :]))
            v.wait_ge(vx.h, vx.n)
            vx.inc(v.tensor_sub(self.cs_lo[:, :], self.cs_all[:, :], self.cs_hi[:, :]))
            vx.inc(v.tensor_copy(cslast[:, :], self.cs_all[:, CH - 1::CH]))
            v.wait_ge(vx.h, vx.n)
            vx.inc(v.reduce_sum(self.logP[:, 0:1], cslast[:, :], axis=AX.X))
            vx.inc(v.memset(logD[:, 0:1], 0.0))
            v.wait_ge(vx.h, vx.n)
            for ci in range(1, NCH):
                vx.inc(v.tensor_add(logD[:, ci:ci + 1], logD[:, ci - 1:ci], cslast[:, ci - 1:ci]))
                v.wait_ge(vx.h, vx.n)
            v.wait_ge(ax.h, ax.n)
            vx.inc(v.tensor_sub(st[:, 3, :], self.cs_all[:, :], st[:, 3, :]))
            for ci in range(NCH):
                vx.inc(v.tensor_scalar(st[:, 4, ci * CH:(ci + 1) * CH],
                                       self.cs_all[:, ci * CH:(ci + 1) * CH],
                                       cslast[:, ci:ci + 1], None, ALU.subtract))

        @blk.scalar
        def _(sc):
            sc.wait_ge(vx.h, vx.n)
            sc.activation(self.ecs16[:, :], self.cs_all[:, :], AF.Exp)
            sc.activation(st[:, 4, :], st[:, 4, :], AF.Exp, scale=-1.0)
            sc.activation(ecl[:, :], cslast[:, :], AF.Exp)
            ax.inc(sc.activation(eD[:, :], logD[:, :], AF.Exp))

        @blk.vector
        def _(v):
            v.wait_ge(ax.h, ax.n)
            vx.inc(v.tensor_mul(st[:, 4, :], self.dt_all[:, :], st[:, 4, :]))

        tr_ev = []
        for ci in range(NCH):
            @blk.tensor
            def _(t, ci=ci):
                if ci == 0:
                    t.wait_ge(vx.h, vx.n)
                elif ci >= 2:
                    t.wait_ge(vx.h, tr_ev[ci - 2])
                for s in range(2):
                    px.inc(t.transpose(pf[ci % 2][:, s * 64:s * 64 + 64],
                                       st[0:64, 3, ci * CH + s * 128: ci * CH + (s + 1) * 128],
                                       self.identf[0:64, 0:64]))
                    px.inc(t.transpose(pf[ci % 2][:, 128 + s * 64: 128 + s * 64 + 64],
                                       st[0:64, 4, ci * CH + s * 128: ci * CH + (s + 1) * 128],
                                       self.identf[0:64, 0:64]))

            @blk.vector
            def _(v, ci=ci):
                v.wait_ge(px.h, px.n)
                src = pf[ci % 2].ap().rearrange("p (s x) -> p s x", x=64)
                vx.inc(v.tensor_copy(self.csldt[:, ci, :, :], src[:, 0:2, :]))
                vx.inc(v.tensor_copy(self.g_tm[:, ci, :, :], src[:, 2:4, :]))
                tr_ev.append(vx.n)

        for ci in range(NCH):
            for which in range(2):
                srccol = ecl if which == 0 else eD
                dst = self.cdcol if which == 0 else self.Dcol

                @blk.vector
                def _(v, ci=ci, srccol=srccol):
                    v.wait_ge(vx.h, vx.n)
                    v.wait_ge(px.h, px.n)
                    vx.inc(v.tensor_scalar(vb[:, :], self.ones_f[:, 0:64], srccol[:, ci:ci + 1], None, ALU.mult))

                @blk.tensor
                def _(t):
                    t.wait_ge(vx.h, vx.n)
                    t.matmul(pq[0:64, :], vb[:, :], self.Eev[:, :], start=True, stop=True)
                    px.inc(t.matmul(pq[64:128, :], vb[:, :], self.Eod[:, :], start=True, stop=True))

                @blk.vector
                def _(v, ci=ci, dst=dst):
                    v.wait_ge(px.h, px.n)
                    vx.inc(v.tensor_copy(dst[:, ci, :], pq[:, :]))

    def p2_main(self, blk, ph):
        nc, c = self.nc, self.ctx
        xbcb = ph.enter_context(nc.sbuf_tensor("xbcb", [128, NXBC, CH + 3], BF16))
        cacc = ph.enter_context(nc.sbuf_tensor("cacc", [128, 2, CH], F32))
        xsil = ph.enter_context(nc.sbuf_tensor("xsil", [128, 4, CH], BF16))
        xtmb = ph.enter_context(nc.sbuf_tensor("xtmb", [128, 2, DI], BF16))
        dxb = ph.enter_context(nc.sbuf_tensor("dxb", [128, KDI, CH], BF16))
        btm = ph.enter_context(nc.sbuf_tensor("btm", [128, 2, 128], BF16))
        bg = ph.enter_context(nc.sbuf_tensor("bg", [128, 4, 128], BF16))
        prevb = ph.enter_context(nc.sbuf_tensor("prevb", [128, NPAIR * DST], BF16))
        zc0 = ph.enter_context(nc.sbuf_tensor("zc0", [64, 1], F32))
        pxb = [ph.enter_context(nc.psum_tensor(f"p2_x{i}", [128, 1024], BF16)) for i in range(2)]
        ps_bt = ph.enter_context(nc.psum_tensor("p2_bt", [128, 256], BF16))
        pst = [ph.enter_context(nc.psum_tensor(f"p2_s{i}", [128, 128], F32)) for i in range(2)]
        s_x = Ctr(nc, c, "p2_sx")
        g_cv = Ctr(nc, c, "p2_gcv")
        v_cv = Ctr(nc, c, "p2_vcv")
        a_sl = Ctr(nc, c, "p2_asl")
        p_tx = Ctr(nc, c, "p2_ptx")
        v_xt = Ctr(nc, c, "p2_vxt")
        v_dx = Ctr(nc, c, "p2_vdx")
        v_bt = self.xV
        v_bg = Ctr(nc, c, "p2_vbg")
        v_pb = Ctr(nc, c, "p2_vpb")
        p_sm = Ctr(nc, c, "p2_psm")
        v_pu = Ctr(nc, c, "p2_vpu")
        s_xt = self.xS
        s_dx = self.xS
        s_pv = self.xS
        g_cc = Ctr(nc, c, "p2_gcc")
        self.cc_sem = g_cc
        cv_done = [(0, 0)]
        sm_done = [0]
        sxt_done = [0]
        sdx_done = [0]
        spv_done = [0]

        for ci in range(NCH):
            @blk.sync
            def _(s, ci=ci):
                if ci >= 1:
                    s.wait_ge(v_cv.h, cv_done[ci][0])
                    s.wait_ge(g_cv.h, cv_done[ci][1])
                s_x.inc(s.dma_start(
                    out=xbcb[:, :, :],
                    in_=self.d_xbc.rearrange("(f p) t -> p f t", p=128)[:, :, ci * CH: ci * CH + CH + 3]), 16)

            sil_base = a_sl.n
            tx_base = p_tx.n
            dx_base = v_dx.n
            conv_done = {}

            @blk.vector
            def _(v, ci=ci, sil_base=sil_base, tx_base=tx_base):
                v.wait_ge(s_x.h, 16 * (ci + 1))
                for f in range(NXBC + 2):
                    if f < NXBC:
                        b = f % 2
                        if sil_base + f - 1 > 0:
                            v.wait_ge(a_sl.h, sil_base + f - 1)
                        v_cv.inc(v.tensor_scalar(cacc[:, b, :], xbcb[:, f, 0:CH], self.wcv[:, f, 0:1], None, ALU.mult))
                        for kk in range(1, 4):
                            v.wait_ge(v_cv.h, v_cv.n)
                            v_cv.inc(v.scalar_tensor_tensor(cacc[:, b, :], xbcb[:, f, kk:CH + kk], self.wcv[:, f, kk:kk + 1],
                                                            cacc[:, b, :], ALU.mult, ALU.add))
                        conv_done[f] = (v_cv, v_cv.n)
                    ff = f - 2
                    if 0 <= ff < 32:
                        v.wait_ge(a_sl.h, sil_base + ff + 1)
                        if ff == 0 and ci >= 1:
                            v.wait_ge(s_pv.h, spv_done[ci])
                        v_dx.inc(v.tensor_scalar(dxb[:, ff, :], xsil[:, ff % 4, :],
                                                 self.D_pair[:, ff:ff + 1], None, ALU.mult))
                        if ff % 8 == 7:
                            g4 = ff // 8
                            v.wait_ge(p_tx.h, tx_base + 16 * (g4 + 1))
                            if g4 == 0 and ci >= 1:
                                v.wait_ge(p_sm.h, sm_done[ci])
                                v.wait_ge(s_pv.h, spv_done[ci])
                            for j in range(2):
                                src = pxb[j].ap().rearrange("p (k x) -> p k x", k=8)
                                v_xt.inc(v.tensor_copy(
                                    xtmb.ap().rearrange("p j (k x) -> p j k x", k=KDI)[:, j, g4 * 8:(g4 + 1) * 8, :], src))
                cv_done.append((v_cv.n, g_cv.n))

            @blk.scalar
            def _(sc, ci=ci, conv_done=conv_done, tx_base=tx_base, dx_base=dx_base):
                for f in range(NXBC):
                    b = f % 2
                    cctr, cval = conv_done[f]
                    sc.wait_ge(cctr.h, cval)
                    if f < 32:
                        if f >= 4:
                            sc.wait_ge(p_tx.h, tx_base + 2 * (f - 3))
                            sc.wait_ge(v_dx.h, dx_base + f - 3)
                        a_sl.inc(sc.activation(xsil[:, f % 4, :], cacc[:, b, :], AF.Silu))
                    elif f == 32:
                        a_sl.inc(sc.activation(self.B_all[:, ci * CH:(ci + 1) * CH], cacc[:, b, :], AF.Silu))
                    else:
                        a_sl.inc(sc.activation(self.C_all[:, ci * CH:(ci + 1) * CH], cacc[:, b, :], AF.Silu))

            sil2 = a_sl.n - NXBC

            @blk.tensor
            def _(t, ci=ci, sil2=sil2):
                for f in range(32):
                    g4 = f // 8
                    t.wait_ge(a_sl.h, sil2 + f + 1)
                    if f % 8 == 0 and (ci > 0 or g4 > 0):
                        t.wait_ge(v_xt.h, 2 * (4 * ci + g4))
                    for j in range(2):
                        p_tx.inc(t.transpose(pxb[j][:, (f % 8) * 128:(f % 8) * 128 + 128],
                                             xsil[:, f % 4, j * 128:(j + 1) * 128],
                                             self.identb[:, :]))
                t.wait_ge(a_sl.h, sil2 + 33)
                for s in range(2):
                    p_tx.inc(t.transpose(ps_bt[:, s * 128:(s + 1) * 128],
                                         self.B_all[:, ci * CH + s * 128: ci * CH + (s + 1) * 128],
                                         self.identb[:, :]))

            tx2 = p_tx.n - 66
            bg_base = v_bg.n
            sm_base = p_sm.n

            @blk.vector
            def _(v, ci=ci, tx2=tx2):
                v.wait_ge(p_tx.h, tx2 + 66)
                v_bt.inc(v.tensor_copy(btm.ap().rearrange("p s x -> p (s x)"), ps_bt[:, :]))
                if ci >= 1:
                    v.wait_ge(s_pv.h, spv_done[ci])
                v_pb.inc(v.tensor_copy(prevb[:, :], self.prev[:, :]))

            pu_base = v_pu.n

            @blk.vector
            def _(v, ci=ci, sm_base=sm_base):
                v.wait_ge(v_pb.h, v_pb.n)
                for j in range(NPAIR):
                    if j >= 1:
                        v.wait_ge(p_sm.h, sm_base + j)
                    for h in (2 * j, 2 * j + 1):
                        for s in range(2):
                            ins = v.tensor_scalar(bg[:, (h % 2) * 2 + s, :], btm[:, s, :],
                                                  self.g_tm[:, ci, s, h:h + 1], None, ALU.mult)
                        v_bg.inc(ins)
                    if j >= 1:
                        jj = j - 1
                        v_pu.inc(v.scalar_tensor_tensor(self.prev[:, jj * DST:(jj + 1) * DST],
                                                        self.prev[:, jj * DST:(jj + 1) * DST],
                                                        self.cdcol[:, ci, jj:jj + 1],
                                                        pst[jj % 2][:, :], ALU.mult, ALU.add))
                v.wait_ge(p_sm.h, sm_base + NPAIR)
                jj = NPAIR - 1
                v_pu.inc(v.scalar_tensor_tensor(self.prev[:, jj * DST:(jj + 1) * DST],
                                                self.prev[:, jj * DST:(jj + 1) * DST],
                                                self.cdcol[:, ci, jj:jj + 1],
                                                pst[jj % 2][:, :], ALU.mult, ALU.add))

            @blk.tensor
            def _(t, ci=ci, bg_base=bg_base, pu_base=pu_base):
                for j in range(NPAIR):
                    t.wait_ge(v_bg.h, bg_base + 2 * (j + 1))
                    if j >= 2:
                        t.wait_ge(v_pu.h, pu_base + j - 1)
                    h0, h1 = 2 * j, 2 * j + 1
                    t.matmul(pst[j % 2][0:64, :], xtmb[:, 0, 64 * h0:64 * h0 + 64], bg[:, 0, :],
                             start=True, stop=False, tile_position=(0, 0))
                    t.matmul(pst[j % 2][0:64, :], xtmb[:, 1, 64 * h0:64 * h0 + 64], bg[:, 1, :],
                             start=False, stop=True, tile_position=(0, 0))
                    t.matmul(pst[j % 2][64:128, :], xtmb[:, 0, 64 * h1:64 * h1 + 64], bg[:, 2, :],
                             start=True, stop=False, tile_position=(0, 64))
                    mm = t.matmul(pst[j % 2][64:128, :], xtmb[:, 1, 64 * h1:64 * h1 + 64], bg[:, 3, :],
                                  start=False, stop=True, tile_position=(0, 64))
                    p_sm.inc(mm)
                sm_done.append(p_sm.n)

            @blk.sync
            def _(s, ci=ci):
                s.wait_ge(v_xt.h, 8 * (ci + 1))
                for j in range(2):
                    s_xt.inc(s.dma_start(out=self.d_xtm[ci * CH + j * 128: ci * CH + (j + 1) * 128, :],
                                         in_=xtmb[:, j, :]), 16)
                sxt_done.append(s_xt.n)
                s.wait_ge(v_dx.h, 32 * (ci + 1))
                s_dx.inc(s.dma_start(out=self.d_dx.rearrange("(f p) t -> p f t", p=128)[:, :, ci * CH:(ci + 1) * CH],
                                     in_=dxb[:, :, :]), 16)
                sdx_done.append(s_dx.n)
                s.wait_ge(v_pb.h, ci + 1)
                s_pv.inc(s.dma_start(out=self.d_prev[ci, :, :], in_=prevb[:, :]), 16)
                spv_done.append(s_pv.n)

        @blk.vector
        def _(v):
            v.wait_ge(v_pu.h, v_pu.n)
            v_bt.inc(v.memset(zc0[:, 0:1], 0.0))

        @blk.sync
        def _(s):
            s.wait_ge(v_pu.h, v_pu.n)
            s.wait_ge(v_bt.h, v_bt.n)
            s_pv.inc(s.dma_start(out=self.cc_in[:, 0:NPAIR * DST], in_=self.prev[:, :]), 16)
            s_pv.inc(s.dma_start(out=self.cc_in[0:64, NPAIR * DST:NPAIR * DST + 1], in_=self.logP[:, 0:1], allow_slow_non_contiguous=True), 16)
            s_pv.inc(s.dma_start(out=self.cc_in[64:128, NPAIR * DST:NPAIR * DST + 1], in_=zc0[:, 0:1], allow_slow_non_contiguous=True), 16)

        self.s_xt_total = (s_pv, s_pv.n)

        @blk.gpsimd
        def _(g):
            g.wait_ge(s_pv.h, s_pv.n)
            g_cc.inc(g.collective_compute("AllGather", ALU.bypass,
                                          replica_groups=[list(range(8))],
                                          ins=[self.cc_in.opt()], outs=[self.cc_out.opt()]))

    def p3b_ffn(self, blk, ph):
        nc, c = self.nc, self.ctx
        xnh = ph.enter_context(nc.sbuf_tensor("xnh", [128, KD, 1024], BF16))
        acc = ph.enter_context(nc.sbuf_tensor("facc", [128, KD, 1024], BF16))
        he = ph.enter_context(nc.sbuf_tensor("he", [128, KE, 1024], BF16))
        wgu = ph.enter_context(nc.sbuf_tensor("wgu", [128, 2, 2, KD, 128], BF16))
        sgb = ph.enter_context(nc.sbuf_tensor("sgb", [128, 2, 2, 512], BF16))
        tb = ph.enter_context(nc.sbuf_tensor("tb", [128, 2, 512], BF16))
        cbe = ph.enter_context(nc.sbuf_tensor("cbe", [128, 1024], BF16))
        wd = ph.enter_context(nc.sbuf_tensor("wd", [128, 2, KE, 512], BF16))
        pp = [ph.enter_context(nc.psum_tensor(f"f_p{i}", [128, 512], F32)) for i in range(8)]
        pcb = pp[0]
        g_f = [Ctr(nc, c, "f_gf0"), Ctr(nc, c, "f_gf1")]
        g_dve = self.xV
        p_f = Ctr(nc, c, "f_pf")
        a_f = Ctr(nc, c, "f_af")
        v_f = Ctr(nc, c, "f_vf")
        p_cb = Ctr(nc, c, "f_pcb")
        v_cb = self.xV
        g_d = [Ctr(nc, c, "f_gd0"), Ctr(nc, c, "f_gd1")]
        p_d = Ctr(nc, c, "f_pd")
        v_d = Ctr(nc, c, "f_vd")
        s_h = Ctr(nc, c, "f_sh")
        s_o = self.xS

        experts = [("s", 0), ("s", 1)] + [("e", e) for e in range(8)]

        def wg_ap(kind, idx, dh):
            col = (idx * KE + dh) * 128 if kind == "s" else dh * 128
            src = self.ws_gate if kind == "s" else self.we_gate[idx]
            return src[:, col:col + 128].rearrange("(kc p) o -> p kc o", p=128)

        def wu_ap(kind, idx, dh):
            col = (idx * KE + dh) * 128 if kind == "s" else dh * 128
            src = self.ws_up if kind == "s" else self.we_up[idx]
            return src[:, col:col + 128].rearrange("(kc p) o -> p kc o", p=128)

        def wd_ap(kind, idx, q):
            r0 = idx * DE if kind == "s" else 0
            src = self.ws_down if kind == "s" else self.we_down[idx]
            return src[r0:r0 + DE, q * 512:(q + 1) * 512].rearrange("(kc p) o -> p kc o", p=128)

        for H in range(2):
            off = H * 1024
            ld_done = s_h.n

            @blk.sync
            def _(s, H=H, off=off):
                if H == 1:
                    s.wait_ge(p_f.h, p_f.n)
                    s.wait_ge(p_d.h, p_d.n)
                s_h.inc(s.dma_start(
                    out=xnh[:, :, :],
                    in_=self.d_xn.rearrange("(k p) t -> p k t", p=128)[:, :, 3 + off: 3 + off + 1024]), 16)

            for ei, (kind, idx) in enumerate(experts):
                has_cb = kind == "e"
                if has_cb:
                    for t2 in range(2):
                        @blk.tensor
                        def _(t, idx=idx, t2=t2, off=off):
                            t.wait_ge(v_cb.h, v_cb.n)
                            t.wait_ge(v_d.h, v_d.n)
                            p_cb.inc(t.matmul(pcb[:, :], self.sel8[:, idx, :],
                                              self.combine[0:8, off + t2 * 512: off + (t2 + 1) * 512],
                                              start=True, stop=True))

                        @blk.vector
                        def _(v, t2=t2):
                            v.wait_ge(p_cb.h, p_cb.n)
                            v_cb.inc(v.tensor_copy(cbe[:, t2 * 512:(t2 + 1) * 512], pcb[:, :]))

                gf_list = []
                pf_base = p_f.n
                af_base = a_f.n
                vf_base = v_f.n
                pd_prev = p_d.n

                @blk.gpsimd
                def _(g, kind=kind, idx=idx, pf_base=pf_base):
                    for dh in range(KE):
                        if dh >= 2:
                            g.wait_ge(p_f.h, pf_base + 4 * (dh - 1))
                        g_f[dh % 2].inc(g.dma_start(out=wgu[:, dh % 2, 0, :, :], in_=wg_ap(kind, idx, dh)), 16)
                        g_f[dh % 2].inc(g.dma_start(out=wgu[:, dh % 2, 1, :, :], in_=wu_ap(kind, idx, dh)), 16)
                        gf_list.append(g_f[dh % 2].n)

                vcb_now = v_cb.n
                vd_prev = v_d.n

                @blk.tensor
                def _(t, ei=ei, H=H, af_base=af_base, vf_base=vf_base, vcb_now=vcb_now, vd_prev=vd_prev):
                    for dh in range(KE):
                        t.wait_ge(g_f[dh % 2].h, gf_list[dh])
                        if dh == 0:
                            t.wait_ge(v_cb.h, vcb_now)
                            t.wait_ge(s_h.h, 16 * (H + 1))
                            t.wait_ge(v_d.h, vd_prev)
                        if dh >= 2:
                            t.wait_ge(a_f.h, af_base + 2 * (dh - 1))
                            t.wait_ge(v_f.h, vf_base + 2 * (dh - 1))
                        for gu in range(2):
                            for t2 in range(2):
                                bank = pp[4 * (dh % 2) + 2 * gu + t2]
                                for k in range(KD):
                                    mm = t.matmul(bank[:, :], wgu[:, dh % 2, gu, k, :],
                                                  xnh[:, k, t2 * 512:(t2 + 1) * 512],
                                                  start=(k == 0), stop=(k == KD - 1))
                                    if k == KD - 1:
                                        p_f.inc(mm)

                @blk.scalar
                def _(sc, pf_base=pf_base, vf_base=vf_base):
                    for dh in range(KE):
                        for t2 in range(2):
                            sc.wait_ge(p_f.h, pf_base + 4 * dh + 2 * 0 + t2 + 1)
                            if dh >= 2:
                                sc.wait_ge(v_f.h, vf_base + 2 * (dh - 1))
                            a_f.inc(sc.activation(sgb[:, dh % 2, t2, :],
                                                  pp[4 * (dh % 2) + t2][:, :], AF.Silu))

                @blk.vector
                def _(v, has_cb=has_cb, pf_base=pf_base, af_base=af_base, pd_prev=pd_prev):
                    for dh in range(KE):
                        for t2 in range(2):
                            v.wait_ge(a_f.h, af_base + 2 * dh + t2 + 1)
                            v.wait_ge(p_f.h, pf_base + 4 * dh + 2 + t2 + 1)
                            if dh == 0 and t2 == 0:
                                v.wait_ge(p_d.h, pd_prev)
                            upsum = pp[4 * (dh % 2) + 2 + t2]
                            if has_cb:
                                g_dve.inc(v.tensor_mul(tb[:, t2, :], upsum[:, :], sgb[:, dh % 2, t2, :]))
                                v.wait_ge(g_dve.h, g_dve.n)
                                v_f.inc(v.tensor_mul(he[:, dh, t2 * 512:(t2 + 1) * 512],
                                                     tb[:, t2, :], cbe[:, t2 * 512:(t2 + 1) * 512]))
                            else:
                                v_f.inc(v.tensor_mul(he[:, dh, t2 * 512:(t2 + 1) * 512],
                                                     upsum[:, :], sgb[:, dh % 2, t2, :]))

                gd_list = []
                pd_base = p_d.n
                vd_base = v_d.n
                vf_end = v_f.n

                @blk.gpsimd
                def _(g, kind=kind, idx=idx, pd_base=pd_base):
                    for q in range(4):
                        if q >= 2:
                            g.wait_ge(p_d.h, pd_base + (q - 1) * KE)
                        g_d[q % 2].inc(g.dma_start(out=wd[:, q % 2, :, :], in_=wd_ap(kind, idx, q)), 16)
                        gd_list.append(g_d[q % 2].n)

                @blk.tensor
                def _(t, ei=ei, vf_end=vf_end, vd_base=vd_base):
                    t.wait_ge(v_f.h, vf_end)
                    for q in range(4):
                        if q >= 1:
                            t.wait_ge(v_d.h, vd_base + q)
                        elif ei >= 1:
                            t.wait_ge(v_d.h, vd_base)
                        t.wait_ge(g_d[q % 2].h, gd_list[q])
                        for k in range(KE):
                            for d4 in range(4):
                                for t2 in range(2):
                                    mm = t.matmul(pp[2 * d4 + t2][:, :],
                                                  wd[:, q % 2, k, d4 * 128:(d4 + 1) * 128],
                                                  he[:, k, t2 * 512:(t2 + 1) * 512],
                                                  start=(k == 0), stop=(k == KE - 1))
                                    if d4 == 3 and t2 == 1:
                                        p_d.inc(mm)

                @blk.vector
                def _(v, ei=ei, H=H, pd_base=pd_base):
                    for q in range(4):
                        v.wait_ge(p_d.h, pd_base + (q + 1) * KE)
                        if H == 1 and ei == 0 and q == 0:
                            v.wait_ge(s_o.h, self._ffn_h0_store)
                        for d4 in range(4):
                            for t2 in range(2):
                                dd = q * 4 + d4
                                dst = acc[:, dd, t2 * 512:(t2 + 1) * 512]
                                last = d4 == 3 and t2 == 1
                                if ei == 0:
                                    ins = v.tensor_copy(dst, pp[2 * d4 + t2][:, :])
                                else:
                                    ins = v.tensor_add(dst, dst, pp[2 * d4 + t2][:, :])
                                if last:
                                    v_d.inc(ins)

            @blk.sync
            def _(s, off=off):
                s.wait_ge(v_d.h, v_d.n)
                s_o.inc(s.dma_start(
                    out=self.d_ffn.rearrange("(k p) t -> p k t", p=128)[:, :, off:off + 1024],
                    in_=acc[:, :, :]), 16)
            if H == 0:
                self._ffn_h0_store = s_o.n

        @blk.sync
        def _(s):
            s.wait_ge(s_o.h, s_o.n)

    def p4_hin(self, blk, ph):
        nc, c = self.nc, self.ctx
        self._p4_ctx = ExitStack()
        self.ccol = self._p4_ctx.enter_context(nc.sbuf_tensor("ccol", [128, 8, NPAIR], F32))
        self.sel = self._p4_ctx.enter_context(nc.sbuf_tensor("sel", [NH, NH, 128], BF16))
        self.h_in = self._p4_ctx.enter_context(nc.sbuf_tensor("h_in", [128, NPAIR * DST], F32))
        lp = ph.enter_context(nc.sbuf_tensor("lp", [64, 8], F32))
        lpT = ph.enter_context(nc.sbuf_tensor("lpT", [8, 64], F32))
        coefT = ph.enter_context(nc.sbuf_tensor("coefT", [64, 8], F32))
        vb2 = ph.enter_context(nc.sbuf_tensor("vb2", [64, 64], F32))
        hjb = ph.enter_context(nc.sbuf_tensor("hjb", [128, 2, NPAIR * DST], F32))
        pl8 = ph.enter_context(nc.psum_tensor("h_pl", [8, 64], F32))
        pq2 = ph.enter_context(nc.psum_tensor("h_pq", [128, 32], F32))
        p64 = ph.enter_context(nc.psum_tensor("h_p64", [64, 8], F32))
        s_l = self.xS
        v_h = self.xV
        p_h = self.xT
        a_h = self.xA
        s_j = [Ctr(nc, c, "h_sj0"), Ctr(nc, c, "h_sj1")]
        v_a = self.xV

        @blk.sync
        def _(s):
            s.wait_ge(self.cc_sem.h, 1)
            for j in range(8):
                s_l.inc(s.dma_start(out=lp[:, j:j + 1],
                                    in_=self.cc_out[j * 128:j * 128 + 64, NPAIR * DST:NPAIR * DST + 1],
                                    allow_slow_non_contiguous=True), 16)

        @blk.vector
        def _(v):
            v.wait_ge(s_l.h, s_l.n)
            v_h.inc(v.memset(self.h_in[:, :], 0.0))

        @blk.tensor
        def _(t):
            t.wait_ge(s_l.h, s_l.n)
            p_h.inc(t.transpose(pl8[0:8, :], lp[:, 0:8], self.identf[0:64, 0:64]))

        @blk.vector
        def _(v):
            v.wait_ge(p_h.h, p_h.n)
            v_h.inc(v.tensor_copy(lpT[:, :], pl8[0:8, :]))

        @blk.tensor
        def _(t):
            t.wait_ge(v_h.h, v_h.n)
            p_h.inc(t.matmul(pl8[0:8, :], self.mselT_sb[:, :], lpT[:, :], start=True, stop=True))

        @blk.scalar
        def _(sc):
            sc.wait_ge(p_h.h, p_h.n)
            a_h.inc(sc.activation(lpT[:, :], pl8[0:8, :], AF.Exp, bias=self.cbias_col[:, 0:1]))

        @blk.tensor
        def _(t):
            t.wait_ge(a_h.h, a_h.n)
            p_h.inc(t.transpose(p64[0:64, 0:8], lpT[0:8, 0:64], self.identf[0:8, 0:8]))

        @blk.vector
        def _(v):
            v.wait_ge(p_h.h, p_h.n)
            v_h.inc(v.tensor_copy(coefT[:, :], p64[0:64, 0:8]))

        for j in range(8):
            @blk.vector
            def _(v, j=j):
                v.wait_ge(v_h.h, v_h.n)
                v.wait_ge(p_h.h, p_h.n)
                v_h.inc(v.tensor_scalar(vb2[:, :], self.ones_f[:, 0:64], coefT[:, j:j + 1], None, ALU.mult))

            @blk.tensor
            def _(t):
                t.wait_ge(v_h.h, v_h.n)
                t.matmul(pq2[0:64, :], vb2[:, :], self.Eev[:, :], start=True, stop=True)
                p_h.inc(t.matmul(pq2[64:128, :], vb2[:, :], self.Eod[:, :], start=True, stop=True))

            @blk.vector
            def _(v, j=j):
                v.wait_ge(p_h.h, p_h.n)
                v_h.inc(v.tensor_copy(self.ccol[:, j, :], pq2[:, :]))

        va_list = []
        for j in range(8):
            @blk.sync
            def _(s, j=j):
                if j >= 2:
                    s.wait_ge(v_a.h, va_list[j - 2])
                s_j[j % 2].inc(s.dma_start(out=hjb[:, j % 2, :], in_=self.cc_out[j * 128:(j + 1) * 128, 0:NPAIR * DST]), 16)

            @blk.vector
            def _(v, j=j):
                v.wait_ge(s_j[j % 2].h, 16 * (j // 2 + 1))
                if j == 0:
                    v.wait_ge(v_h.h, v_h.n)
                else:
                    v.wait_ge(v_a.h, va_list[j - 1])
                for m in range(NPAIR):
                    ins = v.scalar_tensor_tensor(self.h_in[:, m * DST:(m + 1) * DST],
                                                 hjb[:, j % 2, m * DST:(m + 1) * DST],
                                                 self.ccol[:, j, m:m + 1],
                                                 self.h_in[:, m * DST:(m + 1) * DST], ALU.mult, ALU.add)
                v_a.inc(ins)
                va_list.append(v_a.n)

    def p4_mid(self, blk, ph):
        nc, c = self.nc, self.ctx
        NGR = NCH * 4
        NR = NCH * NPAIR
        zg = ph.enter_context(nc.sbuf_tensor("zg", [128, 2, 8, CH], BF16))
        dxg = ph.enter_context(nc.sbuf_tensor("dxg", [128, 2, 8, CH], BF16))
        xtg = ph.enter_context(nc.sbuf_tensor("xtg", [128, 2, 2, 1024], BF16))
        plg = ph.enter_context(nc.sbuf_tensor("plg", [128, 2, 1024], BF16))
        ptbg = ph.enter_context(nc.sbuf_tensor("ptbg", [128, 2, 1024], BF16))
        pTg = ph.enter_context(nc.sbuf_tensor("pTg", [128, 2, 1024], BF16))
        srt = ph.enter_context(nc.sbuf_tensor("srt", [128, 2, CH], BF16))
        segb = ph.enter_context(nc.sbuf_tensor("segb", [128, 4, CH], F32))
        lmb = ph.enter_context(nc.sbuf_tensor("lmb", [128, 2, 4, CH], BF16))
        stdb = ph.enter_context(nc.sbuf_tensor("stdb", [128, 2, 4, CH], BF16))
        ecsb = ph.enter_context(nc.sbuf_tensor("ecsb", [128, 2, CH], BF16))
        v1b = ph.enter_context(nc.sbuf_tensor("v1b", [128, 2, CH], BF16))
        t1b = ph.enter_context(nc.sbuf_tensor("t1b", [128, 2, CH], BF16))
        v2b = ph.enter_context(nc.sbuf_tensor("v2b", [128, 2, CH], BF16))
        ygb2 = ph.enter_context(nc.sbuf_tensor("ygb2", [128, 2, CH], BF16))
        sqb = ph.enter_context(nc.sbuf_tensor("sqb", [128, 2, CH], BF16))
        ygsb = ph.enter_context(nc.sbuf_tensor("ygsb", [128, 2, 8, CH], BF16))
        rr = ph.enter_context(nc.sbuf_tensor("rr", [1, 3, CH], F32))
        M0 = ph.enter_context(nc.psum_tensor("m_M0", [128, 1024], BF16))
        M1 = ph.enter_context(nc.psum_tensor("m_M1", [128, 512], F32))
        M2 = ph.enter_context(nc.psum_tensor("m_M2", [128, 512], F32))
        M3 = [ph.enter_context(nc.psum_tensor(f"m_M3{i}", [128, CH], F32)) for i in range(2)]
        M4 = ph.enter_context(nc.psum_tensor("m_M4", [128, 512], F32))

        s_c = Ctr(nc, c, "m_sc")
        s_y = Ctr(nc, c, "m_sy")
        v_m = Ctr(nc, c, "m_vm")
        a_m = Ctr(nc, c, "m_am")
        p_m = Ctr(nc, c, "m_pm")

        ld_done = [0] * (NGR + 2)
        ptb_done = [0] * (NGR + 1)
        pte_done = [0] * (NGR + 1)
        tr_done = [0] * (NGR + 1)
        st_done = [0] * (NGR + 1)
        bc_done = [0] * (NR + 1)
        yd_done = [0] * (NR + 1)
        yo_done = [0] * (NR + 1)
        ec_done = [0] * (NR + 1)
        rm_done = [0] * (NR + 1)
        sg_done = [0] * (NR + 1)
        ex_done = [0] * (NR + 1)
        sd_done = [0] * (NR + 1)
        t1_done = [0] * (NR + 1)
        v1_done = [0] * (NR + 1)
        ecs_done = [0] * (NR + 1)
        sq_done = [0] * (NR + 1)
        yn_done = [0] * (NR + 1)
        sc_done = [0] * (NCH + 1)
        sr_done = [0] * (NCH + 1)
        rr0_done = [0] * (NCH + 1)
        rsq_done = [0] * (NCH + 1)
        rcp_done = [0] * (NCH + 1)
        rs_done = [0] * (NCH + 1)

        def em_loads(s, gi):
            ci2, g2 = gi // 4, gi % 4
            c0 = ci2 * CH
            sl = gi % 2
            s_c.inc(s.dma_start(
                out=zg[:, sl, :, :],
                in_=self.d_z.rearrange("(f p) t -> p f t", p=128)[:, 8 * g2:8 * g2 + 8, c0:c0 + CH]), 16)
            s_c.inc(s.dma_start(
                out=dxg[:, sl, :, :],
                in_=self.d_dx.rearrange("(f p) t -> p f t", p=128)[:, 8 * g2:8 * g2 + 8, c0:c0 + CH]), 16)
            for s2 in range(2):
                s_c.inc(s.dma_start(
                    out=xtg[:, sl, s2, :],
                    in_=self.d_xtm[c0 + s2 * 128: c0 + (s2 + 1) * 128, 1024 * g2:1024 * (g2 + 1)]), 16)
            s_c.inc(s.dma_start(out=plg[:, sl, :], in_=self.d_prev[ci2, :, 1024 * g2:1024 * (g2 + 1)]), 16)
            ld_done[gi] = s_c.n

        @blk.vector
        def _(v):
            for h in range(NH):
                v.tensor_scalar(self.sel[:, h, :], self.ones_b[0:NH, :],
                                self.identf[0:NH, h:h + 1], None, ALU.mult)
            v_m.inc(v.engine_nop())
        selb_done = v_m.n

        @blk.sync
        def _(s):
            s.wait_ge(self.s_xt_total[0].h, self.s_xt_total[1])
            em_loads(s, 0)

        @blk.tensor
        def _(t):
            for s2 in range(2):
                mm = t.matmul(M1[:, s2 * CH:(s2 + 1) * CH],
                              self.B_all[:, s2 * 128:(s2 + 1) * 128],
                              self.C_all[:, 0:CH], start=(s2 == 0), stop=(s2 == 1))
            p_m.inc(mm)
        sc_done[0] = p_m.n

        @blk.vector
        def _(v):
            v.wait_ge(p_m.h, sc_done[0])
            v.tensor_copy(srt[:, 0, :], M1[:, 0:CH])
            v_m.inc(v.tensor_copy(srt[:, 1, :], M1[:, CH:2 * CH]))
        sr_done[0] = v_m.n

        for r in range(NR + 1):
            bnd = (r % 8 == 0) and r < NR
            gi = r // 8
            ci = r // 32
            j = r % 32
            p = r - 1
            if p >= 0:
                gp_i, cp_i, jp, jjp = p // 8, p // 32, p % 32, p % 8

            if bnd:
                @blk.vector
                def _(v, gi=gi, ci=ci):
                    v.wait_ge(s_c.h, ld_done[gi])
                    if gi == 0:
                        v.wait_ge(self.hin_sem.h, self.hin_total)
                    if gi >= 2:
                        v.wait_ge(p_m.h, tr_done[gi - 2])
                    for jj in range(8):
                        ja = (gi % 4) * 8 + jj
                        v_m.inc(v.scalar_tensor_tensor(
                            ptbg[:, gi % 2, jj * 128:(jj + 1) * 128],
                            self.h_in[:, ja * DST:(ja + 1) * DST],
                            self.Dcol[:, ci, ja:ja + 1],
                            plg[:, gi % 2, jj * 128:(jj + 1) * 128], ALU.mult, ALU.add))
                    ptb_done[gi] = v_m.n

            @blk.tensor
            def _(t, r=r, bnd=bnd, gi=gi, ci=ci, j=j, p=p):
                if bnd:
                    t.wait_ge(v_m.h, ptb_done[gi])
                    if gi >= 1:
                        t.wait_ge(v_m.h, pte_done[gi - 1])
                    for jj in range(8):
                        p_m.inc(t.transpose(M0[:, jj * 128:(jj + 1) * 128],
                                            ptbg[:, gi % 2, jj * 128:(jj + 1) * 128],
                                            self.identb[:, :]))
                    tr_done[gi] = p_m.n
                if r < NR:
                    if r >= 1:
                        t.wait_ge(v_m.h, sg_done[r - 1])
                    for dl in range(2):
                        h = 2 * j + dl
                        t.matmul(M2[:, dl * CH:(dl + 1) * CH], self.sel[0:NH, h, :],
                                 self.cs_hi[:, ci * CH:(ci + 1) * CH], start=True, stop=False)
                        mm = t.matmul(M2[:, dl * CH:(dl + 1) * CH], self.sel[0:NH, h, :],
                                      self.cs_lo[:, ci * CH:(ci + 1) * CH], start=False, stop=True)
                        p_m.inc(mm)
                    bc_done[r] = p_m.n
                if r >= 1:
                    gp_i2, cp_i2, jp2, jjp2 = p // 8, p // 32, p % 32, p % 8
                    t.wait_ge(v_m.h, sd_done[p])
                    t.wait_ge(s_c.h, ld_done[gp_i2])
                    wv = 0
                    if p >= 2:
                        wv = max(wv, t1_done[p - 2])
                    if p >= 1:
                        wv = max(wv, v1_done[p - 1], ecs_done[p - 1])
                    wv = max(wv, pte_done[gp_i2])
                    if wv:
                        t.wait_ge(v_m.h, wv)
                    for q in range(4):
                        dl, s2 = q // 2, q % 2
                        hl = 2 * jjp2 + dl
                        mm = t.matmul(M3[p % 2][64 * dl:64 * dl + 64, :],
                                      xtg[:, gp_i2 % 2, s2, 64 * hl:64 * hl + 64],
                                      stdb[:, p % 2, q, :],
                                      start=(s2 == 0), stop=(s2 == 1), tile_position=(0, 64 * dl))
                        if q == 3:
                            p_m.inc(mm)
                    yd_done[p] = p_m.n
                    for dl in range(2):
                        mm = t.matmul(M4[64 * dl:64 * dl + 64, 0:CH],
                                      pTg[:, gp_i2 % 2, jjp2 * DST + 64 * dl: jjp2 * DST + 64 * dl + 64],
                                      self.C_all[:, cp_i2 * CH:(cp_i2 + 1) * CH],
                                      start=True, stop=True, tile_position=(0, 64 * dl))
                        if dl == 1:
                            p_m.inc(mm)
                    yo_done[p] = p_m.n
                    for dl in range(2):
                        h = 2 * jp2 + dl
                        mm = t.matmul(M4[64 * dl:64 * dl + 64, CH:2 * CH],
                                      self.sel[0:NH, h, 0:64],
                                      self.ecs16[:, cp_i2 * CH:(cp_i2 + 1) * CH],
                                      start=True, stop=True, tile_position=(0, 64 * dl))
                        if dl == 1:
                            p_m.inc(mm)
                    ec_done[p] = p_m.n
                qq = r - 2
                if qq >= 0 and qq % 32 != 31:
                    t.wait_ge(v_m.h, sq_done[qq])
                    if qq % 32 == 0:
                        t.wait_ge(v_m.h, sr_done[qq // 32])
                    mm = t.matmul(M1[:, 0:CH], self.ones_b[:, :], sqb[:, qq % 2, :],
                                  start=(qq % 32 == 0), stop=False)
                    p_m.inc(mm)
                    rm_done[qq] = p_m.n

            @blk.vector
            def _(v, r=r, bnd=bnd, gi=gi, ci=ci, j=j, p=p):
                if bnd:
                    v.wait_ge(p_m.h, tr_done[gi])
                    if gi >= 2:
                        v.wait_ge(p_m.h, yo_done[(gi - 1) * 8 - 1])
                    v_m.inc(v.tensor_copy(pTg[:, gi % 2, :], M0[:, :]))
                    pte_done[gi] = v_m.n
                if r < NR:
                    v.wait_ge(p_m.h, bc_done[r])
                    if r >= 1:
                        v.wait_ge(a_m.h, ex_done[r - 1])
                    for q in range(4):
                        dl, s2 = q // 2, q % 2
                        h = 2 * j + dl
                        v_m.inc(v.scalar_tensor_tensor(
                            segb[:, q, :], M2[:, dl * CH:(dl + 1) * CH],
                            self.csldt[:, ci, s2, h:h + 1],
                            self.maskT[:, s2, :], ALU.subtract, ALU.add))
                    sg_done[r] = v_m.n
                if r >= 1:
                    v.wait_ge(p_m.h, ec_done[p])
                    v_m.inc(v.tensor_copy(ecsb[:, p % 2, :], M4[:, CH:2 * CH]))
                    ecs_done[p] = v_m.n
                    v.wait_ge(v_m.h, v_m.n)
                    v_m.inc(v.tensor_mul(v1b[:, p % 2, :], M4[:, 0:CH], ecsb[:, p % 2, :]))
                    v1_done[p] = v_m.n
                    v.wait_ge(v_m.h, v_m.n)
                    v_m.inc(v.tensor_add(t1b[:, p % 2, :], v1b[:, p % 2, :], M3[p % 2][:, :]))
                    t1_done[p] = v_m.n

            if r < NR:
                @blk.scalar
                def _(sc, r=r):
                    sc.wait_ge(v_m.h, sg_done[r])
                    if r >= 2:
                        sc.wait_ge(v_m.h, sd_done[r - 2])
                    for q in range(4):
                        a_m.inc(sc.activation(lmb[:, r % 2, q, :], segb[:, q, :], AF.Exp))
                    ex_done[r] = a_m.n

            @blk.vector
            def _(v, r=r, ci=ci, p=p):
                if r >= 1:
                    gp_i2, jp2, jjp2 = p // 8, p % 32, p % 8
                    v.wait_ge(s_c.h, ld_done[gp_i2])
                    v_m.inc(v.tensor_add(v2b[:, p % 2, :], t1b[:, p % 2, :], dxg[:, gp_i2 % 2, jjp2, :]))
                    v.wait_ge(v_m.h, v_m.n)
                    v_m.inc(v.tensor_mul(ygb2[:, p % 2, :], v2b[:, p % 2, :], zg[:, gp_i2 % 2, jjp2, :]))
                    v.wait_ge(v_m.h, v_m.n)
                    v_m.inc(v.tensor_mul(sqb[:, p % 2, :], ygb2[:, p % 2, :], ygb2[:, p % 2, :]))
                    sq_done[p] = v_m.n
                    if gp_i2 >= 2:
                        v.wait_ge(s_y.h, st_done[gp_i2 - 2])
                    v_m.inc(v.tensor_scalar(ygsb[:, gp_i2 % 2, jjp2, :], ygb2[:, p % 2, :],
                                            self.normw[:, jp2:jp2 + 1], None, ALU.mult))
                    yn_done[p] = v_m.n
                if r < NR and not (r % 32 == 0 and r >= 32):
                    v.wait_ge(a_m.h, ex_done[r])
                    if r >= 2:
                        v.wait_ge(p_m.h, yd_done[r - 2])
                    for q in range(4):
                        v_m.inc(v.tensor_mul(stdb[:, r % 2, q, :], srt[:, q % 2, :],
                                             lmb[:, r % 2, q, :]))
                    sd_done[r] = v_m.n

            if (r % 32 == 0 and r >= 32) or r == NR:
                cprev = r // 32 - 1
                q31 = r - 1

                @blk.tensor
                def _(t, q31=q31):
                    t.wait_ge(v_m.h, sq_done[q31])
                    mm = t.matmul(M1[:, 0:CH], self.ones_b[:, :], sqb[:, q31 % 2, :],
                                  start=False, stop=True)
                    p_m.inc(mm)
                    rm_done[q31] = p_m.n

                @blk.vector
                def _(v, cprev=cprev, q31=q31):
                    v.wait_ge(p_m.h, rm_done[q31])
                    if cprev >= 1:
                        v.wait_ge(s_y.h, rs_done[cprev - 1])
                    v_m.inc(v.tensor_scalar(rr[0:1, 0, :], M1[0:1, 0:CH], 1.0 / DI, None, ALU.mult))
                    rr0_done[cprev] = v_m.n

                @blk.scalar
                def _(sc, cprev=cprev):
                    sc.wait_ge(v_m.h, rr0_done[cprev])
                    a_m.inc(sc.activation(rr[0:1, 1, :], rr[0:1, 0, :], AF.Sqrt,
                                          bias=self.eps_col[0:1, 0:1]))
                    rsq_done[cprev] = a_m.n

                @blk.vector
                def _(v, cprev=cprev):
                    v.wait_ge(a_m.h, rsq_done[cprev])
                    v_m.inc(v.reciprocal(rr[0:1, 2, :], rr[0:1, 1, :]))
                    rcp_done[cprev] = v_m.n

                if r < NR:
                    ci2 = r // 32

                    @blk.tensor
                    def _(t, ci2=ci2, cprev=cprev):
                        t.wait_ge(v_m.h, rr0_done[cprev])
                        for s2 in range(2):
                            mm = t.matmul(M1[:, s2 * CH:(s2 + 1) * CH],
                                          self.B_all[:, ci2 * CH + s2 * 128: ci2 * CH + (s2 + 1) * 128],
                                          self.C_all[:, ci2 * CH:(ci2 + 1) * CH],
                                          start=(s2 == 0), stop=(s2 == 1))
                        p_m.inc(mm)
                        sc_done[ci2] = p_m.n

                    @blk.vector
                    def _(v, ci2=ci2, r=r):
                        v.wait_ge(p_m.h, sc_done[ci2])
                        v.tensor_copy(srt[:, 0, :], M1[:, 0:CH])
                        v_m.inc(v.tensor_copy(srt[:, 1, :], M1[:, CH:2 * CH]))
                        sr_done[ci2] = v_m.n

                    @blk.vector
                    def _(v, r=r, ci2=ci2):
                        v.wait_ge(a_m.h, ex_done[r])
                        if r >= 2:
                            v.wait_ge(p_m.h, yd_done[r - 2])
                        for q in range(4):
                            v_m.inc(v.tensor_mul(stdb[:, r % 2, q, :], srt[:, q % 2, :],
                                                 lmb[:, r % 2, q, :]))
                        sd_done[r] = v_m.n

            if bnd:
                @blk.sync
                def _(s, gi=gi):
                    if gi >= 1:
                        s.wait_ge(v_m.h, yn_done[gi * 8 - 1])
                        cis, gs = (gi - 1) // 4, (gi - 1) % 4
                        s_y.inc(s.dma_start(
                            out=self.d_ynorm.rearrange("(f p) t -> p f t", p=128)[
                                :, 8 * gs:8 * gs + 8, cis * CH:(cis + 1) * CH],
                            in_=ygsb[:, (gi - 1) % 2, :, :]), 16)
                        st_done[gi - 1] = s_y.n
                    if gi + 1 < NGR:
                        if gi >= 1:
                            s.wait_ge(p_m.h, yd_done[gi * 8 - 1])
                            s.wait_ge(v_m.h, ptb_done[gi - 1])
                        em_loads(s, gi + 1)
            if (r % 32 == 0 and r >= 32) or r == NR:
                cprev = r // 32 - 1

                @blk.sync
                def _(s, cprev=cprev):
                    s.wait_ge(v_m.h, rcp_done[cprev])
                    s_y.inc(s.dma_start(out=self.d_rstd[0:1, cprev * CH:(cprev + 1) * CH],
                                        in_=rr[0:1, 2, :]), 16)
                    rs_done[cprev] = s_y.n

        @blk.sync
        def _(s):
            s.wait_ge(v_m.h, yn_done[NR - 1])
            s_y.inc(s.dma_start(
                out=self.d_ynorm.rearrange("(f p) t -> p f t", p=128)[:, 24:32, 7 * CH:8 * CH],
                in_=ygsb[:, (NGR - 1) % 2, :, :]), 16)
            s.wait_ge(s_y.h, s_y.n)

    def p5_final(self, blk, ph):
        nc, c = self.nc, self.ctx
        wT = ph.enter_context(nc.sbuf_tensor("owT", [128, KDI, 2048], BF16))
        g_w = [Ctr(nc, c, "o_gw0"), Ctr(nc, c, "o_gw1")]
        p_tr = Ctr(nc, c, "o_ptr")
        v_tr = Ctr(nc, c, "o_vtr")
        with ExitStack() as bctx:
            wpb = bctx.enter_context(nc.sbuf_tensor("wpb", [128, 2, DI], BF16))
            wtr = [bctx.enter_context(nc.psum_tensor(f"o_wt{i}", [128, 1024], BF16)) for i in range(2)]
            for m in range(KD):
                @blk.gpsimd
                def _(g, m=m):
                    if m >= 2:
                        g.wait_ge(p_tr.h, 32 * (m - 1))
                    g_w[m % 2].inc(g.dma_start(out=wpb[:, m % 2, :], in_=self.w_outproj[m * 128:(m + 1) * 128, :]), 16)

                @blk.tensor
                def _(t, m=m):
                    t.wait_ge(g_w[m % 2].h, 16 * (m // 2 + 1))
                    for half in range(4):
                        if 4 * m + half >= 2:
                            t.wait_ge(v_tr.h, 4 * m + half - 1)
                        for kk in range(8):
                            k = half * 8 + kk
                            p_tr.inc(t.transpose(wtr[half % 2][:, kk * 128:(kk + 1) * 128],
                                                 wpb[:, m % 2, k * 128:(k + 1) * 128],
                                                 self.identb[:, :]))

                @blk.vector
                def _(v, m=m):
                    for half in range(4):
                        v.wait_ge(p_tr.h, 32 * m + 8 * (half + 1))
                        src = wtr[half % 2].ap().rearrange("p (k x) -> p k x", k=8)
                        v_tr.inc(v.tensor_copy(wT[:, half * 8:(half + 1) * 8, m * 128:(m + 1) * 128], src))
        wT_total = v_tr.n

        yt = ph.enter_context(nc.sbuf_tensor("yt5", [128, 2, KDI, 128], BF16))
        ffb = ph.enter_context(nc.sbuf_tensor("ffb5", [128, 2, KD, 128], BF16))
        ffT = ph.enter_context(nc.sbuf_tensor("ffT5", [128, 2, 2048], BF16))
        ub = ph.enter_context(nc.sbuf_tensor("ub5", [128, 2, 2048], F32))
        rcol = ph.enter_context(nc.sbuf_tensor("rcol5", [128, 2, 1], F32))
        obuf = ph.enter_context(nc.sbuf_tensor("obuf5", [128, 2, 2048], F32))
        pop = [ph.enter_context(nc.psum_tensor(f"o_p{i}", [128, 512], F32)) for i in range(2)]
        pft = [ph.enter_context(nc.psum_tensor(f"o_f{i}", [128, 1024], BF16)) for i in range(2)]
        s_y = Ctr(nc, c, "o_sy")
        g_u = Ctr(nc, c, "o_gu")
        p_mm = Ctr(nc, c, "o_pmm")
        v_s1 = Ctr(nc, c, "o_vs1")
        p_tf = Ctr(nc, c, "o_ptf")
        v_ev = Ctr(nc, c, "o_vev")
        g_ad = Ctr(nc, c, "o_gad")
        s_out = Ctr(nc, c, "o_sout")
        NTB = 16
        for tb in range(NTB):
            sl = tb % 2

            @blk.sync
            def _(s, tb=tb, sl=sl):
                if tb == 0:
                    s.wait_ge(v_tr.h, wT_total)
                if tb >= 2:
                    s.wait_ge(p_mm.h, 4 * (tb - 1))
                    s.wait_ge(p_tf.h, 16 * (tb - 1))
                s_y.inc(s.dma_start(
                    out=yt[:, sl, :, :],
                    in_=self.d_ynorm.rearrange("(f p) t -> p f t", p=128)[:, :, tb * 128:(tb + 1) * 128]), 16)
                s_y.inc(s.dma_start(
                    out=ffb[:, sl, :, :],
                    in_=self.d_ffn.rearrange("(k p) t -> p k t", p=128)[:, :, tb * 128:(tb + 1) * 128]), 16)

            @blk.gpsimd
            def _(g, tb=tb, sl=sl):
                if tb == 0:
                    g.wait_ge(v_tr.h, wT_total)
                if tb >= 2:
                    g.wait_ge(v_s1.h, 4 * (tb - 1))
                g_u.inc(g.dma_start(out=ub[:, sl, :], in_=self.u[3 + tb * 128: 3 + (tb + 1) * 128, :]), 16)
                g_u.inc(g.dma_start(out=rcol[:, sl, :],
                                    in_=self.d_rstd.rearrange("o t -> t o")[tb * 128:(tb + 1) * 128, :],
                                    allow_slow_non_contiguous=True), 16)

            @blk.tensor
            def _(t, tb=tb, sl=sl):
                t.wait_ge(s_y.h, 32 * (tb + 1))
                if tb == 0:
                    t.wait_ge(v_tr.h, wT_total)
                if tb >= 1:
                    t.wait_ge(v_ev.h, 2 * tb)
                for mo in range(KD):
                    p_tf.inc(t.transpose(pft[mo // 8][:, (mo % 8) * 128:(mo % 8 + 1) * 128],
                                         ffb[:, sl, mo, :], self.identb[:, :]))
                for dq in range(4):
                    if 4 * tb + dq >= 2:
                        t.wait_ge(v_s1.h, 4 * tb + dq - 1)
                    for k in range(KDI):
                        mm = t.matmul(pop[dq % 2][:, :], yt[:, sl, k, :],
                                      wT[:, k, dq * 512:(dq + 1) * 512],
                                      start=(k == 0), stop=(k == KDI - 1))
                        if k == KDI - 1:
                            p_mm.inc(mm)

            @blk.vector
            def _(v, tb=tb, sl=sl):
                for dq in range(4):
                    v.wait_ge(p_mm.h, 4 * tb + dq + 1)
                    v.wait_ge(g_u.h, 32 * (tb + 1))
                    if tb >= 2:
                        v.wait_ge(s_out.h, 16 * (tb - 1))
                        v.wait_ge(g_ad.h, 4 * (tb - 1))
                    v_s1.inc(v.scalar_tensor_tensor(obuf[:, sl, dq * 512:(dq + 1) * 512],
                                                    pop[dq % 2][:, :], rcol[:, sl, 0:1],
                                                    ub[:, sl, dq * 512:(dq + 1) * 512],
                                                    ALU.mult, ALU.add))

            @blk.scalar
            def _(sc, tb=tb, sl=sl):
                sc.wait_ge(p_tf.h, 16 * (tb + 1))
                if tb >= 1:
                    sc.wait_ge(g_ad.h, 4 * tb)
                for bk in range(2):
                    v_ev.inc(sc.activation(ffT[:, sl, bk * 1024:(bk + 1) * 1024],
                                           pft[bk][:, :], AF.Copy))

            @blk.vector
            def _(v, tb=tb, sl=sl):
                v.wait_ge(v_ev.h, 2 * (tb + 1))
                for dq in range(4):
                    v.wait_ge(v_s1.h, 4 * tb + dq + 1)
                    g_ad.inc(v.tensor_add(obuf[:, sl, dq * 512:(dq + 1) * 512],
                                          obuf[:, sl, dq * 512:(dq + 1) * 512],
                                          ffT[:, sl, dq * 512:(dq + 1) * 512]))

            @blk.sync
            def _(s, tb=tb, sl=sl):
                s.wait_ge(g_ad.h, 4 * (tb + 1))
                s_out.inc(s.dma_start(out=self.out[tb * 128:(tb + 1) * 128, :], in_=obuf[:, sl, :]), 16)

        @blk.sync
        def _(s):
            s.wait_ge(s_out.h, s_out.n)

    def p_dbg(self, blk):
        nc = self.nc
        sd = Ctr(nc, self.ctx, "dbg_s")

        @blk.sync
        def _(s):
            pairs = [("xn", self.d_xn), ("z", self.d_z), ("xbc", self.d_xbc),
                     ("xtm", self.d_xtm), ("dx", self.d_dx),
                     ("ccout", self.cc_out), ("ynorm", self.d_ynorm), ("ffn", self.d_ffn)]
            for nm, src in pairs:
                sd.inc(s.dma_start(out=self.dbg[nm], in_=src), 16)
            sd.inc(s.dma_start(out=self.dbg["prevloc"],
                               in_=self.d_prev.rearrange("c p x -> (c p) x")
                               if False else self.d_prev[:, :, :]), 16)
            s.wait_ge(sd.h, sd.n)


def _build():
    return B().build()


def kernel(L0_attn_u, L0_attn_in_proj_w, L0_attn_conv_w, L0_attn_dt_bias,
           L0_attn_A_log, L0_attn_D_param, L0_attn_norm_w, L0_attn_out_proj_w,
           L0_ffn_W_router, L0_ffn_We_gate, L0_ffn_We_up, L0_ffn_We_down,
           L0_ffn_Ws_gate, L0_ffn_Ws_up, L0_ffn_Ws_down):
    u = np.ascontiguousarray(np.asarray(L0_attn_u, dtype=np.float32))
    nc = _build()
    in_maps = []
    for k in range(8):
        bi, s = k // 4, k % 4
        if s == 0:
            u_ext = np.concatenate([np.zeros((3, D), np.float32), u[bi, 0:T]], axis=0)
        else:
            u_ext = u[bi, s * T - 3:(s + 1) * T]
        mselT = np.zeros((8, 8), np.float32)
        cbias = np.full((8,), -1e30, np.float32)
        for j in range(8):
            if j // 4 == k // 4 and j < k:
                cbias[j] = 0.0
                for m in range(j + 1, k):
                    mselT[m, j] = 1.0
        in_maps.append({
            "u_ext": np.ascontiguousarray(u_ext),
            "in_proj_w": np.ascontiguousarray(np.asarray(L0_attn_in_proj_w, dtype=np.float32)),
            "conv_w": np.ascontiguousarray(np.asarray(L0_attn_conv_w, dtype=np.float32)),
            "dt_bias": np.ascontiguousarray(np.asarray(L0_attn_dt_bias, dtype=np.float32)),
            "A_log": np.ascontiguousarray(np.asarray(L0_attn_A_log, dtype=np.float32)),
            "D_param": np.ascontiguousarray(np.asarray(L0_attn_D_param, dtype=np.float32)),
            "norm_w": np.ascontiguousarray(np.asarray(L0_attn_norm_w, dtype=np.float32)),
            "out_proj_w": np.ascontiguousarray(np.asarray(L0_attn_out_proj_w, dtype=np.float32)),
            "W_router": np.ascontiguousarray(np.asarray(L0_ffn_W_router, dtype=np.float32)),
            "We_gate": np.ascontiguousarray(np.asarray(L0_ffn_We_gate, dtype=np.float32)),
            "We_up": np.ascontiguousarray(np.asarray(L0_ffn_We_up, dtype=np.float32)),
            "We_down": np.ascontiguousarray(np.asarray(L0_ffn_We_down, dtype=np.float32)),
            "Ws_gate": np.ascontiguousarray(np.asarray(L0_ffn_Ws_gate, dtype=np.float32)),
            "Ws_up": np.ascontiguousarray(np.asarray(L0_ffn_Ws_up, dtype=np.float32)),
            "Ws_down": np.ascontiguousarray(np.asarray(L0_ffn_Ws_down, dtype=np.float32)),
            "mselT": mselT, "cbias": cbias,
        })
    res = run_bass_kernel_spmd(nc, in_maps, core_ids=list(range(8)))
    out = np.empty((2, 8192, D), np.float32)
    for k in range(8):
        bi, s = k // 4, k % 4
        out[bi, s * T:(s + 1) * T] = res.results[k]["out"]
    if _DBG:
        kernel._last = res
    return out
